# version 10
# baseline (speedup 1.0000x reference)
import numpy as np
import ml_dtypes
from contextlib import ExitStack

import concourse.bass as bass
import concourse.mybir as mybir
from concourse.bass_utils import run_bass_kernel_spmd

F32 = mybir.dt.float32
BF16 = mybir.dt.bfloat16
I32 = mybir.dt.int32
AF = mybir.ActivationFunctionType
ALU = mybir.AluOpType
AX = mybir.AxisListType

B, S, D = 2, 8192, 2048
NH, HD, RD = 16, 128, 64
KVC, QC = 512, 1536
NG, GS, NE, GI = 8, 8, 64, 1408
OWN = 2048
NCHUNK = 16
NDC = D // 128
NCC = KVC // 128
NQC = QC // 128
NIC = GI // 128
SCALE = 1.0 / float(np.sqrt(HD + RD))
EPS = 1e-5
NTILE = 8
bfdt = ml_dtypes.bfloat16


def own_chunks(qi):
    return [63 - qi - 4 * j for j in range(NCHUNK)]


def tile_T(j):
    return 64 - 8 * j


class _Op:
    __slots__ = ("idx", "eng", "emit", "deps", "dma", "sem", "inc", "cnt", "waits", "wlist")


NQSUB = 6


class Sched:
    def __init__(self, nc, es):
        self.nc = nc
        self.ops = []
        self.w = {}
        self.r = {}
        self.sems = {}
        for name in ("e:pe", "e:act", "e:dve", "e:pool"):
            self.sems[name] = es.enter_context(nc.semaphore(name.replace(":", "_")))
        for q in ("sp", "pool", "act"):
            for i in range(NQSUB):
                self.sems[("q:" + q, i)] = es.enter_context(nc.semaphore(f"q_{q}_{i}"))
        self.cnt = {k: 0 for k in self.sems}
        self.qn = {"sp": 0, "pool": 0, "act": 0}

    def op(self, eng, emit, reads=(), writes=(), dma=False):
        idx = len(self.ops)
        deps = set()
        for k in reads:
            j = self.w.get(k)
            if j is not None:
                deps.add(j)
        for k in writes:
            j = self.w.get(k)
            if j is not None:
                deps.add(j)
            deps.update(self.r.get(k, ()))
        o = _Op()
        o.idx, o.eng, o.emit, o.deps, o.dma = idx, eng, emit, deps, dma
        if dma:
            n = self.qn[eng]
            self.qn[eng] = n + 1
            o.sem = ("q:" + eng, n % NQSUB)
            o.inc = 16
        else:
            o.sem = "e:" + eng
            o.inc = 1
        self.cnt[o.sem] += o.inc
        o.cnt = self.cnt[o.sem]
        self.ops.append(o)
        for k in reads:
            self.r.setdefault(k, []).append(idx)
        for k in writes:
            self.w[k] = idx
            self.r[k] = []
        return idx

    def pe(self, emit, r=(), w=()):
        return self.op("pe", emit, r, w)

    def dve(self, emit, r=(), w=()):
        return self.op("dve", emit, r, w)

    def act(self, emit, r=(), w=()):
        return self.op("act", emit, r, w)

    def pool(self, emit, r=(), w=()):
        return self.op("pool", emit, r, w)

    def dma(self, eng, emit, r=(), w=()):
        return self.op(eng, emit, r, w, dma=True)

    def barrier(self):
        snapshot = dict(self.cnt)
        for eng in ("pe", "act", "dve", "pool", "sp"):
            o = _Op()
            o.idx, o.eng, o.emit, o.dma = len(self.ops), eng, None, False
            o.deps = set()
            o.sem, o.inc, o.cnt = None, 0, None
            o.waits = snapshot
            self.ops.append(o)

    def finalize(self):
        engvc = {e: {} for e in ("pe", "act", "dve", "pool", "sp")}
        opvc = [None] * len(self.ops)
        for o in self.ops:
            if o.emit is None and o.sem is None:
                vc = engvc[o.eng]
                o.wlist = []
                for semname, val in o.waits.items():
                    if val > vc.get(semname, 0):
                        o.wlist.append((semname, val))
                        vc[semname] = val
                opvc[o.idx] = dict(vc)
                continue
            vc = engvc[o.eng]
            wl = []
            if o.dma and o.cnt > 16 and vc.get(o.sem, 0) < o.cnt - 16:
                wl.append((o.sem, o.cnt - 16))
                vc[o.sem] = o.cnt - 16
            for j in sorted(o.deps):
                d = self.ops[j]
                if d.sem is None:
                    continue
                if d.eng == "pe" and o.eng == "pe" and not d.dma and not o.dma:
                    continue
                if vc.get(d.sem, 0) >= d.cnt:
                    continue
                wl.append((d.sem, d.cnt))
                vc[d.sem] = d.cnt
                dvc = opvc[j]
                if dvc:
                    for s2, v2 in dvc.items():
                        if v2 > vc.get(s2, 0):
                            vc[s2] = v2
            o.wlist = wl
            myvc = dict(vc)
            myvc[o.sem] = max(myvc.get(o.sem, 0), o.cnt)
            opvc[o.idx] = myvc

        nc = self.nc
        with nc.Block() as block:
            def run_engine(name, handle):
                for o in self.ops:
                    if o.eng != name:
                        continue
                    for semname, val in o.wlist:
                        handle.wait_ge(self.sems[semname], val)
                    if o.emit is not None:
                        o.emit(handle).then_inc(self.sems[o.sem], o.inc)

            @block.sync
            def _(h):
                run_engine("sp", h)

            @block.gpsimd
            def _(h):
                run_engine("pool", h)

            @block.vector
            def _(h):
                run_engine("dve", h)

            @block.scalar
            def _(h):
                run_engine("act", h)

            @block.tensor
            def _(h):
                run_engine("pe", h)


class Ctx:
    pass


def _declare_io(nc, stage=()):
    d = Ctx()
    def din(name, shape, dt):
        t = nc.dram_tensor(name, list(shape), dt, kind="ExternalInput").ap()
        setattr(d, name, t)
        return t

    din("xb", [S, D], F32)
    din("xo", [OWN, D], F32)
    din("w_kvd", [128, NDC * KVC], F32); din("b_kvd", [128, NCC], F32)
    din("w_qd", [128, NDC * QC], F32); din("b_qd", [128, NQC], F32)
    din("w_ku", [NH, 128, NCC * 128], F32); din("b_ku", [128, NH], F32)
    din("w_vu", [NH, 128, NCC * 128], F32); din("b_vu", [128, NH], F32)
    din("w_kr", [NH, 128, NCC * RD], F32); din("b_kr", [RD, NH], F32)
    din("w_qu", [NH, 128, NQC * 128], F32); din("b_qu", [128, NH], F32)
    din("w_qr", [NH, 128, NQC * RD], F32); din("b_qr", [RD, NH], F32)
    din("w_o", [128, NDC * D], F32); din("b_o", [1, D], F32)
    din("b_ku_r", [1, D], F32); din("b_kr_r", [1, NH * RD], F32)
    din("gate_wT", [128, NDC * NE], F32)
    din("maskup", [NG, 128, NDC * GS], F32)
    din("wgT", [NG, NIC, 128, NDC * 128], F32)
    din("wuT", [NG, NIC, 128, NDC * 128], F32)
    din("downT", [NG, NIC, 128, D], F32)
    din("inv", [1, NE], I32)
    din("ktbl_c", [32, S], BF16); din("ktbl_s", [32, S], BF16)
    din("qtbl_c", [32, OWN], BF16); din("qtbl_s", [32, OWN], BF16)
    din("masks", [NTILE, 128, 8 * 256], BF16)
    din("ident", [128, 128], BF16)
    din("identf", [128, 128], F32)

    d.out = nc.dram_tensor("out", [OWN, D], F32, kind="ExternalOutput").ap()
    def scr(name, shape, dt):
        kind = "ExternalInput" if name in stage else "Internal"
        t = nc.dram_tensor(name, list(shape), dt, kind=kind).ap()
        setattr(d, name, t)
        return t
    scr("z_qb", [1, NH], F32)
    scr("z_kvc", [NCC, 128, S], BF16)
    scr("z_qc", [NQC, 128, OWN], BF16)
    scr("z_xn", [NDC, 128, OWN], BF16)
    scr("z_q", [NH, 128, OWN], BF16)
    scr("z_qr", [NH, RD, OWN], BF16)
    scr("z_ctx", [NH, 128, OWN], BF16)
    scr("z_attn", [OWN, D], F32)
    scr("z_a", [NG, NIC, 128, OWN], BF16)
    scr("z_dwn", [NG, NIC, 128, D], BF16)
    return d


_SCHED_REF = []


def _build(upto=99, debug=(), upfrom=0, stage=()):
    nc = bass.Bass("TRN2", target_bir_lowering=False, detect_race_conditions=False,
                   dynamic_dma_scratch_size=16384)
    d = _declare_io(nc, stage)
    dbg_outs = {}
    for name in debug:
        zt = getattr(d, name)
        dbg_outs[name] = nc.dram_tensor("dbg_" + name, list(zt.shape), zt.dtype,
                                        kind="ExternalOutput").ap()
    root = ExitStack()
    with root:
        Sh = Sched(nc, root)
        _SCHED_REF.clear(); _SCHED_REF.append(Sh)
        ones_b = root.enter_context(nc.sbuf_tensor("ones_b", [128, 1], BF16))
        ones_f = root.enter_context(nc.sbuf_tensor("ones_f", [1, 128], F32))
        id_b = root.enter_context(nc.sbuf_tensor("id_b", [128, 128], BF16))
        id_f = root.enter_context(nc.sbuf_tensor("id_f", [128, 128], F32))
        qb = root.enter_context(nc.sbuf_tensor("qb", [1, NH], F32))
        p = Ctx()
        p.nc, p.S, p.d = nc, Sh, d
        p.ones_b, p.ones_f, p.id_b, p.id_f, p.qb = ones_b, ones_f, id_b, id_f, qb

        Sh.dve(lambda h: nc.vector.memset(ones_b[:], 1.0), w=[("ones_b",)])
        Sh.dve(lambda h: nc.vector.memset(ones_f[:], 1.0), w=[("ones_f",)])
        Sh.dma("sp", lambda h: h.dma_start(out=id_b[:], in_=d.ident[:, :]), w=[("id_b",)])
        Sh.dma("sp", lambda h: h.dma_start(out=id_f[:], in_=d.identf[:, :]), w=[("id_f",)])

        if upfrom <= 1:
            _phase1a(p)
        if upto >= 2 and upfrom <= 2:
            _phase1b(p)
        if upto >= 3 and upfrom <= 3:
            _phase2(p)
        if upto >= 4 and upfrom <= 4:
            _phase2b(p)
        if upto >= 5:
            _phase3(p)
        _finish(p, upto)
        for name, oap in dbg_outs.items():
            zt = getattr(d, name)
            sl = tuple(slice(None) for _ in zt.shape)
            Sh.dma("sp", (lambda zt, oap, sl: lambda h: h.dma_start(out=oap[sl], in_=zt[sl]))(zt, oap, sl))
        Sh.barrier()
        Sh.finalize()
    return nc


def _ln_block(p, es, src_ap, blk, x_tm, xn_tm, stats, kx):
    nc, Sh = p.nc, p.S
    ssq, sums, mean, var, sd, rstd, sqs, msq = stats
    Sh.dma("sp", lambda h: h.dma_start(
        out=x_tm[:], in_=src_ap[blk * 512:(blk + 1) * 512, :].rearrange("(t p) d -> p t d", p=128)),
        w=[(kx, "xtm")])
    for t in range(4):
        Sh.act((lambda t: lambda h: nc.scalar.activation(
            out=sqs[:, t * D:(t + 1) * D], in_=x_tm[:, t, :], func=AF.Square,
            accum_out=ssq[:, t:t + 1]))(t),
            r=[(kx, "xtm")], w=[(kx, "ssq", t), (kx, "sqs", t)])
        Sh.dve((lambda t: lambda h: nc.vector.reduce_sum(
            sums[:, t:t + 1], x_tm[:, t, :], axis=AX.X))(t),
            r=[(kx, "xtm")], w=[(kx, "sum", t)])
    Sh.dve(lambda h: nc.vector.tensor_scalar_mul(mean[:], sums[:], 1.0 / D),
           r=[(kx, "sum", t) for t in range(4)], w=[(kx, "mean")])
    Sh.dve(lambda h: nc.vector.tensor_scalar_mul(var[:], ssq[:], 1.0 / D),
           r=[(kx, "ssq", t) for t in range(4)], w=[(kx, "var")])
    Sh.dve(lambda h: nc.vector.tensor_tensor(out=msq[:], in0=mean[:], in1=mean[:], op=ALU.mult),
           r=[(kx, "mean")], w=[(kx, "msq")])
    Sh.dve(lambda h: nc.vector.tensor_sub(var[:], var[:], msq[:]),
           r=[(kx, "var"), (kx, "msq")], w=[(kx, "var")])
    Sh.dve(lambda h: nc.vector.tensor_scalar_add(var[:], var[:], EPS),
           r=[(kx, "var")], w=[(kx, "var")])
    Sh.act(lambda h: nc.scalar.activation(out=sd[:], in_=var[:], func=AF.Sqrt),
           r=[(kx, "var")], w=[(kx, "sd")])
    Sh.dve(lambda h: nc.vector.reciprocal(rstd[:], sd[:]),
           r=[(kx, "sd")], w=[(kx, "rstd")])
    for t in range(4):
        Sh.dve((lambda t: lambda h: nc.vector.scalar_tensor_tensor(
            out=xn_tm[:, t, :], in0=x_tm[:, t, :], scalar=mean[:, t:t + 1],
            in1=rstd[:, t:t + 1].to_broadcast([128, D]),
            op0=ALU.subtract, op1=ALU.mult))(t),
            r=[(kx, "xtm"), (kx, "mean"), (kx, "rstd")], w=[(kx, "xn", t)])


def _transpose_block(p, es, xn_tm, xn_fm, ps_tp, kx, kfm):
    nc, Sh = p.nc, p.S
    for dc in range(NDC):
        bank = dc % 2
        for t in range(4):
            Sh.pe((lambda dc, t, bank: lambda h: nc.tensor.transpose(
                ps_tp[bank][:, t * 128:(t + 1) * 128], xn_tm[:, t, dc * 128:(dc + 1) * 128],
                p.id_b[:]))(dc, t, bank),
                r=[(kx, "xn", t), ("id_b",)], w=[("ps_tp", bank, t)])
        Sh.dve((lambda dc, bank: lambda h: nc.vector.tensor_copy(
            xn_fm[:, dc * 512:(dc + 1) * 512], ps_tp[bank][:]))(dc, bank),
            r=[("ps_tp", bank, t) for t in range(4)], w=[kfm + (dc,)])


def _phase1a(p):
    nc, Sh, d = p.nc, p.S, p.d
    with ExitStack() as es:
        x_tm2 = [es.enter_context(nc.sbuf_tensor(f"x_tm{i}", [128, 4, D], F32)) for i in range(2)]
        xn_tm2 = [es.enter_context(nc.sbuf_tensor(f"xn_tm{i}", [128, 4, D], BF16)) for i in range(2)]
        xn_fm = [es.enter_context(nc.sbuf_tensor(f"xn_fm{i}", [128, NDC * 512], BF16)) for i in range(2)]
        sqs = es.enter_context(nc.sbuf_tensor("sqs", [128, 4 * D], BF16))
        stats2 = []
        for i in range(2):
            cols = [es.enter_context(nc.sbuf_tensor(f"lnst{i}", [128, 4 * 7], F32))]
            c = cols[0]
            stats2.append((c[:, 0:4], c[:, 4:8], c[:, 8:12], c[:, 12:16], c[:, 16:20], c[:, 20:24], sqs, c[:, 24:28]))
        w_big = es.enter_context(nc.sbuf_tensor("w_big", [128, NDC * QC], BF16))
        b_kvd = es.enter_context(nc.sbuf_tensor("sb_b_kvd", [128, NCC], F32))
        pco = es.enter_context(nc.sbuf_tensor("pco", [128, NQC * 512], BF16))
        kvco = pco[:, 0:NCC * 512]
        ps_tp = [es.enter_context(nc.psum_tensor(f"ps_tp{i}", [128, 512], BF16)) for i in range(2)]
        ps_kv = [es.enter_context(nc.psum_tensor(f"ps_kv{i}", [128, 512], F32)) for i in range(2)]

        Sh.dma("pool", lambda h: h.dma_start(
            out=w_big[:, 0:NDC * KVC], in_=d.w_kvd[:, :]), w=[("w_kvd",)])
        Sh.dma("sp", lambda h: h.dma_start(
            out=b_kvd[:], in_=d.b_kvd[:, :]), w=[("b_kvd",)])

        for blk in range(S // 512):
            kx = ("b", blk % 2)
            _ln_block(p, es, d.xb, blk, x_tm2[blk % 2], xn_tm2[blk % 2], stats2[blk % 2], kx)
            _transpose_block(p, es, xn_tm2[blk % 2], xn_fm[blk % 2], ps_tp, kx, ("fm", blk % 2))
            for cc in range(NCC):
                bank = cc % 2
                for dc in range(NDC):
                    Sh.pe((lambda cc, dc, bank, blk: lambda h: nc.tensor.matmul(
                        ps_kv[bank][:], w_big[:, dc * KVC + cc * 128:dc * KVC + (cc + 1) * 128],
                        xn_fm[blk % 2][:, dc * 512:(dc + 1) * 512],
                        start=(dc == 0), stop=(dc == NDC - 1)))(cc, dc, bank, blk),
                        r=[("w_kvd",), ("fm", blk % 2, dc)], w=[("ps_kv", bank)])
                Sh.act((lambda cc, bank: lambda h: nc.scalar.activation(
                    out=pco[:, cc * 512:(cc + 1) * 512], in_=ps_kv[bank][:],
                    func=AF.Identity, bias=b_kvd[:, cc:cc + 1], scale=1.0))(cc, bank),
                    r=[("ps_kv", bank), ("b_kvd",)], w=[("kvco", cc)])
                Sh.dma("act", (lambda cc, blk: lambda h: h.dma_start(
                    out=d.z_kvc[cc, :, blk * 512:(blk + 1) * 512],
                    in_=pco[:, cc * 512:(cc + 1) * 512]))(cc, blk),
                    r=[("kvco", cc)], w=[])
        Sh.barrier()

        b_qd = es.enter_context(nc.sbuf_tensor("sb_b_qd", [128, NQC], F32))
        qco = pco
        ps_q = ps_kv

        Sh.dma("pool", lambda h: h.dma_start(
            out=w_big[:, :], in_=d.w_qd[:, :]), w=[("w_qd",)])
        Sh.dma("sp", lambda h: h.dma_start(
            out=b_qd[:], in_=d.b_qd[:, :]), w=[("b_qd",)])

        for blk in range(OWN // 512):
            kx = ("o", blk % 2)
            _ln_block(p, es, d.xo, blk, x_tm2[blk % 2], xn_tm2[blk % 2], stats2[blk % 2], kx)
            _transpose_block(p, es, xn_tm2[blk % 2], xn_fm[blk % 2], ps_tp, kx, ("fm2", blk % 2))
            for dc in range(NDC):
                Sh.dma("sp", (lambda dc, blk: lambda h: h.dma_start(
                    out=d.z_xn[dc, :, blk * 512:(blk + 1) * 512],
                    in_=xn_fm[blk % 2][:, dc * 512:(dc + 1) * 512]))(dc, blk),
                    r=[("fm2", blk % 2, dc)], w=[])
            for qc in range(NQC):
                bank = qc % 2
                for dc in range(NDC):
                    Sh.pe((lambda qc, dc, bank, blk: lambda h: nc.tensor.matmul(
                        ps_q[bank][:], w_big[:, dc * QC + qc * 128:dc * QC + (qc + 1) * 128],
                        xn_fm[blk % 2][:, dc * 512:(dc + 1) * 512],
                        start=(dc == 0), stop=(dc == NDC - 1)))(qc, dc, bank, blk),
                        r=[("w_qd",), ("fm2", blk % 2, dc)], w=[("ps_q", bank)])
                Sh.act((lambda qc, bank: lambda h: nc.scalar.activation(
                    out=qco[:, qc * 512:(qc + 1) * 512], in_=ps_q[bank][:],
                    func=AF.Identity, bias=b_qd[:, qc:qc + 1], scale=1.0))(qc, bank),
                    r=[("ps_q", bank), ("b_qd",)], w=[("qco", qc)])
                Sh.dma("act", (lambda qc, blk: lambda h: h.dma_start(
                    out=d.z_qc[qc, :, blk * 512:(blk + 1) * 512],
                    in_=qco[:, qc * 512:(qc + 1) * 512]))(qc, blk),
                    r=[("qco", qc)], w=[])
        Sh.barrier()


_DBG_QCH = [False]
_P1B_PARTS = {"qr", "rope", "bound"}
_NB_ZERO = [False]


def _phase1b(p):
    nc, Sh, d = p.nc, p.S, p.d
    if _DBG_QCH[0]:
        d.dbg_qch = nc.dram_tensor("dbgq", [128, OWN], F32, kind="ExternalOutput").ap()
        d.dbg_ps = nc.dram_tensor("dbgps", [128, 512], F32, kind="ExternalOutput").ap()
        d.dbg_qcsb = nc.dram_tensor("dbgqcsb", [128, NQC * OWN], BF16, kind="ExternalOutput").ap()
    with ExitStack() as es:
        qc_sb = es.enter_context(nc.sbuf_tensor("qc_sb", [128, NQC, OWN], BF16))
        w_qu = [es.enter_context(nc.sbuf_tensor(f"sb_w_qu{i}", [128, NQC * 128], BF16)) for i in range(2)]
        w_qr = [es.enter_context(nc.sbuf_tensor(f"sb_w_qr{i}", [128, NQC * RD], BF16)) for i in range(2)]
        b_qu = es.enter_context(nc.sbuf_tensor("sb_b_qu", [128, NH], F32))
        b_qr = es.enter_context(nc.sbuf_tensor("sb_b_qr", [RD, NH], F32))
        bqu_s = es.enter_context(nc.sbuf_tensor("bqu_s", [128, NH], F32))
        bqr_s = es.enter_context(nc.sbuf_tensor("bqr_s", [RD, NH], F32))
        qtc = es.enter_context(nc.sbuf_tensor("qtc", [32, OWN], BF16))
        qts = es.enter_context(nc.sbuf_tensor("qts", [32, OWN], BF16))
        qch = es.enter_context(nc.sbuf_tensor("qch", [128, OWN], BF16))
        qrpre = es.enter_context(nc.sbuf_tensor("qrpre", [RD, OWN], BF16))
        t2b = es.enter_context(nc.sbuf_tensor("t2b", [32, OWN], BF16))
        rtmp = es.enter_context(nc.sbuf_tensor("rtmp", [32, OWN], BF16))
        rtm2 = es.enter_context(nc.sbuf_tensor("rtm2", [32, OWN], BF16))
        qrope = es.enter_context(nc.sbuf_tensor("qrope", [RD, OWN], BF16))
        sqb = es.enter_context(nc.sbuf_tensor("sqb", [128, OWN], BF16))
        sqr = es.enter_context(nc.sbuf_tensor("sqr", [RD, OWN], BF16))
        qbt = es.enter_context(nc.sbuf_tensor("qbt", [1, 4], F32))
        ps_a = es.enter_context(nc.psum_tensor("ps1b_a", [128, 512], F32))
        ps_b = es.enter_context(nc.psum_tensor("ps1b_b", [128, 512], F32))
        ps_n = es.enter_context(nc.psum_tensor("ps1b_n", [1, 512], F32))

        for qc in range(NQC):
            Sh.dma("sp", (lambda qc: lambda h: h.dma_start(
                out=qc_sb[:, qc, :], in_=d.z_qc[qc, :, :]))(qc), w=[("qc_sb",)])
        Sh.dma("sp", lambda h: h.dma_start(out=b_qu[:], in_=d.b_qu[:, :]), w=[("b_qu",)])
        Sh.dma("sp", lambda h: h.dma_start(out=b_qr[:], in_=d.b_qr[:, :]), w=[("b_qr",)])
        Sh.dma("pool", lambda h: h.dma_start(out=qtc[:], in_=d.qtbl_c[:, :]), w=[("qtc",)])
        Sh.dma("pool", lambda h: h.dma_start(out=qts[:], in_=d.qtbl_s[:, :]), w=[("qts",)])
        Sh.dve(lambda h: nc.vector.tensor_scalar_mul(bqu_s[:], b_qu[:], SCALE),
               r=[("b_qu",)], w=[("bqu_s",)])
        Sh.dve(lambda h: nc.vector.tensor_scalar_mul(bqr_s[:], b_qr[:], SCALE),
               r=[("b_qr",)], w=[("bqr_s",)])

        for hh in range(NH):
            par = hh % 2
            Sh.dma("pool", (lambda hh, par: lambda h: h.dma_start(
                out=w_qu[par][:], in_=d.w_qu[hh, :, :]))(hh, par), w=[("w_qu", par)])
            Sh.dma("pool", (lambda hh, par: lambda h: h.dma_start(
                out=w_qr[par][:], in_=d.w_qr[hh, :, :]))(hh, par), w=[("w_qr", par)])
            for t5 in range(4):
                bank = t5 % 2
                ps = [ps_a, ps_b][bank]
                for qc in range(NQC):
                    Sh.pe((lambda t5, qc, par, ps: lambda h: nc.tensor.matmul(
                        ps[:], w_qu[par][:, qc * 128:(qc + 1) * 128], qc_sb[:, qc, t5 * 512:(t5 + 1) * 512],
                        start=(qc == 0), stop=(qc == NQC - 1)))(t5, qc, par, ps),
                        r=[("w_qu", par), ("qc_sb",)], w=[("ps1b", bank)])
                Sh.act((lambda t5, hh, bank, ps: lambda h: nc.scalar.activation(
                    out=qch[:, t5 * 512:(t5 + 1) * 512], in_=ps[:],
                    func=AF.Identity, bias=bqu_s[:, hh:hh + 1], scale=SCALE))(t5, hh, bank, ps),
                    r=[("ps1b", bank), ("bqu_s",)], w=[("qch", t5)])
            for t5 in range(4) if "qr" in _P1B_PARTS else []:
                bank = t5 % 2
                ps = [ps_a, ps_b][bank]
                for qc in range(NQC):
                    Sh.pe((lambda t5, qc, par, ps: lambda h: nc.tensor.matmul(
                        ps[0:RD, :], w_qr[par][:, qc * RD:(qc + 1) * RD], qc_sb[:, qc, t5 * 512:(t5 + 1) * 512],
                        start=(qc == 0), stop=(qc == NQC - 1)))(t5, qc, par, ps),
                        r=[("w_qr", par), ("qc_sb",)], w=[("ps1b", bank)])
                Sh.act((lambda t5, hh, bank, ps: lambda h: nc.scalar.activation(
                    out=qrpre[:, t5 * 512:(t5 + 1) * 512], in_=ps[0:RD, :],
                    func=AF.Identity, bias=bqr_s[:, hh:hh + 1], scale=SCALE))(t5, hh, bank, ps),
                    r=[("ps1b", bank), ("bqr_s",)], w=[("qrpre",)])
            if "rope" not in _P1B_PARTS:
                Sh.dve(lambda h: nc.vector.memset(qrope[:], 0.0), w=[("qrope_t1",), ("qrope_t2",)])
            if "rope" in _P1B_PARTS:
                Sh.dma("pool", lambda h: h.dma_start(out=t2b[:], in_=qrpre[32:64, :]),
                       r=[("qrpre",)], w=[("t2b",)])
                Sh.dve(lambda h: nc.vector.tensor_tensor(
                    out=qrope[0:32, :], in0=qrpre[0:32, :], in1=qtc[:], op=ALU.mult),
                    r=[("qrpre",), ("qtc",)], w=[("qrope_t1",)])
                Sh.dve(lambda h: nc.vector.tensor_tensor(
                    out=rtmp[:], in0=t2b[:], in1=qts[:], op=ALU.mult),
                    r=[("t2b",), ("qts",)], w=[("rtmp",)])
                Sh.dve(lambda h: nc.vector.tensor_sub(qrope[0:32, :], qrope[0:32, :], rtmp[:]),
                       r=[("qrope_t1",), ("rtmp",)], w=[("qrope_t1",)])
                Sh.dve(lambda h: nc.vector.tensor_tensor(
                    out=rtm2[:], in0=t2b[:], in1=qtc[:], op=ALU.mult),
                    r=[("t2b",), ("qtc",)], w=[("rtm2",)])
                Sh.dve(lambda h: nc.vector.tensor_tensor(
                    out=rtmp[:], in0=qrpre[0:32, :], in1=qts[:], op=ALU.mult),
                    r=[("qrpre",), ("qts",)], w=[("rtmp",)])
                Sh.dve(lambda h: nc.vector.tensor_add(rtm2[:], rtm2[:], rtmp[:]),
                       r=[("rtm2",), ("rtmp",)], w=[("rtm2",)])
                Sh.dma("pool", lambda h: h.dma_start(out=qrope[32:64, :], in_=rtm2[:]),
                       r=[("rtm2",)], w=[("qrope_t2",)])
            if "bound" not in _P1B_PARTS:
                Sh.dve(lambda h: nc.vector.memset(p.qb[:], 1.0), w=[("qb",)])
            if "bound" in _P1B_PARTS:
                Sh.dve(lambda h: nc.vector.tensor_tensor(out=sqb[:], in0=qch[:], in1=qch[:], op=ALU.mult),
                       r=[("qch", 0), ("qch", 1), ("qch", 2), ("qch", 3)], w=[("sqb",)])
                Sh.dve(lambda h: nc.vector.tensor_tensor(
                    out=sqr[:], in0=qrope[:], in1=qrope[:], op=ALU.mult),
                    r=[("qrope_t1",), ("qrope_t2",)], w=[("sqr",)])
                for t5 in range(4):
                    Sh.pe((lambda t5: lambda h: nc.tensor.matmul(
                        ps_n[:], p.ones_b[:], sqb[:, t5 * 512:(t5 + 1) * 512],
                        start=True, stop=False))(t5),
                        r=[("sqb",), ("ones_b",)], w=[("ps1b_n",)])
                    Sh.pe((lambda t5: lambda h: nc.tensor.matmul(
                        ps_n[:], p.ones_b[0:RD, :], sqr[:, t5 * 512:(t5 + 1) * 512],
                        start=False, stop=True))(t5),
                        r=[("sqr",), ("ones_b",)], w=[("ps1b_n",)])
                    Sh.dve((lambda t5: lambda h: nc.vector.reduce_max(
                        qbt[:, t5:t5 + 1], ps_n[:], axis=AX.X))(t5),
                        r=[("ps1b_n",)], w=[("qbt", t5)])
                Sh.dve((lambda hh: lambda h: nc.vector.reduce_max(
                    p.qb[:, hh:hh + 1], qbt[:], axis=AX.X))(hh),
                    r=[("qbt", t) for t in range(4)], w=[("qb",)])
            Sh.dma("sp", (lambda hh: lambda h: h.dma_start(
                out=d.z_q[hh, :, :], in_=qch[:]))(hh),
                r=[("qch", t) for t in range(4)], w=[])
            Sh.dma("sp", (lambda hh: lambda h: h.dma_start(
                out=d.z_qr[hh, :, :], in_=qrope[:]))(hh),
                r=[("qrope_t1",), ("qrope_t2",)], w=[])
            if hh == NH - 1:
                Sh.dma("sp", lambda h: h.dma_start(out=d.z_qb[:, :], in_=p.qb[:]),
                       r=[("qb",)], w=[])
            if _DBG_QCH[0] and hh == 15:
                dbgf = es.enter_context(nc.sbuf_tensor("dbgf", [128, OWN], F32))
                Sh.dve(lambda h: nc.vector.tensor_copy(dbgf[:], qch[:]),
                       r=[("qch", t) for t in range(4)], w=[("dbgf",)])
                Sh.dma("sp", lambda h: h.dma_start(out=d.dbg_qch[:, :], in_=dbgf[:]),
                       r=[("dbgf",)], w=[])
                dbgp = es.enter_context(nc.sbuf_tensor("dbgp", [128, 512], F32))
                Sh.dve(lambda h: nc.vector.tensor_copy(dbgp[:], ps_b[:]),
                       r=[("ps1b", 1)], w=[("dbgp",)])
                Sh.dma("sp", lambda h: h.dma_start(out=d.dbg_ps[:, :], in_=dbgp[:]),
                       r=[("dbgp",)], w=[])
                Sh.dma("sp", lambda h: h.dma_start(
                    out=d.dbg_qcsb[:, :], in_=qc_sb[:].rearrange("p c n -> p (c n)")),
                    r=[("qc_sb",)], w=[])
        Sh.barrier()


def _phase2(p):
    nc, Sh, d = p.nc, p.S, p.d
    with ExitStack() as es:
        T = lambda n, s, dt: es.enter_context(nc.sbuf_tensor(n, s, dt))
        kvc = T("kvc", [128, NCC, S], BF16)
        ktc = T("ktc", [32, S], BF16)
        kts = T("kts", [32, S], BF16)
        wku = [T(f"wku{i}", [128, NCC * 128], BF16) for i in range(2)]
        wvu = [T(f"wvu{i}", [128, NCC * 128], BF16) for i in range(2)]
        wkr = [T(f"wkr{i}", [128, NCC * RD], BF16) for i in range(2)]
        kc_sb = T("kc_sb", [128, S], BF16)
        v_sb = T("v_sb", [128, S], BF16)
        krope = T("krope", [RD, S], BF16)
        krpre = [T(f"krpre{i}", [RD, 2048], BF16) for i in range(2)]
        t2b2 = [T(f"t2b2{i}", [32, 2048], BF16) for i in range(2)]
        rt1 = T("rt1", [32, 2048], BF16)
        rt2 = T("rt2", [32, 2048], BF16)
        qc_h = T("qc_h", [128, OWN], BF16)
        qr_h = T("qr_h", [RD, OWN], BF16)
        pt = [T(f"pt{i}", [128, 1024], BF16) for i in range(2)]
        msk = [T(f"msk{i}", [128, 8 * 256], BF16) for i in range(2)]
        l1r = T("l1r", [1, 192], F32)
        sc1 = T("sc1", [1, 8], F32)
        qbv = T("qbv", [1, NH], F32)
        nb = T("nb", [128, 1], F32)
        kvm = T("kvm", [128, 1], F32)
        kvm2 = T("kvm2", [RD, 1], F32)
        bvu_c = T("bvu_c", [128, NH], F32)
        bku_c = T("bku_c", [128, NH], F32)
        bkr_c = T("bkr_c", [RD, NH], F32)
        bvro = T("bvro", [1, 128], F32)
        bvbc = T("bvbc", [128, 128], F32)
        rc = T("rc", [1, 256], F32)
        rb = T("rb", [128, 256], F32)
        ctxev = T("ctxev", [128, 256], BF16)
        st = [es.enter_context(nc.psum_tensor(f"ps2st{i}", [128, 1024], F32)) for i in range(2)]
        ctx = [es.enter_context(nc.psum_tensor(f"ps2c{i}", [128, 512], F32)) for i in range(2)]
        eB = es.enter_context(nc.psum_tensor("ps2e", [128, 256], F32))

        if _DBG_QCH[0]:
            d.dbg_b1 = nc.dram_tensor("dbgb1", [1, 8], F32, kind="ExternalOutput").ap()
            d.dbg_b2 = nc.dram_tensor("dbgb2", [1, 192], F32, kind="ExternalOutput").ap()
            d.dbg_b3 = nc.dram_tensor("dbgb3", [128, 1], F32, kind="ExternalOutput").ap()
            d.dbg_b4 = nc.dram_tensor("dbgb4", [1, NH], F32, kind="ExternalOutput").ap()

        for cc in range(NCC):
            Sh.dma("sp", (lambda cc: lambda h: h.dma_start(
                out=kvc[:, cc, :], in_=d.z_kvc[cc, :, :]))(cc), w=[("kvc",)])
        Sh.dma("sp", lambda h: h.dma_start(out=ktc[:], in_=d.ktbl_c[:, :]), w=[("ktc",)])
        Sh.dma("sp", lambda h: h.dma_start(out=kts[:], in_=d.ktbl_s[:, :]), w=[("kts",)])
        Sh.dma("sp", lambda h: h.dma_start(out=qbv[:], in_=d.z_qb[:, :]), w=[("qbv",)])
        Sh.dma("sp", lambda h: h.dma_start(out=bvu_c[:], in_=d.b_vu[:, :]), w=[("bvu_c",)])
        Sh.dma("sp", lambda h: h.dma_start(out=bku_c[:], in_=d.b_ku[:, :]), w=[("bku_c",)])
        Sh.dma("sp", lambda h: h.dma_start(out=bkr_c[:], in_=d.b_kr[:, :]), w=[("bkr_c",)])

        def head_loads(hh):
            par = hh % 2
            Sh.dma("pool", (lambda hh, par: lambda h: h.dma_start(
                out=wku[par][:], in_=d.w_ku[hh, :, :]))(hh, par), w=[("wku", par)])
            Sh.dma("pool", (lambda hh, par: lambda h: h.dma_start(
                out=wvu[par][:], in_=d.w_vu[hh, :, :]))(hh, par), w=[("wvu", par)])
            Sh.dma("pool", (lambda hh, par: lambda h: h.dma_start(
                out=wkr[par][:], in_=d.w_kr[hh, :, :]))(hh, par), w=[("wkr", par)])

        head_loads(0)
        for hh in range(NH):
            par = hh % 2
            if hh + 1 < NH:
                head_loads(hh + 1)
            Sh.dma("sp", (lambda hh: lambda h: h.dma_start(
                out=qc_h[:], in_=d.z_q[hh, :, :]))(hh), w=[("qc_h",)])
            Sh.dma("sp", (lambda hh: lambda h: h.dma_start(
                out=qr_h[:], in_=d.z_qr[hh, :, :]))(hh), w=[("qr_h",)])

            for k5 in range(16):
                bank = k5 % 2
                ba = st[bank][:, 0:512]
                for cc in range(NCC):
                    Sh.pe((lambda k5, cc, par, ba: lambda h: nc.tensor.matmul(
                        ba, wku[par][:, cc * 128:(cc + 1) * 128],
                        kvc[:, cc, k5 * 512:(k5 + 1) * 512],
                        start=(cc == 0), stop=(cc == NCC - 1)))(k5, cc, par, ba),
                        r=[("wku", par), ("kvc",)], w=[("ps2st", bank)])
                Sh.act((lambda k5, hh, bank, ba: lambda h: nc.scalar.activation(
                    out=kc_sb[:, k5 * 512:(k5 + 1) * 512], in_=ba, func=AF.Identity,
                    bias=bku_c[:, hh:hh + 1], scale=1.0))(k5, hh, bank, ba),
                    r=[("ps2st", bank), ("bku_c",)], w=[("kc_sb", k5)])

            for ch in range(4):
                kp = krpre[ch % 2]
                kpk = ("krpre", ch % 2)
                for k5 in range(4):
                    bank = k5 % 2
                    ba = st[bank][:, 512:1024]
                    for cc in range(NCC):
                        Sh.pe((lambda ch, k5, cc, par, ba: lambda h: nc.tensor.matmul(
                            ba[0:RD, :], wkr[par][:, cc * RD:(cc + 1) * RD],
                            kvc[:, cc, ch * 2048 + k5 * 512:ch * 2048 + (k5 + 1) * 512],
                            start=(cc == 0), stop=(cc == NCC - 1)))(ch, k5, cc, par, ba),
                            r=[("wkr", par), ("kvc",)], w=[("ps2st", bank)])
                    Sh.act((lambda ch, k5, hh, bank, ba, kp: lambda h: nc.scalar.activation(
                        out=kp[:, k5 * 512:(k5 + 1) * 512], in_=ba[0:RD, :],
                        func=AF.Identity, bias=bkr_c[:, hh:hh + 1], scale=1.0))(ch, k5, hh, bank, ba, kp),
                        r=[("ps2st", bank), ("bkr_c",)], w=[kpk + (k5,)])
                kx = [kpk + (i,) for i in range(4)]
                Sh.dma("pool", (lambda ch, kp: lambda h: h.dma_start(
                    out=t2b2[ch % 2][:], in_=kp[32:64, :]))(ch, kp), r=kx, w=[("t2b2", ch % 2)])
                cs = ktc[:, ch * 2048:(ch + 1) * 2048]
                sn = kts[:, ch * 2048:(ch + 1) * 2048]
                Sh.dve((lambda ch, cs, kp: lambda h: nc.vector.tensor_tensor(
                    out=krope[0:32, ch * 2048:(ch + 1) * 2048], in0=kp[0:32, :],
                    in1=cs, op=ALU.mult))(ch, cs, kp), r=kx + [("ktc",)], w=[("krope1", ch)])
                Sh.dve((lambda ch, sn: lambda h: nc.vector.tensor_tensor(
                    out=rt1[:], in0=t2b2[ch % 2][:], in1=sn, op=ALU.mult))(ch, sn),
                    r=[("t2b2", ch % 2), ("kts",)], w=[("rt1",)])
                Sh.dve((lambda ch: lambda h: nc.vector.tensor_sub(
                    krope[0:32, ch * 2048:(ch + 1) * 2048],
                    krope[0:32, ch * 2048:(ch + 1) * 2048], rt1[:]))(ch),
                    r=[("krope1", ch), ("rt1",)], w=[("krope1", ch)])
                Sh.dve((lambda ch, cs: lambda h: nc.vector.tensor_tensor(
                    out=rt2[:], in0=t2b2[ch % 2][:], in1=cs, op=ALU.mult))(ch, cs),
                    r=[("t2b2", ch % 2), ("ktc",)], w=[("rt2",)])
                Sh.dve((lambda ch, sn, kp: lambda h: nc.vector.tensor_tensor(
                    out=rt1[:], in0=kp[0:32, :], in1=sn, op=ALU.mult))(ch, sn, kp),
                    r=kx + [("kts",)], w=[("rt1",)])
                Sh.dve(lambda h: nc.vector.tensor_add(rt2[:], rt2[:], rt1[:]),
                       r=[("rt2",), ("rt1",)], w=[("rt2",)])
                Sh.dma("pool", (lambda ch: lambda h: h.dma_start(
                    out=krope[32:64, ch * 2048:(ch + 1) * 2048], in_=rt2[:]))(ch),
                    r=[("rt2",)], w=[("krope2", ch)])

            kck = [("kc_sb", k5) for k5 in range(16)]
            krk = [("krope1", c) for c in range(4)] + [("krope2", c) for c in range(4)]
            Sh.dve(lambda h: nc.vector.tensor_reduce(
                kvm[:], kc_sb[:], axis=AX.X, op=ALU.max, apply_absolute_value=True),
                r=kck, w=[("kvm",)])
            Sh.dve(lambda h: nc.vector.tensor_reduce(
                kvm2[:], krope[:], axis=AX.X, op=ALU.max, apply_absolute_value=True),
                r=krk, w=[("kvm2",)])
            Sh.dve(lambda h: nc.vector.tensor_mul(kvm[:], kvm[:], kvm[:]),
                   r=[("kvm",)], w=[("kvm",)])
            Sh.dve(lambda h: nc.vector.tensor_mul(kvm2[:], kvm2[:], kvm2[:]),
                   r=[("kvm2",)], w=[("kvm2",)])
            Sh.pe(lambda h: nc.tensor.transpose(eB[0:1, 0:128], kvm[:], p.id_f[:]),
                  r=[("kvm",), ("id_f",)], w=[("ps2e",)])
            Sh.pe(lambda h: nc.tensor.transpose(eB[0:1, 128:192], kvm2[:], p.id_f[0:RD, 0:RD]),
                  r=[("kvm2",), ("id_f",)], w=[("ps2e",)])
            Sh.dve(lambda h: nc.vector.tensor_copy(l1r[:], eB[0:1, 0:192]),
                   r=[("ps2e",)], w=[("l1r",)])
            Sh.dve(lambda h: nc.vector.reduce_sum(sc1[:, 1:2], l1r[:], axis=AX.X),
                   r=[("l1r",)], w=[("sc1", 1)])
            Sh.act((lambda hh: lambda h: nc.scalar.activation(
                out=sc1[:, 2:3], in_=qbv[:, hh:hh + 1], func=AF.Ln))(hh),
                r=[("qbv",)], w=[("sc1", 2)])
            Sh.act(lambda h: nc.scalar.activation(
                out=sc1[:, 3:4], in_=sc1[:, 1:2], func=AF.Ln),
                r=[("sc1", 1)], w=[("sc1", 3)])
            Sh.dve(lambda h: nc.vector.tensor_add(sc1[:, 4:5], sc1[:, 2:3], sc1[:, 3:4]),
                   r=[("sc1", 2), ("sc1", 3)], w=[("sc1", 4)])
            Sh.act(lambda h: nc.scalar.activation(
                out=sc1[:, 5:6], in_=sc1[:, 4:5], func=AF.Exp, scale=0.5),
                r=[("sc1", 4)], w=[("sc1", 5)])
            Sh.dve(lambda h: nc.vector.tensor_scalar_mul(sc1[:, 6:7], sc1[:, 5:6], -1.0),
                   r=[("sc1", 5)], w=[("sc1", 6)])
            Sh.pe(lambda h: nc.tensor.matmul(
                eB[:, 0:1], p.ones_f[0:1, :], sc1[0:1, 6:7], start=True, stop=True),
                r=[("sc1", 6), ("ones_f",)], w=[("ps2e",)])
            Sh.dve(lambda h: nc.vector.tensor_copy(nb[:], eB[:, 0:1]),
                   r=[("ps2e",)], w=[("nb",)])
            if _NB_ZERO[0]:
                Sh.dve(lambda h: nc.vector.memset(nb[:], 0.0), w=[("nb",)])
            if _DBG_QCH[0] and hh == 2:
                Sh.dma("sp", lambda h: h.dma_start(out=d.dbg_b1[:, :], in_=sc1[:]),
                       r=[("sc1", i) for i in range(7)], w=[])
                Sh.dma("sp", lambda h: h.dma_start(out=d.dbg_b2[:, :], in_=l1r[:]),
                       r=[("l1r",)], w=[])
                Sh.dma("sp", lambda h: h.dma_start(out=d.dbg_b3[:, :], in_=nb[:]),
                       r=[("nb",)], w=[])
                Sh.dma("sp", lambda h: h.dma_start(out=d.dbg_b4[:, :], in_=qbv[:]),
                       r=[("qbv",)], w=[])

            Sh.pe((lambda hh: lambda h: nc.tensor.transpose(
                eB[0:1, 0:128], bvu_c[:, hh:hh + 1], p.id_f[:]))(hh),
                r=[("bvu_c",), ("id_f",)], w=[("ps2e",)])
            Sh.dve(lambda h: nc.vector.tensor_copy(bvro[:], eB[0:1, 0:128]),
                   r=[("ps2e",)], w=[("bvro",)])
            Sh.pe(lambda h: nc.tensor.matmul(
                eB[:, 0:128], p.ones_f[0:1, :], bvro[:], start=True, stop=True),
                r=[("bvro",), ("ones_f",)], w=[("ps2e",)])
            Sh.dve(lambda h: nc.vector.tensor_copy(bvbc[:], eB[:, 0:128]),
                   r=[("ps2e",)], w=[("bvbc",)])
            for kg in range(16):
                bank = kg % 2
                bt = st[bank][:, 0:512]
                for ki in range(4):
                    kt = kg * 4 + ki
                    for cc in range(NCC):
                        Sh.pe((lambda kt, ki, cc, par, bank: lambda h: nc.tensor.matmul(
                            st[bank][:, ki * 128:(ki + 1) * 128],
                            kvc[:, cc, kt * 128:(kt + 1) * 128],
                            wvu[par][:, cc * 128:(cc + 1) * 128],
                            start=(ki == 0 and cc == 0), stop=(ki == 3 and cc == NCC - 1),
                            skip_group_check=True))(kt, ki, cc, par, bank),
                            r=[("wvu", par), ("kvc",)], w=[("ps2st", bank)])
                Sh.dve((lambda kg, bank: lambda h: nc.vector.tensor_tensor(
                    out=v_sb[:, kg * 512:(kg + 1) * 512].rearrange("p (a b) -> p a b", a=4),
                    in0=st[bank][:, 0:512].rearrange("p (a b) -> p a b", a=4),
                    in1=bvbc[:].rearrange("p (o b) -> p o b", o=1).to_broadcast([128, 4, 128]),
                    op=ALU.add))(kg, bank),
                    r=[("ps2st", bank), ("bvbc",)], w=[("v_sb", kg)])

            vk = [("v_sb", kg) for kg in range(16)]
            pending_epi = []

            def flush_epi():
                while pending_epi:
                    pending_epi.pop(0)()

            def mk_epi(jt, hh):
                def epi():
                    Sh.dve((lambda jt: lambda h: nc.vector.reciprocal(
                        rc[:], ctx[jt % 2][0:1, 256:512]))(jt),
                        r=[("ps2c", jt % 2)], w=[("rc",)])
                    Sh.pe(lambda h: nc.tensor.matmul(
                        eB[:, 0:256], p.ones_f[0:1, :], rc[:], start=True, stop=True),
                        r=[("rc",), ("ones_f",)], w=[("ps2e",)])
                    Sh.dve(lambda h: nc.vector.tensor_copy(rb[:], eB[:, 0:256]),
                           r=[("ps2e",)], w=[("rb",)])
                    Sh.dve((lambda jt: lambda h: nc.vector.tensor_tensor(
                        out=ctxev[:], in0=ctx[jt % 2][:, 0:256], in1=rb[:], op=ALU.mult))(jt),
                        r=[("ps2c", jt % 2), ("rb",)], w=[("ctxev",)])
                    Sh.dma("sp", (lambda hh, jt: lambda h: h.dma_start(
                        out=d.z_ctx[hh, :, jt * 256:(jt + 1) * 256], in_=ctxev[:]))(hh, jt),
                        r=[("ctxev",)], w=[])
                return epi

            for pr in range(NTILE // 2):
                jtE, jtO = 2 * pr, 2 * pr + 1
                TE, TO = tile_T(jtE), tile_T(jtO)
                q0 = jtE * 256
                for jt in (jtE, jtO):
                    Sh.dma("sp", (lambda hh, jt: lambda h: h.dma_start(
                        out=msk[jt % 2][:], in_=d.masks[jt, :, :]))(hh, jt),
                        w=[("msk", jt % 2)])

                def s_joint(b):
                    sb = b % 2
                    for i in range(2):
                        kt = 2 * b + i
                        Sh.pe((lambda kt, i, sb, q0: lambda h: nc.tensor.matmul(
                            st[sb][:, i * 512:(i + 1) * 512],
                            kc_sb[:, kt * 128:(kt + 1) * 128], qc_h[:, q0:q0 + 512],
                            start=True, stop=False))(kt, i, sb, q0),
                            r=kck + [("qc_h",)], w=[("ps2st", sb)])
                        Sh.pe((lambda kt, i, sb, q0: lambda h: nc.tensor.matmul(
                            st[sb][:, i * 512:(i + 1) * 512],
                            krope[:, kt * 128:(kt + 1) * 128], qr_h[:, q0:q0 + 512],
                            start=False, stop=True))(kt, i, sb, q0),
                            r=krk + [("qr_h",)], w=[("ps2st", sb)])

                def exp_b(b):
                    sb = b % 2
                    Sh.act((lambda sb: lambda h: nc.scalar.activation(
                        out=pt[sb][:], in_=st[sb][:], func=AF.Exp,
                        bias=nb[:, 0:1], scale=1.0))(sb),
                        r=[("ps2st", sb), ("nb",)], w=[("pt", sb)])

                def pv_joint(b):
                    sb = b % 2
                    for i in range(2):
                        kt = 2 * b + i
                        if kt >= TO - 8:
                            mi = kt - (TO - 8)
                            Sh.dve((lambda sb, i, mi: lambda h: nc.vector.tensor_tensor(
                                out=pt[sb][:, i * 512 + 256:(i + 1) * 512],
                                in0=pt[sb][:, i * 512 + 256:(i + 1) * 512],
                                in1=msk[1][:, mi * 256:(mi + 1) * 256],
                                op=ALU.mult))(sb, i, mi),
                                r=[("pt", sb), ("msk", 1)], w=[("pt", sb)])
                    for i in range(2):
                        kt = 2 * b + i
                        for (jt, off) in ((jtE, 0), (jtO, 256)):
                            Sh.pe((lambda kt, i, sb, jt, off: lambda h: nc.tensor.matmul(
                                ctx[jt % 2][:, 0:256], v_sb[:, kt * 128:(kt + 1) * 128],
                                pt[sb][:, i * 512 + off:i * 512 + off + 256],
                                start=(kt == 0), stop=False,
                                skip_group_check=True))(kt, i, sb, jt, off),
                                r=vk + [("pt", sb)], w=[("ps2c", jt % 2)])
                            Sh.pe((lambda kt, i, sb, jt, off: lambda h: nc.tensor.matmul(
                                ctx[jt % 2][0:1, 256:512], p.ones_b[:],
                                pt[sb][:, i * 512 + off:i * 512 + off + 256],
                                start=False, stop=(jt == jtO and kt == TO - 1),
                                skip_group_check=True))(kt, i, sb, jt, off),
                                r=[("pt", sb), ("ones_b",)], w=[("ps2c", jt % 2)])

                def s_solo(b2):
                    sb = b2 % 2
                    for i in range(4):
                        kt = TO + 4 * b2 + i
                        Sh.pe((lambda kt, i, sb, q0: lambda h: nc.tensor.matmul(
                            st[sb][:, i * 256:(i + 1) * 256],
                            kc_sb[:, kt * 128:(kt + 1) * 128], qc_h[:, q0:q0 + 256],
                            start=True, stop=False))(kt, i, sb, q0),
                            r=kck + [("qc_h",)], w=[("ps2st", sb)])
                        Sh.pe((lambda kt, i, sb, q0: lambda h: nc.tensor.matmul(
                            st[sb][:, i * 256:(i + 1) * 256],
                            krope[:, kt * 128:(kt + 1) * 128], qr_h[:, q0:q0 + 256],
                            start=False, stop=True))(kt, i, sb, q0),
                            r=krk + [("qr_h",)], w=[("ps2st", sb)])

                def pv_solo(b2):
                    sb = b2 % 2
                    for i in range(4):
                        kt = TO + 4 * b2 + i
                        mi = kt - (TE - 8)
                        Sh.dve((lambda sb, i, mi: lambda h: nc.vector.tensor_tensor(
                            out=pt[sb][:, i * 256:(i + 1) * 256],
                            in0=pt[sb][:, i * 256:(i + 1) * 256],
                            in1=msk[0][:, mi * 256:(mi + 1) * 256],
                            op=ALU.mult))(sb, i, mi),
                            r=[("pt", sb), ("msk", 0)], w=[("pt", sb)])
                    for i in range(4):
                        kt = TO + 4 * b2 + i
                        Sh.pe((lambda kt, i, sb: lambda h: nc.tensor.matmul(
                            ctx[0][:, 0:256], v_sb[:, kt * 128:(kt + 1) * 128],
                            pt[sb][:, i * 256:(i + 1) * 256],
                            start=False, stop=False,
                            skip_group_check=True))(kt, i, sb),
                            r=vk + [("pt", sb)], w=[("ps2c", 0)])
                        Sh.pe((lambda kt, i, sb: lambda h: nc.tensor.matmul(
                            ctx[0][0:1, 256:512], p.ones_b[:],
                            pt[sb][:, i * 256:(i + 1) * 256],
                            start=False, stop=(kt == TE - 1),
                            skip_group_check=True))(kt, i, sb),
                            r=[("pt", sb), ("ones_b",)], w=[("ps2c", 0)])

                nbj = TO // 2
                for b in range(nbj):
                    s_joint(b)
                    if b == 1:
                        flush_epi()
                    if b > 0:
                        pv_joint(b - 1)
                    exp_b(b)
                pv_joint(nbj - 1)
                pending_epi.append(mk_epi(jtO, hh))
                for b2 in range(2):
                    s_solo(b2)
                    if b2 > 0:
                        pv_solo(b2 - 1)
                    exp_b(b2)
                pv_solo(1)
                pending_epi.append(mk_epi(jtE, hh))
            flush_epi()
        Sh.barrier()


def _phase2b(p):
    nc, Sh, d = p.nc, p.S, p.d
    with ExitStack() as es:
        T = lambda n, s, dt: es.enter_context(nc.sbuf_tensor(n, s, dt))
        w_o = T("sb_w_o", [128, NDC * D], BF16)
        ctxt = [T(f"ctxt{i}", [128, NH * 128], BF16) for i in range(2)]
        bo_r = T("bo_r", [1, D], F32)
        bo_bc = T("bo_bc", [128, D], F32)
        aout = T("aout", [128, 512], F32)
        ps = [es.enter_context(nc.psum_tensor(f"ps2b{i}", [128, 512], F32)) for i in range(2)]
        psb = es.enter_context(nc.psum_tensor("ps2bb", [128, 512], F32))

        Sh.dma("pool", lambda h: h.dma_start(out=w_o[:], in_=d.w_o[:, :]), w=[("w_o",)])
        Sh.dma("sp", lambda h: h.dma_start(out=bo_r[:], in_=d.b_o[:, :]), w=[("bo_r",)])
        for q in range(4):
            Sh.pe((lambda q: lambda h: nc.tensor.matmul(
                psb[:], p.ones_f[0:1, :], bo_r[0:1, q * 512:(q + 1) * 512],
                start=True, stop=True))(q), r=[("bo_r",), ("ones_f",)], w=[("ps2bb",)])
            Sh.dve((lambda q: lambda h: nc.vector.tensor_copy(
                bo_bc[:, q * 512:(q + 1) * 512], psb[:]))(q),
                r=[("ps2bb",)], w=[("bo_bc",)])
        for t1 in range(16):
            par = t1 % 2
            for hc in range(NH):
                Sh.dma("sp", (lambda t1, hc, par: lambda h: h.dma_start(
                    out=ctxt[par][:, hc * 128:(hc + 1) * 128],
                    in_=d.z_ctx[hc, :, t1 * 128:(t1 + 1) * 128]))(t1, hc, par),
                    w=[("ctxt", par, hc)])
            for h5 in range(4):
                bank = h5 % 2
                for hc in range(NH):
                    Sh.pe((lambda t1, h5, hc, par, bank: lambda h: nc.tensor.matmul(
                        ps[bank][:], ctxt[par][:, hc * 128:(hc + 1) * 128],
                        w_o[:, hc * D + h5 * 512:hc * D + (h5 + 1) * 512],
                        start=(hc == 0), stop=(hc == NH - 1)))(t1, h5, hc, par, bank),
                        r=[("ctxt", par, hc), ("w_o",)], w=[("ps2b", bank)])
                Sh.dve((lambda t1, h5, bank: lambda h: nc.vector.tensor_tensor(
                    out=aout[:], in0=ps[bank][:], in1=bo_bc[:, h5 * 512:(h5 + 1) * 512],
                    op=ALU.add))(t1, h5, bank),
                    r=[("ps2b", bank), ("bo_bc",)], w=[("aout",)])
                Sh.dma("act", (lambda t1, h5: lambda h: h.dma_start(
                    out=d.z_attn[t1 * 128:(t1 + 1) * 128, h5 * 512:(h5 + 1) * 512],
                    in_=aout[:]))(t1, h5), r=[("aout",)], w=[])
        Sh.barrier()


def _phase3(p):
    nc, Sh, d = p.nc, p.S, p.d
    with ExitStack() as es:
        T = lambda n, s, dt: es.enter_context(nc.sbuf_tensor(n, s, dt))
        xn = T("xn3", [128, NDC, OWN], BF16)
        gw = T("gw", [128, NDC * NE], BF16)
        logit = T("logit", [128, 16 * NE], F32)
        nmx = T("nmx", [128, 16], F32)
        se = T("se", [128, 16], F32)
        rse = T("rse", [128, 16], F32)
        esb = T("esb", [128, 16 * NE], F32)
        scrs = T("scrs", [128, 16 * NE], F32)
        t8 = T("t8", [128, 8], F32)
        thr = T("thr", [128, 16], F32)
        msk3 = T("msk3", [128, NE], F32)
        wesb = T("wesb", [128, 16 * NE], F32)
        invf = T("invf", [1, NE], F32)
        invi = T("invi_sb", [1, NE], I32)
        invb = T("invb", [NE, NE], F32)
        iota = T("iota3", [NE, 1], F32)
        pone = T("pone", [NE, NE], F32)
        weT = T("weT", [NE, 128], F32)
        flat = T("flat", [128, 16 * NE], F32)
        sg = T("sg", [128, 16 * GS], F32)
        qq = T("qq", [128, 16 * NE], F32)
        mkd = T("mkd", [128, 16 * NE], F32)
        fmx = T("fmx", [128, 16 * GS], F32)
        fse = T("fse", [128, 16 * GS], F32)
        soft_bf = T("soft_bf", [128, 16 * NE], BF16)
        softT = T("softT", [GS, OWN], BF16)
        sgrow = T("sgrow", [1, OWN], F32)
        sbb = T("sbb", [128, OWN], F32)
        mu_sb = T("mu_sb", [128, NDC * GS], BF16)
        wgb = [T(f"wgb{i}", [128, NDC * 128], BF16) for i in range(2)]
        wub = [T(f"wub{i}", [128, NDC * 128], BF16) for i in range(2)]
        mgb = T("mgb", [128, GS], BF16)
        m_g = T("m_g", [GS, 128], BF16)
        m_u = T("m_u", [GS, 128], BF16)
        sg_bf = T("sg_bf", [128, 512], BF16)
        a1t = T("a1t", [128, 512], BF16)
        a_t = T("a_t", [128, 512], BF16)
        dcast = [T(f"dcast{i}", [128, D], BF16) for i in range(2)]
        ps_g = [es.enter_context(nc.psum_tensor(f"ps3g{i}", [128, 512], F32)) for i in range(2)]
        ps_u = [es.enter_context(nc.psum_tensor(f"ps3u{i}", [128, 512], F32)) for i in range(2)]
        ps_m = es.enter_context(nc.psum_tensor("ps3m", [128, 512], F32))
        ps_x = es.enter_context(nc.psum_tensor("ps3x", [128, 512], F32))
        ps_t = es.enter_context(nc.psum_tensor("ps3t", [128, 512], BF16))

        for dc in range(NDC):
            Sh.dma("sp", (lambda dc: lambda h: h.dma_start(
                out=xn[:, dc, :], in_=d.z_xn[dc, :, :]))(dc), w=[("xn", dc)])
        Sh.dma("pool", lambda h: h.dma_start(out=gw[:], in_=d.gate_wT[:, :]), w=[("gw",)])
        Sh.dma("sp", lambda h: h.dma_start(out=invi[:], in_=d.inv[:, :]), w=[("invi",)])
        Sh.dve(lambda h: nc.vector.tensor_copy(invf[:], invi[:]), r=[("invi",)], w=[("invf",)])
        Sh.pool(lambda h: nc.gpsimd.iota(iota[:], pattern=[[0, 1]], base=0,
                                         channel_multiplier=1,
                                         allow_small_or_imprecise_dtypes=True), w=[("iota",)])
        Sh.pe(lambda h: nc.tensor.matmul(ps_m[0:NE, 0:NE], p.ones_f[0:1, 0:NE], invf[:],
                                         start=True, stop=True),
              r=[("invf",), ("ones_f",)], w=[("ps3m",)])
        Sh.dve(lambda h: nc.vector.tensor_copy(invb[:], ps_m[0:NE, 0:NE]),
               r=[("ps3m",)], w=[("invb",)])
        Sh.dve(lambda h: nc.vector.tensor_single_scalar(
            out=pone[:], in_=invb[:], scalar=iota[:, 0:1], op=ALU.is_equal),
            r=[("invb",), ("iota",)], w=[("pone",)])
        for t1 in range(16):
            bank = t1 % 2
            pj = [ps_g, ps_u][bank][0]
            pk = ("ps3gb", 0) if bank == 0 else ("ps3ub", 0)
            for dc in range(NDC):
                Sh.pe((lambda t1, dc, pj: lambda h: nc.tensor.matmul(
                    pj[:, 0:NE], xn[:, dc, t1 * 128:(t1 + 1) * 128],
                    gw[:, dc * NE:(dc + 1) * NE],
                    start=(dc == 0), stop=(dc == NDC - 1)))(t1, dc, pj),
                    r=[("xn", dc), ("gw",)], w=[pk])
            Sh.dve((lambda t1, pj: lambda h: nc.vector.tensor_copy(
                logit[:, t1 * NE:(t1 + 1) * NE], pj[:, 0:NE]))(t1, pj),
                r=[pk], w=[("logit", t1)])
        for t1 in range(16):
            Sh.dve((lambda t1: lambda h: nc.vector.reduce_max(
                nmx[:, t1:t1 + 1], logit[:, t1 * NE:(t1 + 1) * NE], axis=AX.X,
                negate=True))(t1), r=[("logit", t1)], w=[("nmx", t1)])
            Sh.act((lambda t1: lambda h: nc.scalar.activation(
                out=esb[:, t1 * NE:(t1 + 1) * NE], in_=logit[:, t1 * NE:(t1 + 1) * NE],
                func=AF.Exp, bias=nmx[:, t1:t1 + 1],
                accum_out=se[:, t1:t1 + 1]))(t1),
                r=[("logit", t1), ("nmx", t1)], w=[("esb", t1), ("se", t1)])
        Sh.dve(lambda h: nc.vector.reciprocal(rse[:], se[:]),
               r=[("se", t1) for t1 in range(16)], w=[("rse",)])
        Sh.dve(lambda h: nc.vector.tensor_tensor(
            out=scrs[:].rearrange("p (a b) -> p a b", a=16),
            in0=esb[:].rearrange("p (a b) -> p a b", a=16),
            in1=rse[:].rearrange("p (a o) -> p a o", o=1).to_broadcast([128, 16, NE]),
            op=ALU.mult),
            r=[("esb", t1) for t1 in range(16)] + [("rse",)], w=[("scrs",)])
        for t1 in range(16):
            Sh.dve((lambda t1: lambda h: nc.vector.max(
                out=t8[:, 0:8], in_=scrs[:, t1 * NE:(t1 + 1) * NE]))(t1),
                r=[("scrs",)], w=[("t8",)])
            Sh.dve((lambda t1: lambda h: nc.vector.tensor_copy(
                thr[:, t1:t1 + 1], t8[:, 7:8]))(t1), r=[("t8",)], w=[("thr", t1)])
            Sh.dve((lambda t1: lambda h: nc.vector.tensor_single_scalar(
                out=msk3[:], in_=scrs[:, t1 * NE:(t1 + 1) * NE],
                scalar=thr[:, t1:t1 + 1], op=ALU.is_ge))(t1),
                r=[("scrs",), ("thr", t1)], w=[("msk3",)])
            Sh.dve((lambda t1: lambda h: nc.vector.tensor_mul(
                wesb[:, t1 * NE:(t1 + 1) * NE], scrs[:, t1 * NE:(t1 + 1) * NE],
                msk3[:]))(t1), r=[("scrs",), ("msk3",)], w=[("wesb", t1)])
            Sh.pe((lambda t1: lambda h: nc.tensor.transpose(
                ps_m[0:NE, 0:128], wesb[:, t1 * NE:(t1 + 1) * NE], p.id_f[:]))(t1),
                r=[("wesb", t1), ("id_f",)], w=[("ps3m",)])
            Sh.dve((lambda t1: lambda h: nc.vector.tensor_copy(
                weT[:], ps_m[0:NE, 0:128]))(t1), r=[("ps3m",)], w=[("weT",)])
            Sh.pe((lambda t1: lambda h: nc.tensor.matmul(
                ps_x[:, 0:NE], weT[:], pone[:], start=True, stop=True))(t1),
                r=[("weT",), ("pone",)], w=[("ps3x",)])
            Sh.dve((lambda t1: lambda h: nc.vector.tensor_copy(
                flat[:, t1 * NE:(t1 + 1) * NE], ps_x[:, 0:NE]))(t1),
                r=[("ps3x",)], w=[("flat", t1)])
        fk = [("flat", t1) for t1 in range(16)]
        Sh.dve(lambda h: nc.vector.reduce_sum(
            sg[:].rearrange("p (a g o) -> p a g o", a=16, o=1),
            flat[:].rearrange("p (a g i) -> p a g i", a=16, g=GS), axis=AX.X),
            r=fk, w=[("sg",)])
        Sh.dve(lambda h: nc.vector.tensor_single_scalar(
            out=qq[:], in_=flat[:], scalar=0.0, op=ALU.is_equal), r=fk, w=[("qq",)])
        Sh.dve(lambda h: nc.vector.scalar_tensor_tensor(
            out=mkd[:], in0=qq[:], scalar=-1e9, in1=flat[:], op0=ALU.mult, op1=ALU.add),
            r=[("qq",)] + fk, w=[("mkd",)])
        Sh.dve(lambda h: nc.vector.tensor_reduce(
            fmx[:].rearrange("p (a g o) -> p a g o", a=16, o=1),
            mkd[:].rearrange("p (a g i) -> p a g i", a=16, g=GS), axis=AX.X,
            op=ALU.max), r=[("mkd",)], w=[("fmx",)])
        Sh.dve(lambda h: nc.vector.tensor_tensor(
            out=mkd[:].rearrange("p (a g i) -> p a g i", a=16, g=GS),
            in0=mkd[:].rearrange("p (a g i) -> p a g i", a=16, g=GS),
            in1=fmx[:].rearrange("p (a g o) -> p a g o", a=16, o=1)
            .to_broadcast([128, 16, GS, GS]), op=ALU.subtract),
            r=[("mkd",), ("fmx",)], w=[("mkd",)])
        Sh.act(lambda h: nc.scalar.activation(out=esb[:], in_=mkd[:], func=AF.Exp),
               r=[("mkd",)], w=[("esb", "x")])
        Sh.dve(lambda h: nc.vector.reduce_sum(
            fse[:].rearrange("p (a g o) -> p a g o", a=16, o=1),
            esb[:].rearrange("p (a g i) -> p a g i", a=16, g=GS), axis=AX.X),
            r=[("esb", "x")], w=[("fse",)])
        Sh.dve(lambda h: nc.vector.reciprocal(fse[:], fse[:]), r=[("fse",)], w=[("fse",)])
        Sh.dve(lambda h: nc.vector.tensor_tensor(
            out=soft_bf[:].rearrange("p (a g i) -> p a g i", a=16, g=GS),
            in0=esb[:].rearrange("p (a g i) -> p a g i", a=16, g=GS),
            in1=fse[:].rearrange("p (a g o) -> p a g o", a=16, o=1)
            .to_broadcast([128, 16, GS, GS]), op=ALU.mult),
            r=[("esb", "x"), ("fse",)], w=[("soft_bf",)])

        for g in range(NG):
            for t1 in range(16):
                Sh.pe((lambda g, t1: lambda h: nc.tensor.transpose(
                    ps_t[0:GS, 0:128],
                    soft_bf[:, t1 * NE + g * GS:t1 * NE + (g + 1) * GS], p.id_b[:]))(g, t1),
                    r=[("soft_bf",), ("id_b",)], w=[("ps3t",)])
                Sh.dve((lambda g, t1: lambda h: nc.vector.tensor_copy(
                    softT[:, t1 * 128:(t1 + 1) * 128], ps_t[0:GS, 0:128]))(g, t1),
                    r=[("ps3t",)], w=[("softT",)])
                Sh.pe((lambda g, t1: lambda h: nc.tensor.transpose(
                    ps_x[0:1, 0:128], sg[:, t1 * GS + g:t1 * GS + g + 1], p.id_f[:]))(g, t1),
                    r=[("sg",), ("id_f",)], w=[("ps3x",)])
                Sh.dve((lambda g, t1: lambda h: nc.vector.tensor_copy(
                    sgrow[:, t1 * 128:(t1 + 1) * 128], ps_x[0:1, 0:128]))(g, t1),
                    r=[("ps3x",)], w=[("sgrow",)])
            for t5 in range(4):
                Sh.pe((lambda g, t5: lambda h: nc.tensor.matmul(
                    ps_x[:], p.ones_f[0:1, :], sgrow[0:1, t5 * 512:(t5 + 1) * 512],
                    start=True, stop=True))(g, t5),
                    r=[("sgrow",), ("ones_f",)], w=[("ps3x",)])
                Sh.dve((lambda g, t5: lambda h: nc.vector.tensor_copy(
                    sbb[:, t5 * 512:(t5 + 1) * 512], ps_x[:]))(g, t5),
                    r=[("ps3x",)], w=[("sbb",)])
            Sh.dma("pool", (lambda g: lambda h: h.dma_start(
                out=mu_sb[:], in_=d.maskup[g, :, :]))(g), w=[("mu_sb",)])
            for ic in range(NIC):
                par = ic % 2
                Sh.dma("pool", (lambda g, ic, par: lambda h: h.dma_start(
                    out=wgb[par][:], in_=d.wgT[g, ic, :, :]))(g, ic, par), w=[("wgb", par)])
                Sh.dma("pool", (lambda g, ic, par: lambda h: h.dma_start(
                    out=wub[par][:], in_=d.wuT[g, ic, :, :]))(g, ic, par), w=[("wub", par)])
                Sh.dma("pool", (lambda g, ic, par: lambda h: h.dma_start(
                    out=dcast[par][:], in_=d.downT[g, ic, :, :]))(g, ic, par),
                    w=[("dcast", par)])
                Sh.dma("act", (lambda g, ic, par: lambda h: h.dma_start(
                    out=d.z_dwn[g, ic, :, :], in_=dcast[par][:]))(g, ic, par),
                    r=[("dcast", par)], w=[])
                for (wb, mdst, mkey) in ((wgb, m_g, "m_g"), (wub, m_u, "m_u")):
                    for dc in range(NDC):
                        Sh.pe((lambda wb, par, dc: lambda h: nc.tensor.matmul(
                            ps_m[:, 0:GS], wb[par][:, dc * 128:(dc + 1) * 128],
                            mu_sb[:, dc * GS:(dc + 1) * GS],
                            start=(dc == 0), stop=(dc == NDC - 1)))(wb, par, dc),
                            r=[(("wgb" if wb is wgb else "wub"), par), ("mu_sb",)],
                            w=[("ps3m",)])
                    Sh.dve(lambda h: nc.vector.tensor_copy(mgb[:], ps_m[:, 0:GS]),
                           r=[("ps3m",)], w=[("mgb",)])
                    Sh.pe(lambda h: nc.tensor.transpose(
                        ps_t[0:GS, 0:128], mgb[:], p.id_b[:]),
                        r=[("mgb",), ("id_b",)], w=[("ps3t",)])
                    Sh.dve((lambda mdst, mkey: lambda h: nc.vector.tensor_copy(
                        mdst[:], ps_t[0:GS, 0:128]))(mdst, mkey),
                        r=[("ps3t",)], w=[(mkey,)])
                for t5 in range(4):
                    bg = t5 % 2
                    for dc in range(NDC):
                        Sh.pe((lambda par, dc, t5, bg: lambda h: nc.tensor.matmul(
                            ps_g[bg][:], wgb[par][:, dc * 128:(dc + 1) * 128],
                            xn[:, dc, t5 * 512:(t5 + 1) * 512],
                            start=(dc == 0), stop=False))(par, dc, t5, bg),
                            r=[("wgb", par), ("xn", dc)], w=[("ps3gb", bg)])
                    Sh.pe((lambda t5, bg: lambda h: nc.tensor.matmul(
                        ps_g[bg][:], m_g[:], softT[:, t5 * 512:(t5 + 1) * 512],
                        start=False, stop=True))(t5, bg),
                        r=[("m_g",), ("softT",)], w=[("ps3gb", bg)])
                    for dc in range(NDC):
                        Sh.pe((lambda par, dc, t5, bg: lambda h: nc.tensor.matmul(
                            ps_u[bg][:], wub[par][:, dc * 128:(dc + 1) * 128],
                            xn[:, dc, t5 * 512:(t5 + 1) * 512],
                            start=(dc == 0), stop=False))(par, dc, t5, bg),
                            r=[("wub", par), ("xn", dc)], w=[("ps3ub", bg)])
                    Sh.pe((lambda t5, bg: lambda h: nc.tensor.matmul(
                        ps_u[bg][:], m_u[:], softT[:, t5 * 512:(t5 + 1) * 512],
                        start=False, stop=True))(t5, bg),
                        r=[("m_u",), ("softT",)], w=[("ps3ub", bg)])
                    Sh.act((lambda t5, bg: lambda h: nc.scalar.activation(
                        out=sg_bf[:], in_=ps_g[bg][:], func=AF.Silu))(t5, bg),
                        r=[("ps3gb", bg)], w=[("sg_bf",)])
                    Sh.dve((lambda t5, bg: lambda h: nc.vector.tensor_tensor(
                        out=a1t[:], in0=ps_u[bg][:], in1=sg_bf[:], op=ALU.mult))(t5, bg),
                        r=[("ps3ub", bg), ("sg_bf",)], w=[("a1t",)])
                    Sh.dve((lambda t5: lambda h: nc.vector.tensor_tensor(
                        out=a_t[:], in0=a1t[:], in1=sbb[:, t5 * 512:(t5 + 1) * 512],
                        op=ALU.mult))(t5), r=[("a1t",), ("sbb",)], w=[("a_t",)])
                    Sh.dma("act", (lambda g, ic, t5: lambda h: h.dma_start(
                        out=d.z_a[g, ic, :, t5 * 512:(t5 + 1) * 512], in_=a_t[:]))(g, ic, t5),
                        r=[("a_t",)], w=[])
        Sh.barrier()

    with ExitStack() as es:
        T = lambda n, s, dt: es.enter_context(nc.sbuf_tensor(n, s, dt))
        dwn = [T(f"dwn{i}", [128, 512], BF16) for i in range(6)]
        abuf = [T(f"abuf{i}", [128, 1024], BF16) for i in range(6)]
        x_t = [T(f"x_t{i}", [128, 512], F32) for i in range(2)]
        at_t = [T(f"at_t{i}", [128, 512], F32) for i in range(2)]
        tmp = T("tmp3", [128, 512], F32)
        osb = T("osb", [128, 512], F32)
        psd = [es.enter_context(nc.psum_tensor(f"ps3d{i}", [128, 512], F32)) for i in range(8)]

        for th in range(2):
            for h5 in range(4):
                for g in range(NG):
                    for ic in range(NIC):
                        par = (g * NIC + ic) % 6
                        Sh.dma("act", (lambda g, ic, h5, par: lambda h: h.dma_start(
                            out=dwn[par][:], in_=d.z_dwn[g, ic, :, h5 * 512:(h5 + 1) * 512]))(g, ic, h5, par),
                            w=[("dwn", par)])
                        Sh.dma("pool", (lambda g, ic, th, par: lambda h: h.dma_start(
                            out=abuf[par][:], in_=d.z_a[g, ic, :, th * 1024:(th + 1) * 1024]))(g, ic, th, par),
                            w=[("abuf", par)])
                        first = (g == 0 and ic == 0)
                        last = (g == NG - 1 and ic == NIC - 1)
                        for t1 in range(8):
                            Sh.pe((lambda g, ic, t1, par, first, last: lambda h: nc.tensor.matmul(
                                psd[t1][:], abuf[par][:, t1 * 128:(t1 + 1) * 128],
                                dwn[par][:], start=first, stop=last))(g, ic, t1, par, first, last),
                                r=[("abuf", par), ("dwn", par)], w=[("ps3d", t1)])
                for t1 in range(8):
                    par = t1 % 2
                    tg = th * 8 + t1
                    Sh.dma("sp", (lambda tg, h5, par: lambda h: h.dma_start(
                        out=x_t[par][:], in_=d.xo[tg * 128:(tg + 1) * 128, h5 * 512:(h5 + 1) * 512]))(tg, h5, par),
                        w=[("x_t", par)])
                    Sh.dma("sp", (lambda tg, h5, par: lambda h: h.dma_start(
                        out=at_t[par][:], in_=d.z_attn[tg * 128:(tg + 1) * 128, h5 * 512:(h5 + 1) * 512]))(tg, h5, par),
                        w=[("at_t", par)])
                    Sh.dve((lambda t1, par: lambda h: nc.vector.tensor_tensor(
                        out=tmp[:], in0=psd[t1][:], in1=at_t[par][:], op=ALU.add))(t1, par),
                        r=[("ps3d", t1), ("at_t", par)], w=[("tmp3",)])
                    Sh.dve((lambda par: lambda h: nc.vector.tensor_tensor(
                        out=osb[:], in0=tmp[:], in1=x_t[par][:], op=ALU.add))(par),
                        r=[("tmp3",), ("x_t", par)], w=[("osb",)])
                    Sh.dma("sp", (lambda tg, h5: lambda h: h.dma_start(
                        out=d.out[tg * 128:(tg + 1) * 128, h5 * 512:(h5 + 1) * 512],
                        in_=osb[:]))(tg, h5), r=[("osb",)], w=[])
        Sh.barrier()


def _finish(p, upto):
    nc, Sh, d = p.nc, p.S, p.d
    if upto >= 5:
        return
    with ExitStack() as es:
        t = es.enter_context(nc.sbuf_tensor("fin_t", [128, D], F32))
        for j in range(NCHUNK):
            Sh.dma("sp", (lambda j: lambda h: h.dma_start(
                out=t[:], in_=d.xo[j * 128:(j + 1) * 128, :]))(j), r=(), w=[("fin_t",)])
            Sh.dma("sp", (lambda j: lambda h: h.dma_start(
                out=d.out[j * 128:(j + 1) * 128, :], in_=t[:]))(j), r=[("fin_t",)], w=())
        Sh.barrier()


def _rope_tables(positions):
    inv_freq = 1.0 / (10000.0 ** (np.arange(0, RD, 2, dtype=np.float64) / RD))
    ang = positions[None, :].astype(np.float64) * inv_freq[:, None]
    return (np.cos(ang).astype(np.float32).astype(bfdt),
            np.sin(ang).astype(np.float32).astype(bfdt))


def _host_masks(rows):
    m = np.zeros((NTILE, 8, 128, 256), np.float32)
    lim = rows.reshape(NTILE, 256) + 1
    for j in range(NTILE):
        T = tile_T(j)
        for i in range(8):
            kt = T - 8 + i
            kpos = 128 * kt + np.arange(128)
            m[j, i] = (kpos[:, None] < lim[j][None, :]).astype(np.float32)
    return np.ascontiguousarray(m.transpose(0, 2, 1, 3).reshape(NTILE, 128, 8 * 256)).astype(bfdt)


def _host_shared(g):
    f = lambda a: np.ascontiguousarray(np.asarray(a, np.float32))
    C = np.ascontiguousarray

    def pmaj_in(w, nch):
        n = w.shape[1]
        return C(w.reshape(nch, 128, n).transpose(1, 0, 2).reshape(128, nch * n))

    def pmaj_heads(w, nch, hd):
        n = w.shape[1] // hd
        a = w.reshape(nch, 128, n, hd).transpose(2, 1, 0, 3)
        return C(a.reshape(n, 128, nch * hd))

    shared = {
        "w_kvd": pmaj_in(f(g["L0_attn_W_kvd"]), NDC),
        "b_kvd": C(f(g["L0_attn_b_kvd"]).reshape(NCC, 128).T),
        "w_qd": pmaj_in(f(g["L0_attn_W_qd"]), NDC),
        "b_qd": C(f(g["L0_attn_b_qd"]).reshape(NQC, 128).T),
        "w_ku": pmaj_heads(f(g["L0_attn_W_ku"]), NCC, 128),
        "b_ku": C(f(g["L0_attn_b_ku"]).reshape(NH, 128).T),
        "w_vu": pmaj_heads(f(g["L0_attn_W_vu"]), NCC, 128),
        "b_vu": C(f(g["L0_attn_b_vu"]).reshape(NH, 128).T),
        "w_kr": pmaj_heads(f(g["L0_attn_W_kr"]), NCC, RD),
        "b_kr": C(f(g["L0_attn_b_kr"]).reshape(NH, RD).T),
        "w_qu": pmaj_heads(f(g["L0_attn_W_qu"]), NQC, 128),
        "b_qu": C(f(g["L0_attn_b_qu"]).reshape(NH, 128).T),
        "w_qr": pmaj_heads(f(g["L0_attn_W_qr"]), NQC, RD),
        "b_qr": C(f(g["L0_attn_b_qr"]).reshape(NH, RD).T),
        "w_o": pmaj_in(f(g["L0_attn_W_o"]), NDC),
        "b_o": f(g["L0_attn_b_o"]).reshape(1, D),
        "b_ku_r": f(g["L0_attn_b_ku"]).reshape(1, D),
        "b_kr_r": f(g["L0_attn_b_kr"]).reshape(1, NH * RD),
        "gate_wT": pmaj_in(C(f(g["L0_ffn_gate_w"]).T), NDC),
        "maskup": np.stack([pmaj_in(f(g["L0_ffn_mask_up_w"][i]), NDC) for i in range(NG)]),
        "wgT": np.stack([np.stack([pmaj_in(C(f(g["L0_ffn_gate_proj_w"][i]).T)[:, j * 128:(j + 1) * 128], NDC)
                                   for j in range(NIC)]) for i in range(NG)]),
        "wuT": np.stack([np.stack([pmaj_in(C(f(g["L0_ffn_up_proj_w"][i]).T)[:, j * 128:(j + 1) * 128], NDC)
                                   for j in range(NIC)]) for i in range(NG)]),
        "downT": np.stack([C(f(g["L0_ffn_down_proj_w"][i]).T).reshape(NIC, 128, D) for i in range(NG)]),
        "inv": np.asarray(g["L0_ffn_inv_map"], np.int64).astype(np.int32).reshape(1, NE),
        "ident": np.eye(128, dtype=bfdt),
        "identf": np.eye(128, dtype=np.float32),
    }
    kc, ks = _rope_tables(np.arange(S))
    shared["ktbl_c"], shared["ktbl_s"] = kc, ks
    return shared


def _host_core(x, c):
    b, qi = c // 4, c % 4
    chunks = own_chunks(qi)
    rows = np.concatenate([np.arange(128 * gch, 128 * gch + 128) for gch in chunks])
    qc_, qs_ = _rope_tables(rows)
    im = {
        "xb": x[b],
        "xo": np.ascontiguousarray(x[b][rows]),
        "qtbl_c": qc_, "qtbl_s": qs_,
        "masks": _host_masks(rows),
    }
    return im, b, rows


def kernel(**inputs):
    g = {k: np.asarray(v) for k, v in inputs.items()}
    x = np.ascontiguousarray(g["L0_attn_x"].astype(np.float32, copy=False))
    shared = _host_shared(g)

    nc = _build()

    in_maps = []
    owns = []
    for c in range(8):
        im, b, rows = _host_core(x, c)
        owns.append((b, rows))
        im.update(shared)
        in_maps.append(im)

    res = run_bass_kernel_spmd(nc, in_maps, core_ids=list(range(8)))

    out = np.empty((B, S, D), np.float32)
    for c in range(8):
        b, rows = owns[c]
        out[b][rows] = np.asarray(res.results[c]["out"], np.float32)
    return out


# version 11
# speedup vs baseline: 1.0116x; 1.0116x over previous
import numpy as np
import ml_dtypes
from contextlib import ExitStack

import concourse.bass as bass
import concourse.mybir as mybir
from concourse.bass_utils import run_bass_kernel_spmd

F32 = mybir.dt.float32
BF16 = mybir.dt.bfloat16
I32 = mybir.dt.int32
AF = mybir.ActivationFunctionType
ALU = mybir.AluOpType
AX = mybir.AxisListType

B, S, D = 2, 8192, 2048
NH, HD, RD = 16, 128, 64
KVC, QC = 512, 1536
NG, GS, NE, GI = 8, 8, 64, 1408
OWN = 2048
NCHUNK = 16
NDC = D // 128
NCC = KVC // 128
NQC = QC // 128
NIC = GI // 128
SCALE = 1.0 / float(np.sqrt(HD + RD))
EPS = 1e-5
NTILE = 8
bfdt = ml_dtypes.bfloat16


def own_chunks(qi):
    return [63 - qi - 4 * j for j in range(NCHUNK)]


def tile_T(j):
    return 64 - 8 * j


class _Op:
    __slots__ = ("idx", "eng", "emit", "deps", "dma", "sem", "inc", "cnt", "waits", "wlist")


NQSUB = 6


class Sched:
    def __init__(self, nc, es):
        self.nc = nc
        self.ops = []
        self.w = {}
        self.r = {}
        self.sems = {}
        for name in ("e:pe", "e:act", "e:dve", "e:pool"):
            self.sems[name] = es.enter_context(nc.semaphore(name.replace(":", "_")))
        for q in ("sp", "pool", "act"):
            for i in range(NQSUB):
                self.sems[("q:" + q, i)] = es.enter_context(nc.semaphore(f"q_{q}_{i}"))
        self.cnt = {k: 0 for k in self.sems}
        self.qn = {"sp": 0, "pool": 0, "act": 0}

    def op(self, eng, emit, reads=(), writes=(), dma=False):
        idx = len(self.ops)
        deps = set()
        for k in reads:
            j = self.w.get(k)
            if j is not None:
                deps.add(j)
        for k in writes:
            j = self.w.get(k)
            if j is not None:
                deps.add(j)
            deps.update(self.r.get(k, ()))
        o = _Op()
        o.idx, o.eng, o.emit, o.deps, o.dma = idx, eng, emit, deps, dma
        if dma:
            n = self.qn[eng]
            self.qn[eng] = n + 1
            o.sem = ("q:" + eng, n % NQSUB)
            o.inc = 16
        else:
            o.sem = "e:" + eng
            o.inc = 1
        self.cnt[o.sem] += o.inc
        o.cnt = self.cnt[o.sem]
        self.ops.append(o)
        for k in reads:
            self.r.setdefault(k, []).append(idx)
        for k in writes:
            self.w[k] = idx
            self.r[k] = []
        return idx

    def pe(self, emit, r=(), w=()):
        return self.op("pe", emit, r, w)

    def dve(self, emit, r=(), w=()):
        return self.op("dve", emit, r, w)

    def act(self, emit, r=(), w=()):
        return self.op("act", emit, r, w)

    def pool(self, emit, r=(), w=()):
        return self.op("pool", emit, r, w)

    def dma(self, eng, emit, r=(), w=()):
        return self.op(eng, emit, r, w, dma=True)

    def barrier(self):
        snapshot = dict(self.cnt)
        for eng in ("pe", "act", "dve", "pool", "sp"):
            o = _Op()
            o.idx, o.eng, o.emit, o.dma = len(self.ops), eng, None, False
            o.deps = set()
            o.sem, o.inc, o.cnt = None, 0, None
            o.waits = snapshot
            self.ops.append(o)

    def finalize(self):
        engvc = {e: {} for e in ("pe", "act", "dve", "pool", "sp")}
        opvc = [None] * len(self.ops)
        for o in self.ops:
            if o.emit is None and o.sem is None:
                vc = engvc[o.eng]
                o.wlist = []
                for semname, val in o.waits.items():
                    if val > vc.get(semname, 0):
                        o.wlist.append((semname, val))
                        vc[semname] = val
                opvc[o.idx] = dict(vc)
                continue
            vc = engvc[o.eng]
            wl = []
            if o.dma and o.cnt > 16 and vc.get(o.sem, 0) < o.cnt - 16:
                wl.append((o.sem, o.cnt - 16))
                vc[o.sem] = o.cnt - 16
            for j in sorted(o.deps):
                d = self.ops[j]
                if d.sem is None:
                    continue
                if d.eng == "pe" and o.eng == "pe" and not d.dma and not o.dma:
                    continue
                if vc.get(d.sem, 0) >= d.cnt:
                    continue
                wl.append((d.sem, d.cnt))
                vc[d.sem] = d.cnt
                dvc = opvc[j]
                if dvc:
                    for s2, v2 in dvc.items():
                        if v2 > vc.get(s2, 0):
                            vc[s2] = v2
            o.wlist = wl
            myvc = dict(vc)
            myvc[o.sem] = max(myvc.get(o.sem, 0), o.cnt)
            opvc[o.idx] = myvc

        nc = self.nc
        with nc.Block() as block:
            def run_engine(name, handle):
                for o in self.ops:
                    if o.eng != name:
                        continue
                    for semname, val in o.wlist:
                        handle.wait_ge(self.sems[semname], val)
                    if o.emit is not None:
                        o.emit(handle).then_inc(self.sems[o.sem], o.inc)

            @block.sync
            def _(h):
                run_engine("sp", h)

            @block.gpsimd
            def _(h):
                run_engine("pool", h)

            @block.vector
            def _(h):
                run_engine("dve", h)

            @block.scalar
            def _(h):
                run_engine("act", h)

            @block.tensor
            def _(h):
                run_engine("pe", h)


class Ctx:
    pass


def _declare_io(nc, stage=()):
    d = Ctx()
    def din(name, shape, dt):
        t = nc.dram_tensor(name, list(shape), dt, kind="ExternalInput").ap()
        setattr(d, name, t)
        return t

    din("xb", [S, D], F32)
    din("xo", [OWN, D], F32)
    din("w_kvd", [128, NDC * KVC], F32); din("b_kvd", [128, NCC], F32)
    din("w_qd", [128, NDC * QC], F32); din("b_qd", [128, NQC], F32)
    din("w_ku", [NH, 128, NCC * 128], F32); din("b_ku", [128, NH], F32)
    din("w_vu", [NH, 128, NCC * 128], F32); din("b_vu", [128, NH], F32)
    din("w_kr", [NH, 128, NCC * RD], F32); din("b_kr", [RD, NH], F32)
    din("w_qu", [NH, 128, NQC * 128], F32); din("b_qu", [128, NH], F32)
    din("w_qr", [NH, 128, NQC * RD], F32); din("b_qr", [RD, NH], F32)
    din("w_o", [128, NDC * D], F32); din("b_o", [1, D], F32)
    din("b_ku_r", [1, D], F32); din("b_kr_r", [1, NH * RD], F32)
    din("gate_wT", [128, NDC * NE], F32)
    din("maskup", [NG, 128, NDC * GS], F32)
    din("wgT", [NG, NIC, 128, NDC * 128], F32)
    din("wuT", [NG, NIC, 128, NDC * 128], F32)
    din("downT", [NG, NIC, 128, D], F32)
    din("inv", [1, NE], I32)
    din("ktbl_c", [32, S], BF16); din("ktbl_s", [32, S], BF16)
    din("qtbl_c", [32, OWN], BF16); din("qtbl_s", [32, OWN], BF16)
    din("masks", [NTILE, 128, 8 * 256], BF16)
    din("ident", [128, 128], BF16)
    din("identf", [128, 128], F32)

    d.out = nc.dram_tensor("out", [OWN, D], F32, kind="ExternalOutput").ap()
    def scr(name, shape, dt):
        kind = "ExternalInput" if name in stage else "Internal"
        t = nc.dram_tensor(name, list(shape), dt, kind=kind).ap()
        setattr(d, name, t)
        return t
    scr("z_qb", [1, NH], F32)
    scr("z_kvc", [NCC, 128, S], BF16)
    scr("z_qc", [NQC, 128, OWN], BF16)
    scr("z_xn", [NDC, 128, OWN], BF16)
    scr("z_q", [NH, 128, OWN], BF16)
    scr("z_qr", [NH, RD, OWN], BF16)
    scr("z_ctx", [NH, 128, OWN], BF16)
    scr("z_attn", [OWN, D], F32)
    scr("z_a", [NG, NIC, 128, OWN], BF16)
    scr("z_dwn", [NG, NIC, 128, D], BF16)
    return d


_SCHED_REF = []


def _build(upto=99, debug=(), upfrom=0, stage=()):
    nc = bass.Bass("TRN2", target_bir_lowering=False, detect_race_conditions=False,
                   dynamic_dma_scratch_size=16384)
    d = _declare_io(nc, stage)
    dbg_outs = {}
    for name in debug:
        zt = getattr(d, name)
        dbg_outs[name] = nc.dram_tensor("dbg_" + name, list(zt.shape), zt.dtype,
                                        kind="ExternalOutput").ap()
    root = ExitStack()
    with root:
        Sh = Sched(nc, root)
        _SCHED_REF.clear(); _SCHED_REF.append(Sh)
        ones_b = root.enter_context(nc.sbuf_tensor("ones_b", [128, 1], BF16))
        ones_f = root.enter_context(nc.sbuf_tensor("ones_f", [1, 128], F32))
        id_b = root.enter_context(nc.sbuf_tensor("id_b", [128, 128], BF16))
        id_f = root.enter_context(nc.sbuf_tensor("id_f", [128, 128], F32))
        qb = root.enter_context(nc.sbuf_tensor("qb", [1, NH], F32))
        p = Ctx()
        p.nc, p.S, p.d = nc, Sh, d
        p.ones_b, p.ones_f, p.id_b, p.id_f, p.qb = ones_b, ones_f, id_b, id_f, qb

        Sh.dve(lambda h: nc.vector.memset(ones_b[:], 1.0), w=[("ones_b",)])
        Sh.dve(lambda h: nc.vector.memset(ones_f[:], 1.0), w=[("ones_f",)])
        Sh.dma("sp", lambda h: h.dma_start(out=id_b[:], in_=d.ident[:, :]), w=[("id_b",)])
        Sh.dma("sp", lambda h: h.dma_start(out=id_f[:], in_=d.identf[:, :]), w=[("id_f",)])

        if upfrom <= 1:
            _phase1a(p)
        if upto >= 2 and upfrom <= 2:
            _phase1b(p)
        if upto >= 3 and upfrom <= 3:
            _phase2(p)
        if upto >= 4 and upfrom <= 4:
            _phase2b(p)
        if upto >= 5:
            _phase3(p)
        _finish(p, upto)
        for name, oap in dbg_outs.items():
            zt = getattr(d, name)
            sl = tuple(slice(None) for _ in zt.shape)
            Sh.dma("sp", (lambda zt, oap, sl: lambda h: h.dma_start(out=oap[sl], in_=zt[sl]))(zt, oap, sl))
        Sh.barrier()
        Sh.finalize()
    return nc


def _ln_block(p, es, src_ap, blk, x_tm, xn_tm, stats, kx):
    nc, Sh = p.nc, p.S
    ssq, sums, mean, var, sd, rstd, sqs, msq = stats
    Sh.dma("sp", lambda h: h.dma_start(
        out=x_tm[:], in_=src_ap[blk * 512:(blk + 1) * 512, :].rearrange("(t p) d -> p t d", p=128)),
        w=[(kx, "xtm")])
    for t in range(4):
        Sh.act((lambda t: lambda h: nc.scalar.activation(
            out=sqs[:, t * D:(t + 1) * D], in_=x_tm[:, t, :], func=AF.Square,
            accum_out=ssq[:, t:t + 1]))(t),
            r=[(kx, "xtm")], w=[(kx, "ssq", t), (kx, "sqs", t)])
        Sh.dve((lambda t: lambda h: nc.vector.reduce_sum(
            sums[:, t:t + 1], x_tm[:, t, :], axis=AX.X))(t),
            r=[(kx, "xtm")], w=[(kx, "sum", t)])
    Sh.dve(lambda h: nc.vector.tensor_scalar_mul(mean[:], sums[:], 1.0 / D),
           r=[(kx, "sum", t) for t in range(4)], w=[(kx, "mean")])
    Sh.dve(lambda h: nc.vector.tensor_scalar_mul(var[:], ssq[:], 1.0 / D),
           r=[(kx, "ssq", t) for t in range(4)], w=[(kx, "var")])
    Sh.dve(lambda h: nc.vector.tensor_tensor(out=msq[:], in0=mean[:], in1=mean[:], op=ALU.mult),
           r=[(kx, "mean")], w=[(kx, "msq")])
    Sh.dve(lambda h: nc.vector.tensor_sub(var[:], var[:], msq[:]),
           r=[(kx, "var"), (kx, "msq")], w=[(kx, "var")])
    Sh.dve(lambda h: nc.vector.tensor_scalar_add(var[:], var[:], EPS),
           r=[(kx, "var")], w=[(kx, "var")])
    Sh.act(lambda h: nc.scalar.activation(out=sd[:], in_=var[:], func=AF.Sqrt),
           r=[(kx, "var")], w=[(kx, "sd")])
    Sh.dve(lambda h: nc.vector.reciprocal(rstd[:], sd[:]),
           r=[(kx, "sd")], w=[(kx, "rstd")])
    for t in range(4):
        Sh.dve((lambda t: lambda h: nc.vector.scalar_tensor_tensor(
            out=xn_tm[:, t, :], in0=x_tm[:, t, :], scalar=mean[:, t:t + 1],
            in1=rstd[:, t:t + 1].to_broadcast([128, D]),
            op0=ALU.subtract, op1=ALU.mult))(t),
            r=[(kx, "xtm"), (kx, "mean"), (kx, "rstd")], w=[(kx, "xn", t)])


def _transpose_block(p, es, xn_tm, xn_fm, ps_tp, kx, kfm):
    nc, Sh = p.nc, p.S
    for dc in range(NDC):
        bank = dc % 2
        for t in range(4):
            Sh.pe((lambda dc, t, bank: lambda h: nc.tensor.transpose(
                ps_tp[bank][:, t * 128:(t + 1) * 128], xn_tm[:, t, dc * 128:(dc + 1) * 128],
                p.id_b[:]))(dc, t, bank),
                r=[(kx, "xn", t), ("id_b",)], w=[("ps_tp", bank, t)])
        Sh.dve((lambda dc, bank: lambda h: nc.vector.tensor_copy(
            xn_fm[:, dc * 512:(dc + 1) * 512], ps_tp[bank][:]))(dc, bank),
            r=[("ps_tp", bank, t) for t in range(4)], w=[kfm + (dc,)])


def _phase1a(p):
    nc, Sh, d = p.nc, p.S, p.d
    with ExitStack() as es:
        x_tm2 = [es.enter_context(nc.sbuf_tensor(f"x_tm{i}", [128, 4, D], F32)) for i in range(2)]
        xn_tm2 = [es.enter_context(nc.sbuf_tensor(f"xn_tm{i}", [128, 4, D], BF16)) for i in range(2)]
        xn_fm = [es.enter_context(nc.sbuf_tensor(f"xn_fm{i}", [128, NDC * 512], BF16)) for i in range(2)]
        sqs = es.enter_context(nc.sbuf_tensor("sqs", [128, 4 * D], BF16))
        stats2 = []
        for i in range(2):
            cols = [es.enter_context(nc.sbuf_tensor(f"lnst{i}", [128, 4 * 7], F32))]
            c = cols[0]
            stats2.append((c[:, 0:4], c[:, 4:8], c[:, 8:12], c[:, 12:16], c[:, 16:20], c[:, 20:24], sqs, c[:, 24:28]))
        w_big = es.enter_context(nc.sbuf_tensor("w_big", [128, NDC * QC], BF16))
        b_kvd = es.enter_context(nc.sbuf_tensor("sb_b_kvd", [128, NCC], F32))
        pco = es.enter_context(nc.sbuf_tensor("pco", [128, NQC * 512], BF16))
        kvco = pco[:, 0:NCC * 512]
        ps_tp = [es.enter_context(nc.psum_tensor(f"ps_tp{i}", [128, 512], BF16)) for i in range(2)]
        ps_kv = [es.enter_context(nc.psum_tensor(f"ps_kv{i}", [128, 512], F32)) for i in range(2)]

        Sh.dma("pool", lambda h: h.dma_start(
            out=w_big[:, 0:NDC * KVC], in_=d.w_kvd[:, :]), w=[("w_kvd",)])
        Sh.dma("sp", lambda h: h.dma_start(
            out=b_kvd[:], in_=d.b_kvd[:, :]), w=[("b_kvd",)])

        for blk in range(S // 512):
            kx = ("b", blk % 2)
            _ln_block(p, es, d.xb, blk, x_tm2[blk % 2], xn_tm2[blk % 2], stats2[blk % 2], kx)
            _transpose_block(p, es, xn_tm2[blk % 2], xn_fm[blk % 2], ps_tp, kx, ("fm", blk % 2))
            for cc in range(NCC):
                bank = cc % 2
                for dc in range(NDC):
                    Sh.pe((lambda cc, dc, bank, blk: lambda h: nc.tensor.matmul(
                        ps_kv[bank][:], w_big[:, dc * KVC + cc * 128:dc * KVC + (cc + 1) * 128],
                        xn_fm[blk % 2][:, dc * 512:(dc + 1) * 512],
                        start=(dc == 0), stop=(dc == NDC - 1)))(cc, dc, bank, blk),
                        r=[("w_kvd",), ("fm", blk % 2, dc)], w=[("ps_kv", bank)])
                Sh.act((lambda cc, bank: lambda h: nc.scalar.activation(
                    out=pco[:, cc * 512:(cc + 1) * 512], in_=ps_kv[bank][:],
                    func=AF.Identity, bias=b_kvd[:, cc:cc + 1], scale=1.0))(cc, bank),
                    r=[("ps_kv", bank), ("b_kvd",)], w=[("kvco", cc)])
                Sh.dma("act", (lambda cc, blk: lambda h: h.dma_start(
                    out=d.z_kvc[cc, :, blk * 512:(blk + 1) * 512],
                    in_=pco[:, cc * 512:(cc + 1) * 512]))(cc, blk),
                    r=[("kvco", cc)], w=[])
        Sh.barrier()

        b_qd = es.enter_context(nc.sbuf_tensor("sb_b_qd", [128, NQC], F32))
        qco = pco
        ps_q = ps_kv

        Sh.dma("pool", lambda h: h.dma_start(
            out=w_big[:, :], in_=d.w_qd[:, :]), w=[("w_qd",)])
        Sh.dma("sp", lambda h: h.dma_start(
            out=b_qd[:], in_=d.b_qd[:, :]), w=[("b_qd",)])

        for blk in range(OWN // 512):
            kx = ("o", blk % 2)
            _ln_block(p, es, d.xo, blk, x_tm2[blk % 2], xn_tm2[blk % 2], stats2[blk % 2], kx)
            _transpose_block(p, es, xn_tm2[blk % 2], xn_fm[blk % 2], ps_tp, kx, ("fm2", blk % 2))
            for dc in range(NDC):
                Sh.dma("sp", (lambda dc, blk: lambda h: h.dma_start(
                    out=d.z_xn[dc, :, blk * 512:(blk + 1) * 512],
                    in_=xn_fm[blk % 2][:, dc * 512:(dc + 1) * 512]))(dc, blk),
                    r=[("fm2", blk % 2, dc)], w=[])
            for qc in range(NQC):
                bank = qc % 2
                for dc in range(NDC):
                    Sh.pe((lambda qc, dc, bank, blk: lambda h: nc.tensor.matmul(
                        ps_q[bank][:], w_big[:, dc * QC + qc * 128:dc * QC + (qc + 1) * 128],
                        xn_fm[blk % 2][:, dc * 512:(dc + 1) * 512],
                        start=(dc == 0), stop=(dc == NDC - 1)))(qc, dc, bank, blk),
                        r=[("w_qd",), ("fm2", blk % 2, dc)], w=[("ps_q", bank)])
                Sh.act((lambda qc, bank: lambda h: nc.scalar.activation(
                    out=qco[:, qc * 512:(qc + 1) * 512], in_=ps_q[bank][:],
                    func=AF.Identity, bias=b_qd[:, qc:qc + 1], scale=1.0))(qc, bank),
                    r=[("ps_q", bank), ("b_qd",)], w=[("qco", qc)])
                Sh.dma("act", (lambda qc, blk: lambda h: h.dma_start(
                    out=d.z_qc[qc, :, blk * 512:(blk + 1) * 512],
                    in_=qco[:, qc * 512:(qc + 1) * 512]))(qc, blk),
                    r=[("qco", qc)], w=[])
        Sh.barrier()


_DBG_QCH = [False]
_P1B_PARTS = {"qr", "rope", "bound"}
_NB_ZERO = [False]


def _phase1b(p):
    nc, Sh, d = p.nc, p.S, p.d
    if _DBG_QCH[0]:
        d.dbg_qch = nc.dram_tensor("dbgq", [128, OWN], F32, kind="ExternalOutput").ap()
        d.dbg_ps = nc.dram_tensor("dbgps", [128, 512], F32, kind="ExternalOutput").ap()
        d.dbg_qcsb = nc.dram_tensor("dbgqcsb", [128, NQC * OWN], BF16, kind="ExternalOutput").ap()
    with ExitStack() as es:
        qc_sb = es.enter_context(nc.sbuf_tensor("qc_sb", [128, NQC, OWN], BF16))
        w_qu = [es.enter_context(nc.sbuf_tensor(f"sb_w_qu{i}", [128, NQC * 128], BF16)) for i in range(2)]
        w_qr = [es.enter_context(nc.sbuf_tensor(f"sb_w_qr{i}", [128, NQC * RD], BF16)) for i in range(2)]
        b_qu = es.enter_context(nc.sbuf_tensor("sb_b_qu", [128, NH], F32))
        b_qr = es.enter_context(nc.sbuf_tensor("sb_b_qr", [RD, NH], F32))
        bqu_s = es.enter_context(nc.sbuf_tensor("bqu_s", [128, NH], F32))
        bqr_s = es.enter_context(nc.sbuf_tensor("bqr_s", [RD, NH], F32))
        qtc = es.enter_context(nc.sbuf_tensor("qtc", [32, OWN], BF16))
        qts = es.enter_context(nc.sbuf_tensor("qts", [32, OWN], BF16))
        qch = es.enter_context(nc.sbuf_tensor("qch", [128, OWN], BF16))
        qrpre = es.enter_context(nc.sbuf_tensor("qrpre", [RD, OWN], BF16))
        t2b = es.enter_context(nc.sbuf_tensor("t2b", [32, OWN], BF16))
        rtmp = es.enter_context(nc.sbuf_tensor("rtmp", [32, OWN], BF16))
        rtm2 = es.enter_context(nc.sbuf_tensor("rtm2", [32, OWN], BF16))
        qrope = es.enter_context(nc.sbuf_tensor("qrope", [RD, OWN], BF16))
        sqb = es.enter_context(nc.sbuf_tensor("sqb", [128, OWN], BF16))
        sqr = es.enter_context(nc.sbuf_tensor("sqr", [RD, OWN], BF16))
        qbt = es.enter_context(nc.sbuf_tensor("qbt", [1, 4], F32))
        ps_a = es.enter_context(nc.psum_tensor("ps1b_a", [128, 512], F32))
        ps_b = es.enter_context(nc.psum_tensor("ps1b_b", [128, 512], F32))
        ps_n = es.enter_context(nc.psum_tensor("ps1b_n", [1, 512], F32))

        for qc in range(NQC):
            Sh.dma("sp", (lambda qc: lambda h: h.dma_start(
                out=qc_sb[:, qc, :], in_=d.z_qc[qc, :, :]))(qc), w=[("qc_sb",)])
        Sh.dma("sp", lambda h: h.dma_start(out=b_qu[:], in_=d.b_qu[:, :]), w=[("b_qu",)])
        Sh.dma("sp", lambda h: h.dma_start(out=b_qr[:], in_=d.b_qr[:, :]), w=[("b_qr",)])
        Sh.dma("pool", lambda h: h.dma_start(out=qtc[:], in_=d.qtbl_c[:, :]), w=[("qtc",)])
        Sh.dma("pool", lambda h: h.dma_start(out=qts[:], in_=d.qtbl_s[:, :]), w=[("qts",)])
        Sh.dve(lambda h: nc.vector.tensor_scalar_mul(bqu_s[:], b_qu[:], SCALE),
               r=[("b_qu",)], w=[("bqu_s",)])
        Sh.dve(lambda h: nc.vector.tensor_scalar_mul(bqr_s[:], b_qr[:], SCALE),
               r=[("b_qr",)], w=[("bqr_s",)])

        for hh in range(NH):
            par = hh % 2
            Sh.dma("pool", (lambda hh, par: lambda h: h.dma_start(
                out=w_qu[par][:], in_=d.w_qu[hh, :, :]))(hh, par), w=[("w_qu", par)])
            Sh.dma("pool", (lambda hh, par: lambda h: h.dma_start(
                out=w_qr[par][:], in_=d.w_qr[hh, :, :]))(hh, par), w=[("w_qr", par)])
            for t5 in range(4):
                bank = t5 % 2
                ps = [ps_a, ps_b][bank]
                for qc in range(NQC):
                    Sh.pe((lambda t5, qc, par, ps: lambda h: nc.tensor.matmul(
                        ps[:], w_qu[par][:, qc * 128:(qc + 1) * 128], qc_sb[:, qc, t5 * 512:(t5 + 1) * 512],
                        start=(qc == 0), stop=(qc == NQC - 1)))(t5, qc, par, ps),
                        r=[("w_qu", par), ("qc_sb",)], w=[("ps1b", bank)])
                Sh.act((lambda t5, hh, bank, ps: lambda h: nc.scalar.activation(
                    out=qch[:, t5 * 512:(t5 + 1) * 512], in_=ps[:],
                    func=AF.Identity, bias=bqu_s[:, hh:hh + 1], scale=SCALE))(t5, hh, bank, ps),
                    r=[("ps1b", bank), ("bqu_s",)], w=[("qch", t5)])
            for t5 in range(4) if "qr" in _P1B_PARTS else []:
                bank = t5 % 2
                ps = [ps_a, ps_b][bank]
                for qc in range(NQC):
                    Sh.pe((lambda t5, qc, par, ps: lambda h: nc.tensor.matmul(
                        ps[0:RD, :], w_qr[par][:, qc * RD:(qc + 1) * RD], qc_sb[:, qc, t5 * 512:(t5 + 1) * 512],
                        start=(qc == 0), stop=(qc == NQC - 1)))(t5, qc, par, ps),
                        r=[("w_qr", par), ("qc_sb",)], w=[("ps1b", bank)])
                Sh.act((lambda t5, hh, bank, ps: lambda h: nc.scalar.activation(
                    out=qrpre[:, t5 * 512:(t5 + 1) * 512], in_=ps[0:RD, :],
                    func=AF.Identity, bias=bqr_s[:, hh:hh + 1], scale=SCALE))(t5, hh, bank, ps),
                    r=[("ps1b", bank), ("bqr_s",)], w=[("qrpre",)])
            if "rope" not in _P1B_PARTS:
                Sh.dve(lambda h: nc.vector.memset(qrope[:], 0.0), w=[("qrope_t1",), ("qrope_t2",)])
            if "rope" in _P1B_PARTS:
                Sh.dma("pool", lambda h: h.dma_start(out=t2b[:], in_=qrpre[32:64, :]),
                       r=[("qrpre",)], w=[("t2b",)])
                Sh.dve(lambda h: nc.vector.tensor_tensor(
                    out=qrope[0:32, :], in0=qrpre[0:32, :], in1=qtc[:], op=ALU.mult),
                    r=[("qrpre",), ("qtc",)], w=[("qrope_t1",)])
                Sh.dve(lambda h: nc.vector.tensor_tensor(
                    out=rtmp[:], in0=t2b[:], in1=qts[:], op=ALU.mult),
                    r=[("t2b",), ("qts",)], w=[("rtmp",)])
                Sh.dve(lambda h: nc.vector.tensor_sub(qrope[0:32, :], qrope[0:32, :], rtmp[:]),
                       r=[("qrope_t1",), ("rtmp",)], w=[("qrope_t1",)])
                Sh.dve(lambda h: nc.vector.tensor_tensor(
                    out=rtm2[:], in0=t2b[:], in1=qtc[:], op=ALU.mult),
                    r=[("t2b",), ("qtc",)], w=[("rtm2",)])
                Sh.dve(lambda h: nc.vector.tensor_tensor(
                    out=rtmp[:], in0=qrpre[0:32, :], in1=qts[:], op=ALU.mult),
                    r=[("qrpre",), ("qts",)], w=[("rtmp",)])
                Sh.dve(lambda h: nc.vector.tensor_add(rtm2[:], rtm2[:], rtmp[:]),
                       r=[("rtm2",), ("rtmp",)], w=[("rtm2",)])
                Sh.dma("pool", lambda h: h.dma_start(out=qrope[32:64, :], in_=rtm2[:]),
                       r=[("rtm2",)], w=[("qrope_t2",)])
            if "bound" not in _P1B_PARTS:
                Sh.dve(lambda h: nc.vector.memset(p.qb[:], 1.0), w=[("qb",)])
            if "bound" in _P1B_PARTS:
                Sh.dve(lambda h: nc.vector.tensor_tensor(out=sqb[:], in0=qch[:], in1=qch[:], op=ALU.mult),
                       r=[("qch", 0), ("qch", 1), ("qch", 2), ("qch", 3)], w=[("sqb",)])
                Sh.dve(lambda h: nc.vector.tensor_tensor(
                    out=sqr[:], in0=qrope[:], in1=qrope[:], op=ALU.mult),
                    r=[("qrope_t1",), ("qrope_t2",)], w=[("sqr",)])
                for t5 in range(4):
                    Sh.pe((lambda t5: lambda h: nc.tensor.matmul(
                        ps_n[:], p.ones_b[:], sqb[:, t5 * 512:(t5 + 1) * 512],
                        start=True, stop=False))(t5),
                        r=[("sqb",), ("ones_b",)], w=[("ps1b_n",)])
                    Sh.pe((lambda t5: lambda h: nc.tensor.matmul(
                        ps_n[:], p.ones_b[0:RD, :], sqr[:, t5 * 512:(t5 + 1) * 512],
                        start=False, stop=True))(t5),
                        r=[("sqr",), ("ones_b",)], w=[("ps1b_n",)])
                    Sh.dve((lambda t5: lambda h: nc.vector.reduce_max(
                        qbt[:, t5:t5 + 1], ps_n[:], axis=AX.X))(t5),
                        r=[("ps1b_n",)], w=[("qbt", t5)])
                Sh.dve((lambda hh: lambda h: nc.vector.reduce_max(
                    p.qb[:, hh:hh + 1], qbt[:], axis=AX.X))(hh),
                    r=[("qbt", t) for t in range(4)], w=[("qb",)])
            Sh.dma("sp", (lambda hh: lambda h: h.dma_start(
                out=d.z_q[hh, :, :], in_=qch[:]))(hh),
                r=[("qch", t) for t in range(4)], w=[])
            Sh.dma("sp", (lambda hh: lambda h: h.dma_start(
                out=d.z_qr[hh, :, :], in_=qrope[:]))(hh),
                r=[("qrope_t1",), ("qrope_t2",)], w=[])
            if hh == NH - 1:
                Sh.dma("sp", lambda h: h.dma_start(out=d.z_qb[:, :], in_=p.qb[:]),
                       r=[("qb",)], w=[])
            if _DBG_QCH[0] and hh == 15:
                dbgf = es.enter_context(nc.sbuf_tensor("dbgf", [128, OWN], F32))
                Sh.dve(lambda h: nc.vector.tensor_copy(dbgf[:], qch[:]),
                       r=[("qch", t) for t in range(4)], w=[("dbgf",)])
                Sh.dma("sp", lambda h: h.dma_start(out=d.dbg_qch[:, :], in_=dbgf[:]),
                       r=[("dbgf",)], w=[])
                dbgp = es.enter_context(nc.sbuf_tensor("dbgp", [128, 512], F32))
                Sh.dve(lambda h: nc.vector.tensor_copy(dbgp[:], ps_b[:]),
                       r=[("ps1b", 1)], w=[("dbgp",)])
                Sh.dma("sp", lambda h: h.dma_start(out=d.dbg_ps[:, :], in_=dbgp[:]),
                       r=[("dbgp",)], w=[])
                Sh.dma("sp", lambda h: h.dma_start(
                    out=d.dbg_qcsb[:, :], in_=qc_sb[:].rearrange("p c n -> p (c n)")),
                    r=[("qc_sb",)], w=[])
        Sh.barrier()


def _phase2(p):
    nc, Sh, d = p.nc, p.S, p.d
    with ExitStack() as es:
        T = lambda n, s, dt: es.enter_context(nc.sbuf_tensor(n, s, dt))
        kvc = T("kvc", [128, NCC, S], BF16)
        ktc = T("ktc", [32, S], BF16)
        kts = T("kts", [32, S], BF16)
        wku = [T(f"wku{i}", [128, NCC * 128], BF16) for i in range(2)]
        wvu = [T(f"wvu{i}", [128, NCC * 128], BF16) for i in range(2)]
        wkr = [T(f"wkr{i}", [128, NCC * RD], BF16) for i in range(2)]
        kc_sb = T("kc_sb", [128, S], BF16)
        v_sb = T("v_sb", [128, S], BF16)
        krope = [T(f"krope{i}", [RD, S], BF16) for i in range(2)]
        CH = 1024
        NCH = S // CH
        krpre = [T(f"krpre{i}", [RD, CH], BF16) for i in range(2)]
        t2b2 = [T(f"t2b2{i}", [32, CH], BF16) for i in range(2)]
        rt1 = T("rt1", [32, CH], BF16)
        rt2 = T("rt2", [32, CH], BF16)
        qc_h = T("qc_h", [128, OWN], BF16)
        qr_h = T("qr_h", [RD, OWN], BF16)
        pt = [T(f"pt{i}", [128, 1024], BF16) for i in range(2)]
        msk = [T(f"msk{i}", [128, 8 * 256], BF16) for i in range(2)]
        l1r = T("l1r", [1, 192], F32)
        sc1 = T("sc1", [1, 8], F32)
        qbv = T("qbv", [1, NH], F32)
        nb = T("nb", [128, 1], F32)
        kvm = T("kvm", [128, 1], F32)
        kvm2 = T("kvm2", [RD, 1], F32)
        bvu_c = T("bvu_c", [128, NH], F32)
        bku_c = T("bku_c", [128, NH], F32)
        bkr_c = T("bkr_c", [RD, NH], F32)
        bvro = T("bvro", [1, 128], F32)
        bvbc = T("bvbc", [128, 128], F32)
        rc = T("rc", [1, 256], F32)
        rb = T("rb", [128, 256], F32)
        ctxev = T("ctxev", [128, 256], BF16)
        st = [es.enter_context(nc.psum_tensor(f"ps2st{i}", [128, 1024], F32)) for i in range(2)]
        ctx = [es.enter_context(nc.psum_tensor(f"ps2c{i}", [128, 512], F32)) for i in range(2)]
        eB = es.enter_context(nc.psum_tensor("ps2e", [128, 256], F32))
        bld = es.enter_context(nc.psum_tensor("ps2b", [RD, 512], F32))

        if _DBG_QCH[0]:
            d.dbg_b1 = nc.dram_tensor("dbgb1", [1, 8], F32, kind="ExternalOutput").ap()
            d.dbg_b2 = nc.dram_tensor("dbgb2", [1, 192], F32, kind="ExternalOutput").ap()
            d.dbg_b3 = nc.dram_tensor("dbgb3", [128, 1], F32, kind="ExternalOutput").ap()
            d.dbg_b4 = nc.dram_tensor("dbgb4", [1, NH], F32, kind="ExternalOutput").ap()

        for cc in range(NCC):
            Sh.dma("sp", (lambda cc: lambda h: h.dma_start(
                out=kvc[:, cc, :], in_=d.z_kvc[cc, :, :]))(cc), w=[("kvc",)])
        Sh.dma("sp", lambda h: h.dma_start(out=ktc[:], in_=d.ktbl_c[:, :]), w=[("ktc",)])
        Sh.dma("sp", lambda h: h.dma_start(out=kts[:], in_=d.ktbl_s[:, :]), w=[("kts",)])
        Sh.dma("sp", lambda h: h.dma_start(out=qbv[:], in_=d.z_qb[:, :]), w=[("qbv",)])
        Sh.dma("sp", lambda h: h.dma_start(out=bvu_c[:], in_=d.b_vu[:, :]), w=[("bvu_c",)])
        Sh.dma("sp", lambda h: h.dma_start(out=bku_c[:], in_=d.b_ku[:, :]), w=[("bku_c",)])
        Sh.dma("sp", lambda h: h.dma_start(out=bkr_c[:], in_=d.b_kr[:, :]), w=[("bkr_c",)])

        def head_loads(hh):
            par = hh % 2
            Sh.dma("pool", (lambda hh, par: lambda h: h.dma_start(
                out=wku[par][:], in_=d.w_ku[hh, :, :]))(hh, par), w=[("wku", par)])
            Sh.dma("pool", (lambda hh, par: lambda h: h.dma_start(
                out=wvu[par][:], in_=d.w_vu[hh, :, :]))(hh, par), w=[("wvu", par)])
            Sh.dma("pool", (lambda hh, par: lambda h: h.dma_start(
                out=wkr[par][:], in_=d.w_kr[hh, :, :]))(hh, par), w=[("wkr", par)])

        kck = [("kc_sb", k5) for k5 in range(16)]
        krkeys = lambda par: ([("krope1", par, c) for c in range(NCH)]
                              + [("krope2", par, c) for c in range(NCH)])

        def kr_chunks(hh):
            par = hh % 2
            kro = krope[par]
            out = []
            for ch in range(NCH):
                def c1(ch=ch, par=par, hh=hh):
                    kp = krpre[ch % 2]
                    for k5 in range(CH // 512):
                        for cc in range(NCC):
                            Sh.pe((lambda ch, k5, cc, par: lambda h: nc.tensor.matmul(
                                bld[:, :], wkr[par][:, cc * RD:(cc + 1) * RD],
                                kvc[:, cc, ch * CH + k5 * 512:ch * CH + (k5 + 1) * 512],
                                start=(cc == 0), stop=(cc == NCC - 1)))(ch, k5, cc, par),
                                r=[("wkr", par), ("kvc",)], w=[("bld",)])
                        Sh.act((lambda ch, k5, hh, kp: lambda h: nc.scalar.activation(
                            out=kp[:, k5 * 512:(k5 + 1) * 512], in_=bld[:, :],
                            func=AF.Identity, bias=bkr_c[:, hh:hh + 1], scale=1.0))(ch, k5, hh, kp),
                            r=[("bld",), ("bkr_c",)], w=[("krpre", ch % 2, k5)])

                def c2(ch=ch, par=par, kro=kro):
                    kp = krpre[ch % 2]
                    tb = t2b2[ch % 2]
                    kx = [("krpre", ch % 2, k5) for k5 in range(CH // 512)]
                    cs = ktc[:, ch * CH:(ch + 1) * CH]
                    sn = kts[:, ch * CH:(ch + 1) * CH]
                    Sh.dma("pool", (lambda kp, tb: lambda h: h.dma_start(
                        out=tb[:], in_=kp[32:64, :]))(kp, tb), r=kx, w=[("t2b2", ch % 2)])
                    Sh.dve((lambda ch, cs, kp, kro, par: lambda h: nc.vector.tensor_tensor(
                        out=kro[0:32, ch * CH:(ch + 1) * CH], in0=kp[0:32, :],
                        in1=cs, op=ALU.mult))(ch, cs, kp, kro, par),
                        r=kx + [("ktc",)], w=[("krope1", par, ch)])
                    Sh.dve((lambda sn, tb: lambda h: nc.vector.tensor_tensor(
                        out=rt1[:], in0=tb[:], in1=sn, op=ALU.mult))(sn, tb),
                        r=[("t2b2", ch % 2), ("kts",)], w=[("rt1",)])
                    Sh.dve((lambda ch, kro, par: lambda h: nc.vector.tensor_sub(
                        kro[0:32, ch * CH:(ch + 1) * CH],
                        kro[0:32, ch * CH:(ch + 1) * CH], rt1[:]))(ch, kro, par),
                        r=[("krope1", par, ch), ("rt1",)], w=[("krope1", par, ch)])
                    Sh.dve((lambda cs, tb: lambda h: nc.vector.tensor_tensor(
                        out=rt2[:], in0=tb[:], in1=cs, op=ALU.mult))(cs, tb),
                        r=[("t2b2", ch % 2), ("ktc",)], w=[("rt2",)])
                    Sh.dve((lambda sn, kp, kx: lambda h: nc.vector.tensor_tensor(
                        out=rt1[:], in0=kp[0:32, :], in1=sn, op=ALU.mult))(sn, kp, kx),
                        r=kx + [("kts",)], w=[("rt1",)])
                    Sh.dve(lambda h: nc.vector.tensor_add(rt2[:], rt2[:], rt1[:]),
                           r=[("rt2",), ("rt1",)], w=[("rt2",)])
                    Sh.dma("pool", (lambda ch, kro: lambda h: h.dma_start(
                        out=kro[32:64, ch * CH:(ch + 1) * CH], in_=rt2[:]))(ch, kro),
                        r=[("rt2",)], w=[("krope2", par, ch)])
                out += [c1, c2]

            def c3(par=par, kro=kro):
                Sh.dve((lambda kro, par: lambda h: nc.vector.tensor_reduce(
                    kvm2[:], kro[:], axis=AX.X, op=ALU.max,
                    apply_absolute_value=True))(kro, par),
                    r=krkeys(par), w=[("kvm2",)])
                Sh.dve(lambda h: nc.vector.tensor_mul(kvm2[:], kvm2[:], kvm2[:]),
                       r=[("kvm2",)], w=[("kvm2",)])
                Sh.pe(lambda h: nc.tensor.transpose(eB[0:1, 128:192], kvm2[:], p.id_f[0:RD, 0:RD]),
                      r=[("kvm2",), ("id_f",)], w=[("ps2e",)])
                Sh.dve(lambda h: nc.vector.tensor_copy(l1r[:, 128:192], eB[0:1, 128:192]),
                       r=[("ps2e",)], w=[("l1r", 1)])
            out.append(c3)
            return out

        def bound_finish(hh):
            Sh.dve(lambda h: nc.vector.tensor_reduce(
                kvm[:], kc_sb[:], axis=AX.X, op=ALU.max, apply_absolute_value=True),
                r=kck, w=[("kvm",)])
            Sh.dve(lambda h: nc.vector.tensor_mul(kvm[:], kvm[:], kvm[:]),
                   r=[("kvm",)], w=[("kvm",)])
            Sh.pe(lambda h: nc.tensor.transpose(eB[0:1, 0:128], kvm[:], p.id_f[:]),
                  r=[("kvm",), ("id_f",)], w=[("ps2e",)])
            Sh.dve(lambda h: nc.vector.tensor_copy(l1r[:, 0:128], eB[0:1, 0:128]),
                   r=[("ps2e",)], w=[("l1r", 0)])
            Sh.dve(lambda h: nc.vector.reduce_sum(sc1[:, 1:2], l1r[:], axis=AX.X),
                   r=[("l1r", 0), ("l1r", 1)], w=[("sc1", 1)])
            Sh.act((lambda hh: lambda h: nc.scalar.activation(
                out=sc1[:, 2:3], in_=qbv[:, hh:hh + 1], func=AF.Ln))(hh),
                r=[("qbv",)], w=[("sc1", 2)])
            Sh.act(lambda h: nc.scalar.activation(
                out=sc1[:, 3:4], in_=sc1[:, 1:2], func=AF.Ln),
                r=[("sc1", 1)], w=[("sc1", 3)])
            Sh.dve(lambda h: nc.vector.tensor_add(sc1[:, 4:5], sc1[:, 2:3], sc1[:, 3:4]),
                   r=[("sc1", 2), ("sc1", 3)], w=[("sc1", 4)])
            Sh.act(lambda h: nc.scalar.activation(
                out=sc1[:, 5:6], in_=sc1[:, 4:5], func=AF.Exp, scale=0.5),
                r=[("sc1", 4)], w=[("sc1", 5)])
            Sh.dve(lambda h: nc.vector.tensor_scalar_mul(sc1[:, 6:7], sc1[:, 5:6], -1.0),
                   r=[("sc1", 5)], w=[("sc1", 6)])
            Sh.pe(lambda h: nc.tensor.matmul(
                eB[:, 0:1], p.ones_f[0:1, :], sc1[0:1, 6:7], start=True, stop=True),
                r=[("sc1", 6), ("ones_f",)], w=[("ps2e",)])
            Sh.dve(lambda h: nc.vector.tensor_copy(nb[:], eB[:, 0:1]),
                   r=[("ps2e",)], w=[("nb",)])
            if _NB_ZERO[0]:
                Sh.dve(lambda h: nc.vector.memset(nb[:], 0.0), w=[("nb",)])

        head_loads(0)
        pend = []

        def emit_slot():
            if pend:
                pend.pop(0)()

        for hh in range(NH):
            par = hh % 2
            kroA = krope[par]
            krkA = krkeys(par)
            if hh + 1 < NH:
                head_loads(hh + 1)
            Sh.dma("sp", (lambda hh: lambda h: h.dma_start(
                out=qc_h[:], in_=d.z_q[hh, :, :]))(hh), w=[("qc_h",)])
            Sh.dma("sp", (lambda hh: lambda h: h.dma_start(
                out=qr_h[:], in_=d.z_qr[hh, :, :]))(hh), w=[("qr_h",)])

            for k5 in range(16):
                bank = k5 % 2
                ba = st[bank][:, 0:512]
                for cc in range(NCC):
                    Sh.pe((lambda k5, cc, par, ba: lambda h: nc.tensor.matmul(
                        ba, wku[par][:, cc * 128:(cc + 1) * 128],
                        kvc[:, cc, k5 * 512:(k5 + 1) * 512],
                        start=(cc == 0), stop=(cc == NCC - 1)))(k5, cc, par, ba),
                        r=[("wku", par), ("kvc",)], w=[("ps2st", bank)])
                Sh.act((lambda k5, hh, bank, ba: lambda h: nc.scalar.activation(
                    out=kc_sb[:, k5 * 512:(k5 + 1) * 512], in_=ba, func=AF.Identity,
                    bias=bku_c[:, hh:hh + 1], scale=1.0))(k5, hh, bank, ba),
                    r=[("ps2st", bank), ("bku_c",)], w=[("kc_sb", k5)])

            if hh == 0:
                for c in kr_chunks(0):
                    c()
            bound_finish(hh)
            if _DBG_QCH[0] and hh == 2:
                Sh.dma("sp", lambda h: h.dma_start(out=d.dbg_b1[:, :], in_=sc1[:]),
                       r=[("sc1", i) for i in range(7)], w=[])
                Sh.dma("sp", lambda h: h.dma_start(out=d.dbg_b2[:, :], in_=l1r[:]),
                       r=[("l1r", 0), ("l1r", 1)], w=[])
                Sh.dma("sp", lambda h: h.dma_start(out=d.dbg_b3[:, :], in_=nb[:]),
                       r=[("nb",)], w=[])
                Sh.dma("sp", lambda h: h.dma_start(out=d.dbg_b4[:, :], in_=qbv[:]),
                       r=[("qbv",)], w=[])

            Sh.pe((lambda hh: lambda h: nc.tensor.transpose(
                eB[0:1, 0:128], bvu_c[:, hh:hh + 1], p.id_f[:]))(hh),
                r=[("bvu_c",), ("id_f",)], w=[("ps2e",)])
            Sh.dve(lambda h: nc.vector.tensor_copy(bvro[:], eB[0:1, 0:128]),
                   r=[("ps2e",)], w=[("bvro",)])
            Sh.pe(lambda h: nc.tensor.matmul(
                eB[:, 0:128], p.ones_f[0:1, :], bvro[:], start=True, stop=True),
                r=[("bvro",), ("ones_f",)], w=[("ps2e",)])
            Sh.dve(lambda h: nc.vector.tensor_copy(bvbc[:], eB[:, 0:128]),
                   r=[("ps2e",)], w=[("bvbc",)])
            for kg in range(16):
                bank = kg % 2
                bt = st[bank][:, 0:512]
                for ki in range(4):
                    kt = kg * 4 + ki
                    for cc in range(NCC):
                        Sh.pe((lambda kt, ki, cc, par, bank: lambda h: nc.tensor.matmul(
                            st[bank][:, ki * 128:(ki + 1) * 128],
                            kvc[:, cc, kt * 128:(kt + 1) * 128],
                            wvu[par][:, cc * 128:(cc + 1) * 128],
                            start=(ki == 0 and cc == 0), stop=(ki == 3 and cc == NCC - 1),
                            skip_group_check=True))(kt, ki, cc, par, bank),
                            r=[("wvu", par), ("kvc",)], w=[("ps2st", bank)])
                Sh.dve((lambda kg, bank: lambda h: nc.vector.tensor_tensor(
                    out=v_sb[:, kg * 512:(kg + 1) * 512].rearrange("p (a b) -> p a b", a=4),
                    in0=st[bank][:, 0:512].rearrange("p (a b) -> p a b", a=4),
                    in1=bvbc[:].rearrange("p (o b) -> p o b", o=1).to_broadcast([128, 4, 128]),
                    op=ALU.add))(kg, bank),
                    r=[("ps2st", bank), ("bvbc",)], w=[("v_sb", kg)])

            pend = kr_chunks(hh + 1) if hh + 1 < NH else []
            vk = [("v_sb", kg) for kg in range(16)]
            pending_epi = []

            def flush_epi():
                while pending_epi:
                    pending_epi.pop(0)()

            def mk_epi(jt, hh):
                def epi():
                    Sh.dve((lambda jt: lambda h: nc.vector.reciprocal(
                        rc[:], ctx[jt % 2][0:1, 256:512]))(jt),
                        r=[("ps2c", jt % 2)], w=[("rc",)])
                    Sh.pe(lambda h: nc.tensor.matmul(
                        eB[:, 0:256], p.ones_f[0:1, :], rc[:], start=True, stop=True),
                        r=[("rc",), ("ones_f",)], w=[("ps2e",)])
                    Sh.dve(lambda h: nc.vector.tensor_copy(rb[:], eB[:, 0:256]),
                           r=[("ps2e",)], w=[("rb",)])
                    Sh.dve((lambda jt: lambda h: nc.vector.tensor_tensor(
                        out=ctxev[:], in0=ctx[jt % 2][:, 0:256], in1=rb[:], op=ALU.mult))(jt),
                        r=[("ps2c", jt % 2), ("rb",)], w=[("ctxev",)])
                    Sh.dma("sp", (lambda hh, jt: lambda h: h.dma_start(
                        out=d.z_ctx[hh, :, jt * 256:(jt + 1) * 256], in_=ctxev[:]))(hh, jt),
                        r=[("ctxev",)], w=[])
                return epi

            for pr in range(NTILE // 2):
                jtE, jtO = 2 * pr, 2 * pr + 1
                TE, TO = tile_T(jtE), tile_T(jtO)
                q0 = jtE * 256
                for jt in (jtE, jtO):
                    Sh.dma("sp", (lambda hh, jt: lambda h: h.dma_start(
                        out=msk[jt % 2][:], in_=d.masks[jt, :, :]))(hh, jt),
                        w=[("msk", jt % 2)])

                def s_joint(b):
                    sb = b % 2
                    for i in range(2):
                        kt = 2 * b + i
                        Sh.pe((lambda kt, i, sb, q0: lambda h: nc.tensor.matmul(
                            st[sb][:, i * 512:(i + 1) * 512],
                            kc_sb[:, kt * 128:(kt + 1) * 128], qc_h[:, q0:q0 + 512],
                            start=True, stop=False))(kt, i, sb, q0),
                            r=kck + [("qc_h",)], w=[("ps2st", sb)])
                        Sh.pe((lambda kt, i, sb, q0, kroA: lambda h: nc.tensor.matmul(
                            st[sb][:, i * 512:(i + 1) * 512],
                            kroA[:, kt * 128:(kt + 1) * 128], qr_h[:, q0:q0 + 512],
                            start=False, stop=True))(kt, i, sb, q0, kroA),
                            r=krkA + [("qr_h",)], w=[("ps2st", sb)])

                def exp_b(b):
                    sb = b % 2
                    Sh.act((lambda sb: lambda h: nc.scalar.activation(
                        out=pt[sb][:], in_=st[sb][:], func=AF.Exp,
                        bias=nb[:, 0:1], scale=1.0))(sb),
                        r=[("ps2st", sb), ("nb",)], w=[("pt", sb)])

                def pv_joint(b):
                    sb = b % 2
                    for i in range(2):
                        kt = 2 * b + i
                        if kt >= TO - 8:
                            mi = kt - (TO - 8)
                            Sh.dve((lambda sb, i, mi: lambda h: nc.vector.tensor_tensor(
                                out=pt[sb][:, i * 512 + 256:(i + 1) * 512],
                                in0=pt[sb][:, i * 512 + 256:(i + 1) * 512],
                                in1=msk[1][:, mi * 256:(mi + 1) * 256],
                                op=ALU.mult))(sb, i, mi),
                                r=[("pt", sb), ("msk", 1)], w=[("pt", sb)])
                    for i in range(2):
                        kt = 2 * b + i
                        for (jt, off) in ((jtE, 0), (jtO, 256)):
                            Sh.pe((lambda kt, i, sb, jt, off: lambda h: nc.tensor.matmul(
                                ctx[jt % 2][:, 0:256], v_sb[:, kt * 128:(kt + 1) * 128],
                                pt[sb][:, i * 512 + off:i * 512 + off + 256],
                                start=(kt == 0), stop=False,
                                skip_group_check=True))(kt, i, sb, jt, off),
                                r=vk + [("pt", sb)], w=[("ps2c", jt % 2)])
                            Sh.pe((lambda kt, i, sb, jt, off: lambda h: nc.tensor.matmul(
                                ctx[jt % 2][0:1, 256:512], p.ones_b[:],
                                pt[sb][:, i * 512 + off:i * 512 + off + 256],
                                start=False, stop=(jt == jtO and kt == TO - 1),
                                skip_group_check=True))(kt, i, sb, jt, off),
                                r=[("pt", sb), ("ones_b",)], w=[("ps2c", jt % 2)])

                def s_solo(b2):
                    sb = b2 % 2
                    for i in range(4):
                        kt = TO + 4 * b2 + i
                        Sh.pe((lambda kt, i, sb, q0: lambda h: nc.tensor.matmul(
                            st[sb][:, i * 256:(i + 1) * 256],
                            kc_sb[:, kt * 128:(kt + 1) * 128], qc_h[:, q0:q0 + 256],
                            start=True, stop=False))(kt, i, sb, q0),
                            r=kck + [("qc_h",)], w=[("ps2st", sb)])
                        Sh.pe((lambda kt, i, sb, q0, kroA: lambda h: nc.tensor.matmul(
                            st[sb][:, i * 256:(i + 1) * 256],
                            kroA[:, kt * 128:(kt + 1) * 128], qr_h[:, q0:q0 + 256],
                            start=False, stop=True))(kt, i, sb, q0, kroA),
                            r=krkA + [("qr_h",)], w=[("ps2st", sb)])

                def pv_solo(b2):
                    sb = b2 % 2
                    for i in range(4):
                        kt = TO + 4 * b2 + i
                        mi = kt - (TE - 8)
                        Sh.dve((lambda sb, i, mi: lambda h: nc.vector.tensor_tensor(
                            out=pt[sb][:, i * 256:(i + 1) * 256],
                            in0=pt[sb][:, i * 256:(i + 1) * 256],
                            in1=msk[0][:, mi * 256:(mi + 1) * 256],
                            op=ALU.mult))(sb, i, mi),
                            r=[("pt", sb), ("msk", 0)], w=[("pt", sb)])
                    for i in range(4):
                        kt = TO + 4 * b2 + i
                        Sh.pe((lambda kt, i, sb: lambda h: nc.tensor.matmul(
                            ctx[0][:, 0:256], v_sb[:, kt * 128:(kt + 1) * 128],
                            pt[sb][:, i * 256:(i + 1) * 256],
                            start=False, stop=False,
                            skip_group_check=True))(kt, i, sb),
                            r=vk + [("pt", sb)], w=[("ps2c", 0)])
                        Sh.pe((lambda kt, i, sb: lambda h: nc.tensor.matmul(
                            ctx[0][0:1, 256:512], p.ones_b[:],
                            pt[sb][:, i * 256:(i + 1) * 256],
                            start=False, stop=(kt == TE - 1),
                            skip_group_check=True))(kt, i, sb),
                            r=[("pt", sb), ("ones_b",)], w=[("ps2c", 0)])

                nbj = TO // 2
                for b in range(nbj):
                    emit_slot()
                    s_joint(b)
                    if b == 1:
                        flush_epi()
                    if b > 0:
                        pv_joint(b - 1)
                    exp_b(b)
                pv_joint(nbj - 1)
                pending_epi.append(mk_epi(jtO, hh))
                for b2 in range(2):
                    emit_slot()
                    s_solo(b2)
                    if b2 > 0:
                        pv_solo(b2 - 1)
                    exp_b(b2)
                pv_solo(1)
                pending_epi.append(mk_epi(jtE, hh))
            flush_epi()
            while pend:
                pend.pop(0)()
        Sh.barrier()


def _phase2b(p):
    nc, Sh, d = p.nc, p.S, p.d
    with ExitStack() as es:
        T = lambda n, s, dt: es.enter_context(nc.sbuf_tensor(n, s, dt))
        w_o = T("sb_w_o", [128, NDC * D], BF16)
        ctxt = [T(f"ctxt{i}", [128, NH * 128], BF16) for i in range(2)]
        bo_r = T("bo_r", [1, D], F32)
        bo_bc = T("bo_bc", [128, D], F32)
        aout = T("aout", [128, 512], F32)
        ps = [es.enter_context(nc.psum_tensor(f"ps2b{i}", [128, 512], F32)) for i in range(2)]
        psb = es.enter_context(nc.psum_tensor("ps2bb", [128, 512], F32))

        Sh.dma("pool", lambda h: h.dma_start(out=w_o[:], in_=d.w_o[:, :]), w=[("w_o",)])
        Sh.dma("sp", lambda h: h.dma_start(out=bo_r[:], in_=d.b_o[:, :]), w=[("bo_r",)])
        for q in range(4):
            Sh.pe((lambda q: lambda h: nc.tensor.matmul(
                psb[:], p.ones_f[0:1, :], bo_r[0:1, q * 512:(q + 1) * 512],
                start=True, stop=True))(q), r=[("bo_r",), ("ones_f",)], w=[("ps2bb",)])
            Sh.dve((lambda q: lambda h: nc.vector.tensor_copy(
                bo_bc[:, q * 512:(q + 1) * 512], psb[:]))(q),
                r=[("ps2bb",)], w=[("bo_bc",)])
        for t1 in range(16):
            par = t1 % 2
            for hc in range(NH):
                Sh.dma("sp", (lambda t1, hc, par: lambda h: h.dma_start(
                    out=ctxt[par][:, hc * 128:(hc + 1) * 128],
                    in_=d.z_ctx[hc, :, t1 * 128:(t1 + 1) * 128]))(t1, hc, par),
                    w=[("ctxt", par, hc)])
            for h5 in range(4):
                bank = h5 % 2
                for hc in range(NH):
                    Sh.pe((lambda t1, h5, hc, par, bank: lambda h: nc.tensor.matmul(
                        ps[bank][:], ctxt[par][:, hc * 128:(hc + 1) * 128],
                        w_o[:, hc * D + h5 * 512:hc * D + (h5 + 1) * 512],
                        start=(hc == 0), stop=(hc == NH - 1)))(t1, h5, hc, par, bank),
                        r=[("ctxt", par, hc), ("w_o",)], w=[("ps2b", bank)])
                Sh.dve((lambda t1, h5, bank: lambda h: nc.vector.tensor_tensor(
                    out=aout[:], in0=ps[bank][:], in1=bo_bc[:, h5 * 512:(h5 + 1) * 512],
                    op=ALU.add))(t1, h5, bank),
                    r=[("ps2b", bank), ("bo_bc",)], w=[("aout",)])
                Sh.dma("act", (lambda t1, h5: lambda h: h.dma_start(
                    out=d.z_attn[t1 * 128:(t1 + 1) * 128, h5 * 512:(h5 + 1) * 512],
                    in_=aout[:]))(t1, h5), r=[("aout",)], w=[])
        Sh.barrier()


def _phase3(p):
    nc, Sh, d = p.nc, p.S, p.d
    with ExitStack() as es:
        T = lambda n, s, dt: es.enter_context(nc.sbuf_tensor(n, s, dt))
        xn = T("xn3", [128, NDC, OWN], BF16)
        gw = T("gw", [128, NDC * NE], BF16)
        logit = T("logit", [128, 16 * NE], F32)
        nmx = T("nmx", [128, 16], F32)
        se = T("se", [128, 16], F32)
        rse = T("rse", [128, 16], F32)
        esb = T("esb", [128, 16 * NE], F32)
        scrs = T("scrs", [128, 16 * NE], F32)
        t8 = T("t8", [128, 8], F32)
        thr = T("thr", [128, 16], F32)
        msk3 = T("msk3", [128, NE], F32)
        wesb = T("wesb", [128, 16 * NE], F32)
        invf = T("invf", [1, NE], F32)
        invi = T("invi_sb", [1, NE], I32)
        invb = T("invb", [NE, NE], F32)
        iota = T("iota3", [NE, 1], F32)
        pone = T("pone", [NE, NE], F32)
        weT = T("weT", [NE, 128], F32)
        flat = T("flat", [128, 16 * NE], F32)
        sg = T("sg", [128, 16 * GS], F32)
        qq = T("qq", [128, 16 * NE], F32)
        mkd = T("mkd", [128, 16 * NE], F32)
        fmx = T("fmx", [128, 16 * GS], F32)
        fse = T("fse", [128, 16 * GS], F32)
        soft_bf = T("soft_bf", [128, 16 * NE], BF16)
        softT = T("softT", [GS, OWN], BF16)
        sgrow = T("sgrow", [1, OWN], F32)
        sbb = T("sbb", [128, OWN], F32)
        mu_sb = T("mu_sb", [128, NDC * GS], BF16)
        wgb = [T(f"wgb{i}", [128, NDC * 128], BF16) for i in range(2)]
        wub = [T(f"wub{i}", [128, NDC * 128], BF16) for i in range(2)]
        mgb = T("mgb", [128, GS], BF16)
        m_g = T("m_g", [GS, 128], BF16)
        m_u = T("m_u", [GS, 128], BF16)
        sg_bf = T("sg_bf", [128, 512], BF16)
        a1t = T("a1t", [128, 512], BF16)
        a_t = T("a_t", [128, 512], BF16)
        dcast = [T(f"dcast{i}", [128, D], BF16) for i in range(2)]
        ps_g = [es.enter_context(nc.psum_tensor(f"ps3g{i}", [128, 512], F32)) for i in range(2)]
        ps_u = [es.enter_context(nc.psum_tensor(f"ps3u{i}", [128, 512], F32)) for i in range(2)]
        ps_m = es.enter_context(nc.psum_tensor("ps3m", [128, 512], F32))
        ps_x = es.enter_context(nc.psum_tensor("ps3x", [128, 512], F32))
        ps_t = es.enter_context(nc.psum_tensor("ps3t", [128, 512], BF16))

        for dc in range(NDC):
            Sh.dma("sp", (lambda dc: lambda h: h.dma_start(
                out=xn[:, dc, :], in_=d.z_xn[dc, :, :]))(dc), w=[("xn", dc)])
        Sh.dma("pool", lambda h: h.dma_start(out=gw[:], in_=d.gate_wT[:, :]), w=[("gw",)])
        Sh.dma("sp", lambda h: h.dma_start(out=invi[:], in_=d.inv[:, :]), w=[("invi",)])
        Sh.dve(lambda h: nc.vector.tensor_copy(invf[:], invi[:]), r=[("invi",)], w=[("invf",)])
        Sh.pool(lambda h: nc.gpsimd.iota(iota[:], pattern=[[0, 1]], base=0,
                                         channel_multiplier=1,
                                         allow_small_or_imprecise_dtypes=True), w=[("iota",)])
        Sh.pe(lambda h: nc.tensor.matmul(ps_m[0:NE, 0:NE], p.ones_f[0:1, 0:NE], invf[:],
                                         start=True, stop=True),
              r=[("invf",), ("ones_f",)], w=[("ps3m",)])
        Sh.dve(lambda h: nc.vector.tensor_copy(invb[:], ps_m[0:NE, 0:NE]),
               r=[("ps3m",)], w=[("invb",)])
        Sh.dve(lambda h: nc.vector.tensor_single_scalar(
            out=pone[:], in_=invb[:], scalar=iota[:, 0:1], op=ALU.is_equal),
            r=[("invb",), ("iota",)], w=[("pone",)])
        for t1 in range(16):
            bank = t1 % 2
            pj = [ps_g, ps_u][bank][0]
            pk = ("ps3gb", 0) if bank == 0 else ("ps3ub", 0)
            for dc in range(NDC):
                Sh.pe((lambda t1, dc, pj: lambda h: nc.tensor.matmul(
                    pj[:, 0:NE], xn[:, dc, t1 * 128:(t1 + 1) * 128],
                    gw[:, dc * NE:(dc + 1) * NE],
                    start=(dc == 0), stop=(dc == NDC - 1)))(t1, dc, pj),
                    r=[("xn", dc), ("gw",)], w=[pk])
            Sh.dve((lambda t1, pj: lambda h: nc.vector.tensor_copy(
                logit[:, t1 * NE:(t1 + 1) * NE], pj[:, 0:NE]))(t1, pj),
                r=[pk], w=[("logit", t1)])
        for t1 in range(16):
            Sh.dve((lambda t1: lambda h: nc.vector.reduce_max(
                nmx[:, t1:t1 + 1], logit[:, t1 * NE:(t1 + 1) * NE], axis=AX.X,
                negate=True))(t1), r=[("logit", t1)], w=[("nmx", t1)])
            Sh.act((lambda t1: lambda h: nc.scalar.activation(
                out=esb[:, t1 * NE:(t1 + 1) * NE], in_=logit[:, t1 * NE:(t1 + 1) * NE],
                func=AF.Exp, bias=nmx[:, t1:t1 + 1],
                accum_out=se[:, t1:t1 + 1]))(t1),
                r=[("logit", t1), ("nmx", t1)], w=[("esb", t1), ("se", t1)])
        Sh.dve(lambda h: nc.vector.reciprocal(rse[:], se[:]),
               r=[("se", t1) for t1 in range(16)], w=[("rse",)])
        Sh.dve(lambda h: nc.vector.tensor_tensor(
            out=scrs[:].rearrange("p (a b) -> p a b", a=16),
            in0=esb[:].rearrange("p (a b) -> p a b", a=16),
            in1=rse[:].rearrange("p (a o) -> p a o", o=1).to_broadcast([128, 16, NE]),
            op=ALU.mult),
            r=[("esb", t1) for t1 in range(16)] + [("rse",)], w=[("scrs",)])
        for t1 in range(16):
            Sh.dve((lambda t1: lambda h: nc.vector.max(
                out=t8[:, 0:8], in_=scrs[:, t1 * NE:(t1 + 1) * NE]))(t1),
                r=[("scrs",)], w=[("t8",)])
            Sh.dve((lambda t1: lambda h: nc.vector.tensor_copy(
                thr[:, t1:t1 + 1], t8[:, 7:8]))(t1), r=[("t8",)], w=[("thr", t1)])
            Sh.dve((lambda t1: lambda h: nc.vector.tensor_single_scalar(
                out=msk3[:], in_=scrs[:, t1 * NE:(t1 + 1) * NE],
                scalar=thr[:, t1:t1 + 1], op=ALU.is_ge))(t1),
                r=[("scrs",), ("thr", t1)], w=[("msk3",)])
            Sh.dve((lambda t1: lambda h: nc.vector.tensor_mul(
                wesb[:, t1 * NE:(t1 + 1) * NE], scrs[:, t1 * NE:(t1 + 1) * NE],
                msk3[:]))(t1), r=[("scrs",), ("msk3",)], w=[("wesb", t1)])
            Sh.pe((lambda t1: lambda h: nc.tensor.transpose(
                ps_m[0:NE, 0:128], wesb[:, t1 * NE:(t1 + 1) * NE], p.id_f[:]))(t1),
                r=[("wesb", t1), ("id_f",)], w=[("ps3m",)])
            Sh.dve((lambda t1: lambda h: nc.vector.tensor_copy(
                weT[:], ps_m[0:NE, 0:128]))(t1), r=[("ps3m",)], w=[("weT",)])
            Sh.pe((lambda t1: lambda h: nc.tensor.matmul(
                ps_x[:, 0:NE], weT[:], pone[:], start=True, stop=True))(t1),
                r=[("weT",), ("pone",)], w=[("ps3x",)])
            Sh.dve((lambda t1: lambda h: nc.vector.tensor_copy(
                flat[:, t1 * NE:(t1 + 1) * NE], ps_x[:, 0:NE]))(t1),
                r=[("ps3x",)], w=[("flat", t1)])
        fk = [("flat", t1) for t1 in range(16)]
        Sh.dve(lambda h: nc.vector.reduce_sum(
            sg[:].rearrange("p (a g o) -> p a g o", a=16, o=1),
            flat[:].rearrange("p (a g i) -> p a g i", a=16, g=GS), axis=AX.X),
            r=fk, w=[("sg",)])
        Sh.dve(lambda h: nc.vector.tensor_single_scalar(
            out=qq[:], in_=flat[:], scalar=0.0, op=ALU.is_equal), r=fk, w=[("qq",)])
        Sh.dve(lambda h: nc.vector.scalar_tensor_tensor(
            out=mkd[:], in0=qq[:], scalar=-1e9, in1=flat[:], op0=ALU.mult, op1=ALU.add),
            r=[("qq",)] + fk, w=[("mkd",)])
        Sh.dve(lambda h: nc.vector.tensor_reduce(
            fmx[:].rearrange("p (a g o) -> p a g o", a=16, o=1),
            mkd[:].rearrange("p (a g i) -> p a g i", a=16, g=GS), axis=AX.X,
            op=ALU.max), r=[("mkd",)], w=[("fmx",)])
        Sh.dve(lambda h: nc.vector.tensor_tensor(
            out=mkd[:].rearrange("p (a g i) -> p a g i", a=16, g=GS),
            in0=mkd[:].rearrange("p (a g i) -> p a g i", a=16, g=GS),
            in1=fmx[:].rearrange("p (a g o) -> p a g o", a=16, o=1)
            .to_broadcast([128, 16, GS, GS]), op=ALU.subtract),
            r=[("mkd",), ("fmx",)], w=[("mkd",)])
        Sh.act(lambda h: nc.scalar.activation(out=esb[:], in_=mkd[:], func=AF.Exp),
               r=[("mkd",)], w=[("esb", "x")])
        Sh.dve(lambda h: nc.vector.reduce_sum(
            fse[:].rearrange("p (a g o) -> p a g o", a=16, o=1),
            esb[:].rearrange("p (a g i) -> p a g i", a=16, g=GS), axis=AX.X),
            r=[("esb", "x")], w=[("fse",)])
        Sh.dve(lambda h: nc.vector.reciprocal(fse[:], fse[:]), r=[("fse",)], w=[("fse",)])
        Sh.dve(lambda h: nc.vector.tensor_tensor(
            out=soft_bf[:].rearrange("p (a g i) -> p a g i", a=16, g=GS),
            in0=esb[:].rearrange("p (a g i) -> p a g i", a=16, g=GS),
            in1=fse[:].rearrange("p (a g o) -> p a g o", a=16, o=1)
            .to_broadcast([128, 16, GS, GS]), op=ALU.mult),
            r=[("esb", "x"), ("fse",)], w=[("soft_bf",)])

        for g in range(NG):
            for t1 in range(16):
                Sh.pe((lambda g, t1: lambda h: nc.tensor.transpose(
                    ps_t[0:GS, 0:128],
                    soft_bf[:, t1 * NE + g * GS:t1 * NE + (g + 1) * GS], p.id_b[:]))(g, t1),
                    r=[("soft_bf",), ("id_b",)], w=[("ps3t",)])
                Sh.dve((lambda g, t1: lambda h: nc.vector.tensor_copy(
                    softT[:, t1 * 128:(t1 + 1) * 128], ps_t[0:GS, 0:128]))(g, t1),
                    r=[("ps3t",)], w=[("softT",)])
                Sh.pe((lambda g, t1: lambda h: nc.tensor.transpose(
                    ps_x[0:1, 0:128], sg[:, t1 * GS + g:t1 * GS + g + 1], p.id_f[:]))(g, t1),
                    r=[("sg",), ("id_f",)], w=[("ps3x",)])
                Sh.dve((lambda g, t1: lambda h: nc.vector.tensor_copy(
                    sgrow[:, t1 * 128:(t1 + 1) * 128], ps_x[0:1, 0:128]))(g, t1),
                    r=[("ps3x",)], w=[("sgrow",)])
            for t5 in range(4):
                Sh.pe((lambda g, t5: lambda h: nc.tensor.matmul(
                    ps_x[:], p.ones_f[0:1, :], sgrow[0:1, t5 * 512:(t5 + 1) * 512],
                    start=True, stop=True))(g, t5),
                    r=[("sgrow",), ("ones_f",)], w=[("ps3x",)])
                Sh.dve((lambda g, t5: lambda h: nc.vector.tensor_copy(
                    sbb[:, t5 * 512:(t5 + 1) * 512], ps_x[:]))(g, t5),
                    r=[("ps3x",)], w=[("sbb",)])
            Sh.dma("pool", (lambda g: lambda h: h.dma_start(
                out=mu_sb[:], in_=d.maskup[g, :, :]))(g), w=[("mu_sb",)])
            for ic in range(NIC):
                par = ic % 2
                Sh.dma("pool", (lambda g, ic, par: lambda h: h.dma_start(
                    out=wgb[par][:], in_=d.wgT[g, ic, :, :]))(g, ic, par), w=[("wgb", par)])
                Sh.dma("pool", (lambda g, ic, par: lambda h: h.dma_start(
                    out=wub[par][:], in_=d.wuT[g, ic, :, :]))(g, ic, par), w=[("wub", par)])
                Sh.dma("pool", (lambda g, ic, par: lambda h: h.dma_start(
                    out=dcast[par][:], in_=d.downT[g, ic, :, :]))(g, ic, par),
                    w=[("dcast", par)])
                Sh.dma("act", (lambda g, ic, par: lambda h: h.dma_start(
                    out=d.z_dwn[g, ic, :, :], in_=dcast[par][:]))(g, ic, par),
                    r=[("dcast", par)], w=[])
                for (wb, mdst, mkey) in ((wgb, m_g, "m_g"), (wub, m_u, "m_u")):
                    for dc in range(NDC):
                        Sh.pe((lambda wb, par, dc: lambda h: nc.tensor.matmul(
                            ps_m[:, 0:GS], wb[par][:, dc * 128:(dc + 1) * 128],
                            mu_sb[:, dc * GS:(dc + 1) * GS],
                            start=(dc == 0), stop=(dc == NDC - 1)))(wb, par, dc),
                            r=[(("wgb" if wb is wgb else "wub"), par), ("mu_sb",)],
                            w=[("ps3m",)])
                    Sh.dve(lambda h: nc.vector.tensor_copy(mgb[:], ps_m[:, 0:GS]),
                           r=[("ps3m",)], w=[("mgb",)])
                    Sh.pe(lambda h: nc.tensor.transpose(
                        ps_t[0:GS, 0:128], mgb[:], p.id_b[:]),
                        r=[("mgb",), ("id_b",)], w=[("ps3t",)])
                    Sh.dve((lambda mdst, mkey: lambda h: nc.vector.tensor_copy(
                        mdst[:], ps_t[0:GS, 0:128]))(mdst, mkey),
                        r=[("ps3t",)], w=[(mkey,)])
                for t5 in range(4):
                    bg = t5 % 2
                    for dc in range(NDC):
                        Sh.pe((lambda par, dc, t5, bg: lambda h: nc.tensor.matmul(
                            ps_g[bg][:], wgb[par][:, dc * 128:(dc + 1) * 128],
                            xn[:, dc, t5 * 512:(t5 + 1) * 512],
                            start=(dc == 0), stop=False))(par, dc, t5, bg),
                            r=[("wgb", par), ("xn", dc)], w=[("ps3gb", bg)])
                    Sh.pe((lambda t5, bg: lambda h: nc.tensor.matmul(
                        ps_g[bg][:], m_g[:], softT[:, t5 * 512:(t5 + 1) * 512],
                        start=False, stop=True))(t5, bg),
                        r=[("m_g",), ("softT",)], w=[("ps3gb", bg)])
                    for dc in range(NDC):
                        Sh.pe((lambda par, dc, t5, bg: lambda h: nc.tensor.matmul(
                            ps_u[bg][:], wub[par][:, dc * 128:(dc + 1) * 128],
                            xn[:, dc, t5 * 512:(t5 + 1) * 512],
                            start=(dc == 0), stop=False))(par, dc, t5, bg),
                            r=[("wub", par), ("xn", dc)], w=[("ps3ub", bg)])
                    Sh.pe((lambda t5, bg: lambda h: nc.tensor.matmul(
                        ps_u[bg][:], m_u[:], softT[:, t5 * 512:(t5 + 1) * 512],
                        start=False, stop=True))(t5, bg),
                        r=[("m_u",), ("softT",)], w=[("ps3ub", bg)])
                    Sh.act((lambda t5, bg: lambda h: nc.scalar.activation(
                        out=sg_bf[:], in_=ps_g[bg][:], func=AF.Silu))(t5, bg),
                        r=[("ps3gb", bg)], w=[("sg_bf",)])
                    Sh.dve((lambda t5, bg: lambda h: nc.vector.tensor_tensor(
                        out=a1t[:], in0=ps_u[bg][:], in1=sg_bf[:], op=ALU.mult))(t5, bg),
                        r=[("ps3ub", bg), ("sg_bf",)], w=[("a1t",)])
                    Sh.dve((lambda t5: lambda h: nc.vector.tensor_tensor(
                        out=a_t[:], in0=a1t[:], in1=sbb[:, t5 * 512:(t5 + 1) * 512],
                        op=ALU.mult))(t5), r=[("a1t",), ("sbb",)], w=[("a_t",)])
                    Sh.dma("act", (lambda g, ic, t5: lambda h: h.dma_start(
                        out=d.z_a[g, ic, :, t5 * 512:(t5 + 1) * 512], in_=a_t[:]))(g, ic, t5),
                        r=[("a_t",)], w=[])
        Sh.barrier()

    with ExitStack() as es:
        T = lambda n, s, dt: es.enter_context(nc.sbuf_tensor(n, s, dt))
        dwn = [T(f"dwn{i}", [128, 512], BF16) for i in range(6)]
        abuf = [T(f"abuf{i}", [128, 1024], BF16) for i in range(6)]
        x_t = [T(f"x_t{i}", [128, 512], F32) for i in range(2)]
        at_t = [T(f"at_t{i}", [128, 512], F32) for i in range(2)]
        tmp = T("tmp3", [128, 512], F32)
        osb = T("osb", [128, 512], F32)
        psd = [es.enter_context(nc.psum_tensor(f"ps3d{i}", [128, 512], F32)) for i in range(8)]

        for th in range(2):
            for h5 in range(4):
                for g in range(NG):
                    for ic in range(NIC):
                        par = (g * NIC + ic) % 6
                        Sh.dma("act", (lambda g, ic, h5, par: lambda h: h.dma_start(
                            out=dwn[par][:], in_=d.z_dwn[g, ic, :, h5 * 512:(h5 + 1) * 512]))(g, ic, h5, par),
                            w=[("dwn", par)])
                        Sh.dma("pool", (lambda g, ic, th, par: lambda h: h.dma_start(
                            out=abuf[par][:], in_=d.z_a[g, ic, :, th * 1024:(th + 1) * 1024]))(g, ic, th, par),
                            w=[("abuf", par)])
                        first = (g == 0 and ic == 0)
                        last = (g == NG - 1 and ic == NIC - 1)
                        for t1 in range(8):
                            Sh.pe((lambda g, ic, t1, par, first, last: lambda h: nc.tensor.matmul(
                                psd[t1][:], abuf[par][:, t1 * 128:(t1 + 1) * 128],
                                dwn[par][:], start=first, stop=last))(g, ic, t1, par, first, last),
                                r=[("abuf", par), ("dwn", par)], w=[("ps3d", t1)])
                for t1 in range(8):
                    par = t1 % 2
                    tg = th * 8 + t1
                    Sh.dma("sp", (lambda tg, h5, par: lambda h: h.dma_start(
                        out=x_t[par][:], in_=d.xo[tg * 128:(tg + 1) * 128, h5 * 512:(h5 + 1) * 512]))(tg, h5, par),
                        w=[("x_t", par)])
                    Sh.dma("sp", (lambda tg, h5, par: lambda h: h.dma_start(
                        out=at_t[par][:], in_=d.z_attn[tg * 128:(tg + 1) * 128, h5 * 512:(h5 + 1) * 512]))(tg, h5, par),
                        w=[("at_t", par)])
                    Sh.dve((lambda t1, par: lambda h: nc.vector.tensor_tensor(
                        out=tmp[:], in0=psd[t1][:], in1=at_t[par][:], op=ALU.add))(t1, par),
                        r=[("ps3d", t1), ("at_t", par)], w=[("tmp3",)])
                    Sh.dve((lambda par: lambda h: nc.vector.tensor_tensor(
                        out=osb[:], in0=tmp[:], in1=x_t[par][:], op=ALU.add))(par),
                        r=[("tmp3",), ("x_t", par)], w=[("osb",)])
                    Sh.dma("sp", (lambda tg, h5: lambda h: h.dma_start(
                        out=d.out[tg * 128:(tg + 1) * 128, h5 * 512:(h5 + 1) * 512],
                        in_=osb[:]))(tg, h5), r=[("osb",)], w=[])
        Sh.barrier()


def _finish(p, upto):
    nc, Sh, d = p.nc, p.S, p.d
    if upto >= 5:
        return
    with ExitStack() as es:
        t = es.enter_context(nc.sbuf_tensor("fin_t", [128, D], F32))
        for j in range(NCHUNK):
            Sh.dma("sp", (lambda j: lambda h: h.dma_start(
                out=t[:], in_=d.xo[j * 128:(j + 1) * 128, :]))(j), r=(), w=[("fin_t",)])
            Sh.dma("sp", (lambda j: lambda h: h.dma_start(
                out=d.out[j * 128:(j + 1) * 128, :], in_=t[:]))(j), r=[("fin_t",)], w=())
        Sh.barrier()


def _rope_tables(positions):
    inv_freq = 1.0 / (10000.0 ** (np.arange(0, RD, 2, dtype=np.float64) / RD))
    ang = positions[None, :].astype(np.float64) * inv_freq[:, None]
    return (np.cos(ang).astype(np.float32).astype(bfdt),
            np.sin(ang).astype(np.float32).astype(bfdt))


def _host_masks(rows):
    m = np.zeros((NTILE, 8, 128, 256), np.float32)
    lim = rows.reshape(NTILE, 256) + 1
    for j in range(NTILE):
        T = tile_T(j)
        for i in range(8):
            kt = T - 8 + i
            kpos = 128 * kt + np.arange(128)
            m[j, i] = (kpos[:, None] < lim[j][None, :]).astype(np.float32)
    return np.ascontiguousarray(m.transpose(0, 2, 1, 3).reshape(NTILE, 128, 8 * 256)).astype(bfdt)


def _host_shared(g):
    f = lambda a: np.ascontiguousarray(np.asarray(a, np.float32))
    C = np.ascontiguousarray

    def pmaj_in(w, nch):
        n = w.shape[1]
        return C(w.reshape(nch, 128, n).transpose(1, 0, 2).reshape(128, nch * n))

    def pmaj_heads(w, nch, hd):
        n = w.shape[1] // hd
        a = w.reshape(nch, 128, n, hd).transpose(2, 1, 0, 3)
        return C(a.reshape(n, 128, nch * hd))

    shared = {
        "w_kvd": pmaj_in(f(g["L0_attn_W_kvd"]), NDC),
        "b_kvd": C(f(g["L0_attn_b_kvd"]).reshape(NCC, 128).T),
        "w_qd": pmaj_in(f(g["L0_attn_W_qd"]), NDC),
        "b_qd": C(f(g["L0_attn_b_qd"]).reshape(NQC, 128).T),
        "w_ku": pmaj_heads(f(g["L0_attn_W_ku"]), NCC, 128),
        "b_ku": C(f(g["L0_attn_b_ku"]).reshape(NH, 128).T),
        "w_vu": pmaj_heads(f(g["L0_attn_W_vu"]), NCC, 128),
        "b_vu": C(f(g["L0_attn_b_vu"]).reshape(NH, 128).T),
        "w_kr": pmaj_heads(f(g["L0_attn_W_kr"]), NCC, RD),
        "b_kr": C(f(g["L0_attn_b_kr"]).reshape(NH, RD).T),
        "w_qu": pmaj_heads(f(g["L0_attn_W_qu"]), NQC, 128),
        "b_qu": C(f(g["L0_attn_b_qu"]).reshape(NH, 128).T),
        "w_qr": pmaj_heads(f(g["L0_attn_W_qr"]), NQC, RD),
        "b_qr": C(f(g["L0_attn_b_qr"]).reshape(NH, RD).T),
        "w_o": pmaj_in(f(g["L0_attn_W_o"]), NDC),
        "b_o": f(g["L0_attn_b_o"]).reshape(1, D),
        "b_ku_r": f(g["L0_attn_b_ku"]).reshape(1, D),
        "b_kr_r": f(g["L0_attn_b_kr"]).reshape(1, NH * RD),
        "gate_wT": pmaj_in(C(f(g["L0_ffn_gate_w"]).T), NDC),
        "maskup": np.stack([pmaj_in(f(g["L0_ffn_mask_up_w"][i]), NDC) for i in range(NG)]),
        "wgT": np.stack([np.stack([pmaj_in(C(f(g["L0_ffn_gate_proj_w"][i]).T)[:, j * 128:(j + 1) * 128], NDC)
                                   for j in range(NIC)]) for i in range(NG)]),
        "wuT": np.stack([np.stack([pmaj_in(C(f(g["L0_ffn_up_proj_w"][i]).T)[:, j * 128:(j + 1) * 128], NDC)
                                   for j in range(NIC)]) for i in range(NG)]),
        "downT": np.stack([C(f(g["L0_ffn_down_proj_w"][i]).T).reshape(NIC, 128, D) for i in range(NG)]),
        "inv": np.asarray(g["L0_ffn_inv_map"], np.int64).astype(np.int32).reshape(1, NE),
        "ident": np.eye(128, dtype=bfdt),
        "identf": np.eye(128, dtype=np.float32),
    }
    kc, ks = _rope_tables(np.arange(S))
    shared["ktbl_c"], shared["ktbl_s"] = kc, ks
    return shared


def _host_core(x, c):
    b, qi = c // 4, c % 4
    chunks = own_chunks(qi)
    rows = np.concatenate([np.arange(128 * gch, 128 * gch + 128) for gch in chunks])
    qc_, qs_ = _rope_tables(rows)
    im = {
        "xb": x[b],
        "xo": np.ascontiguousarray(x[b][rows]),
        "qtbl_c": qc_, "qtbl_s": qs_,
        "masks": _host_masks(rows),
    }
    return im, b, rows


def kernel(**inputs):
    g = {k: np.asarray(v) for k, v in inputs.items()}
    x = np.ascontiguousarray(g["L0_attn_x"].astype(np.float32, copy=False))
    shared = _host_shared(g)

    nc = _build()

    in_maps = []
    owns = []
    for c in range(8):
        im, b, rows = _host_core(x, c)
        owns.append((b, rows))
        im.update(shared)
        in_maps.append(im)

    res = run_bass_kernel_spmd(nc, in_maps, core_ids=list(range(8)))

    out = np.empty((B, S, D), np.float32)
    for c in range(8):
        b, rows = owns[c]
        out[b][rows] = np.asarray(res.results[c]["out"], np.float32)
    return out


# version 12 (final)
# speedup vs baseline: 1.0608x; 1.0487x over previous
import numpy as np
import ml_dtypes
from contextlib import ExitStack

import concourse.bass as bass
import concourse.mybir as mybir
from concourse.bass_utils import run_bass_kernel_spmd

F32 = mybir.dt.float32
BF16 = mybir.dt.bfloat16
I32 = mybir.dt.int32
AF = mybir.ActivationFunctionType
ALU = mybir.AluOpType
AX = mybir.AxisListType

B, S, D = 2, 8192, 2048
NH, HD, RD = 16, 128, 64
KVC, QC = 512, 1536
NG, GS, NE, GI = 8, 8, 64, 1408
OWN = 2048
NCHUNK = 16
NDC = D // 128
NCC = KVC // 128
NQC = QC // 128
NIC = GI // 128
SCALE = 1.0 / float(np.sqrt(HD + RD))
EPS = 1e-5
NTILE = 8
bfdt = ml_dtypes.bfloat16


def own_chunks(qi):
    return [63 - qi - 4 * j for j in range(NCHUNK)]


def tile_T(j):
    return 64 - 8 * j


class _Op:
    __slots__ = ("idx", "eng", "emit", "deps", "dma", "sem", "inc", "cnt", "waits", "wlist")


NQSUB = 6


class Sched:
    def __init__(self, nc, es):
        self.nc = nc
        self.ops = []
        self.w = {}
        self.r = {}
        self.sems = {}
        for name in ("e:pe", "e:act", "e:dve", "e:pool"):
            self.sems[name] = es.enter_context(nc.semaphore(name.replace(":", "_")))
        for q in ("sp", "pool", "act"):
            for i in range(NQSUB):
                self.sems[("q:" + q, i)] = es.enter_context(nc.semaphore(f"q_{q}_{i}"))
        self.cnt = {k: 0 for k in self.sems}
        self.qn = {"sp": 0, "pool": 0, "act": 0}

    def op(self, eng, emit, reads=(), writes=(), dma=False):
        idx = len(self.ops)
        deps = set()
        for k in reads:
            j = self.w.get(k)
            if j is not None:
                deps.add(j)
        for k in writes:
            j = self.w.get(k)
            if j is not None:
                deps.add(j)
            deps.update(self.r.get(k, ()))
        o = _Op()
        o.idx, o.eng, o.emit, o.deps, o.dma = idx, eng, emit, deps, dma
        if dma:
            n = self.qn[eng]
            self.qn[eng] = n + 1
            o.sem = ("q:" + eng, n % NQSUB)
            o.inc = 16
        else:
            o.sem = "e:" + eng
            o.inc = 1
        self.cnt[o.sem] += o.inc
        o.cnt = self.cnt[o.sem]
        self.ops.append(o)
        for k in reads:
            self.r.setdefault(k, []).append(idx)
        for k in writes:
            self.w[k] = idx
            self.r[k] = []
        return idx

    def pe(self, emit, r=(), w=()):
        return self.op("pe", emit, r, w)

    def dve(self, emit, r=(), w=()):
        return self.op("dve", emit, r, w)

    def act(self, emit, r=(), w=()):
        return self.op("act", emit, r, w)

    def pool(self, emit, r=(), w=()):
        return self.op("pool", emit, r, w)

    def dma(self, eng, emit, r=(), w=()):
        return self.op(eng, emit, r, w, dma=True)

    def barrier(self):
        snapshot = dict(self.cnt)
        for eng in ("pe", "act", "dve", "pool", "sp"):
            o = _Op()
            o.idx, o.eng, o.emit, o.dma = len(self.ops), eng, None, False
            o.deps = set()
            o.sem, o.inc, o.cnt = None, 0, None
            o.waits = snapshot
            self.ops.append(o)

    def finalize(self):
        engvc = {e: {} for e in ("pe", "act", "dve", "pool", "sp")}
        opvc = [None] * len(self.ops)
        for o in self.ops:
            if o.emit is None and o.sem is None:
                vc = engvc[o.eng]
                o.wlist = []
                for semname, val in o.waits.items():
                    if val > vc.get(semname, 0):
                        o.wlist.append((semname, val))
                        vc[semname] = val
                opvc[o.idx] = dict(vc)
                continue
            vc = engvc[o.eng]
            wl = []
            if o.dma and o.cnt > 16 and vc.get(o.sem, 0) < o.cnt - 16:
                wl.append((o.sem, o.cnt - 16))
                vc[o.sem] = o.cnt - 16
            for j in sorted(o.deps):
                d = self.ops[j]
                if d.sem is None:
                    continue
                if d.eng == "pe" and o.eng == "pe" and not d.dma and not o.dma:
                    continue
                if vc.get(d.sem, 0) >= d.cnt:
                    continue
                wl.append((d.sem, d.cnt))
                vc[d.sem] = d.cnt
                dvc = opvc[j]
                if dvc:
                    for s2, v2 in dvc.items():
                        if v2 > vc.get(s2, 0):
                            vc[s2] = v2
            o.wlist = wl
            myvc = dict(vc)
            myvc[o.sem] = max(myvc.get(o.sem, 0), o.cnt)
            opvc[o.idx] = myvc

        nc = self.nc
        with nc.Block() as block:
            def run_engine(name, handle):
                for o in self.ops:
                    if o.eng != name:
                        continue
                    for semname, val in o.wlist:
                        handle.wait_ge(self.sems[semname], val)
                    if o.emit is not None:
                        o.emit(handle).then_inc(self.sems[o.sem], o.inc)

            @block.sync
            def _(h):
                run_engine("sp", h)

            @block.gpsimd
            def _(h):
                run_engine("pool", h)

            @block.vector
            def _(h):
                run_engine("dve", h)

            @block.scalar
            def _(h):
                run_engine("act", h)

            @block.tensor
            def _(h):
                run_engine("pe", h)


class Ctx:
    pass


def _declare_io(nc, stage=()):
    d = Ctx()
    def din(name, shape, dt):
        t = nc.dram_tensor(name, list(shape), dt, kind="ExternalInput").ap()
        setattr(d, name, t)
        return t

    din("xb", [S, D], F32)
    din("xo", [OWN, D], F32)
    din("w_kvd", [128, NDC * KVC], F32); din("b_kvd", [128, NCC], F32)
    din("w_qd", [128, NDC * QC], F32); din("b_qd", [128, NQC], F32)
    din("w_ku", [NH, 128, NCC * 128], F32); din("b_ku", [128, NH], F32)
    din("w_vu", [NH, 128, NCC * 128], F32); din("b_vu", [128, NH], F32)
    din("w_kr", [NH, 128, NCC * RD], F32); din("b_kr", [RD, NH], F32)
    din("w_qu", [NH, 128, NQC * 128], F32); din("b_qu", [128, NH], F32)
    din("w_qr", [NH, 128, NQC * RD], F32); din("b_qr", [RD, NH], F32)
    din("w_o", [128, NDC * D], F32); din("b_o", [1, D], F32)
    din("b_ku_r", [1, D], F32); din("b_kr_r", [1, NH * RD], F32)
    din("gate_wT", [128, NDC * NE], F32)
    din("maskup", [NG, 128, NDC * GS], F32)
    din("wgT", [NG, NIC, 128, NDC * 128], F32)
    din("wuT", [NG, NIC, 128, NDC * 128], F32)
    din("downT", [NG, NIC, 128, D], F32)
    din("inv", [1, NE], I32)
    din("ktbl_c", [32, S], BF16); din("ktbl_s", [32, S], BF16)
    din("qtbl_c", [32, OWN], BF16); din("qtbl_s", [32, OWN], BF16)
    din("masks", [NTILE, 128, 8 * 256], BF16)
    din("ident", [128, 128], BF16)
    din("identf", [128, 128], F32)

    d.out = nc.dram_tensor("out", [OWN, D], F32, kind="ExternalOutput").ap()
    def scr(name, shape, dt):
        kind = "ExternalInput" if name in stage else "Internal"
        t = nc.dram_tensor(name, list(shape), dt, kind=kind).ap()
        setattr(d, name, t)
        return t
    scr("z_qb", [1, NH], F32)
    scr("z_kvc", [NCC, 128, S], BF16)
    scr("z_qc", [NQC, 128, OWN], BF16)
    scr("z_xn", [NDC, 128, OWN], BF16)
    scr("z_q", [NH, 128, OWN], BF16)
    scr("z_qr", [NH, RD, OWN], BF16)
    scr("z_ctx", [NH, 128, OWN], BF16)
    scr("z_attn", [OWN, D], F32)
    scr("z_a", [NG, NIC, 128, OWN], BF16)
    scr("z_dwn", [NG, NIC, 128, D], BF16)
    return d


_SCHED_REF = []


def _build(upto=99, debug=(), upfrom=0, stage=()):
    nc = bass.Bass("TRN2", target_bir_lowering=False, detect_race_conditions=False,
                   dynamic_dma_scratch_size=16384)
    d = _declare_io(nc, stage)
    dbg_outs = {}
    for name in debug:
        zt = getattr(d, name)
        dbg_outs[name] = nc.dram_tensor("dbg_" + name, list(zt.shape), zt.dtype,
                                        kind="ExternalOutput").ap()
    root = ExitStack()
    with root:
        Sh = Sched(nc, root)
        _SCHED_REF.clear(); _SCHED_REF.append(Sh)
        ones_b = root.enter_context(nc.sbuf_tensor("ones_b", [128, 1], BF16))
        ones_f = root.enter_context(nc.sbuf_tensor("ones_f", [1, 128], F32))
        id_b = root.enter_context(nc.sbuf_tensor("id_b", [128, 128], BF16))
        id_f = root.enter_context(nc.sbuf_tensor("id_f", [128, 128], F32))
        qb = root.enter_context(nc.sbuf_tensor("qb", [1, NH], F32))
        p = Ctx()
        p.nc, p.S, p.d = nc, Sh, d
        p.ones_b, p.ones_f, p.id_b, p.id_f, p.qb = ones_b, ones_f, id_b, id_f, qb

        Sh.dve(lambda h: nc.vector.memset(ones_b[:], 1.0), w=[("ones_b",)])
        Sh.dve(lambda h: nc.vector.memset(ones_f[:], 1.0), w=[("ones_f",)])
        Sh.dma("sp", lambda h: h.dma_start(out=id_b[:], in_=d.ident[:, :]), w=[("id_b",)])
        Sh.dma("sp", lambda h: h.dma_start(out=id_f[:], in_=d.identf[:, :]), w=[("id_f",)])

        if upfrom <= 1:
            _phase1a(p)
        if upto >= 2 and upfrom <= 2:
            _phase1b(p)
        if upto >= 3 and upfrom <= 3:
            _phase2(p)
        if upto >= 4 and upfrom <= 4:
            _phase2b(p)
        if upto >= 5:
            _phase3(p)
        _finish(p, upto)
        for name, oap in dbg_outs.items():
            zt = getattr(d, name)
            sl = tuple(slice(None) for _ in zt.shape)
            Sh.dma("sp", (lambda zt, oap, sl: lambda h: h.dma_start(out=oap[sl], in_=zt[sl]))(zt, oap, sl))
        Sh.barrier()
        Sh.finalize()
    return nc


def _ln_block(p, es, src_ap, blk, x_tm, xn_tm, stats, kx):
    nc, Sh = p.nc, p.S
    ssq, sums, mean, var, sd, rstd, sqs, msq = stats
    Sh.dma("sp", lambda h: h.dma_start(
        out=x_tm[:], in_=src_ap[blk * 512:(blk + 1) * 512, :].rearrange("(t p) d -> p t d", p=128)),
        w=[(kx, "xtm")])
    for t in range(4):
        Sh.act((lambda t: lambda h: nc.scalar.activation(
            out=sqs[:, t * D:(t + 1) * D], in_=x_tm[:, t, :], func=AF.Square,
            accum_out=ssq[:, t:t + 1]))(t),
            r=[(kx, "xtm")], w=[(kx, "ssq", t), (kx, "sqs", t)])
        Sh.dve((lambda t: lambda h: nc.vector.reduce_sum(
            sums[:, t:t + 1], x_tm[:, t, :], axis=AX.X))(t),
            r=[(kx, "xtm")], w=[(kx, "sum", t)])
    Sh.dve(lambda h: nc.vector.tensor_scalar_mul(mean[:], sums[:], 1.0 / D),
           r=[(kx, "sum", t) for t in range(4)], w=[(kx, "mean")])
    Sh.dve(lambda h: nc.vector.tensor_scalar_mul(var[:], ssq[:], 1.0 / D),
           r=[(kx, "ssq", t) for t in range(4)], w=[(kx, "var")])
    Sh.dve(lambda h: nc.vector.tensor_tensor(out=msq[:], in0=mean[:], in1=mean[:], op=ALU.mult),
           r=[(kx, "mean")], w=[(kx, "msq")])
    Sh.dve(lambda h: nc.vector.tensor_sub(var[:], var[:], msq[:]),
           r=[(kx, "var"), (kx, "msq")], w=[(kx, "var")])
    Sh.dve(lambda h: nc.vector.tensor_scalar_add(var[:], var[:], EPS),
           r=[(kx, "var")], w=[(kx, "var")])
    Sh.act(lambda h: nc.scalar.activation(out=sd[:], in_=var[:], func=AF.Sqrt),
           r=[(kx, "var")], w=[(kx, "sd")])
    Sh.dve(lambda h: nc.vector.reciprocal(rstd[:], sd[:]),
           r=[(kx, "sd")], w=[(kx, "rstd")])
    for t in range(4):
        Sh.dve((lambda t: lambda h: nc.vector.scalar_tensor_tensor(
            out=xn_tm[:, t, :], in0=x_tm[:, t, :], scalar=mean[:, t:t + 1],
            in1=rstd[:, t:t + 1].to_broadcast([128, D]),
            op0=ALU.subtract, op1=ALU.mult))(t),
            r=[(kx, "xtm"), (kx, "mean"), (kx, "rstd")], w=[(kx, "xn", t)])


def _transpose_block(p, es, xn_tm, xn_fm, ps_tp, kx, kfm):
    nc, Sh = p.nc, p.S
    for dc in range(NDC):
        bank = dc % 2
        for t in range(4):
            Sh.pe((lambda dc, t, bank: lambda h: nc.tensor.transpose(
                ps_tp[bank][:, t * 128:(t + 1) * 128], xn_tm[:, t, dc * 128:(dc + 1) * 128],
                p.id_b[:]))(dc, t, bank),
                r=[(kx, "xn", t), ("id_b",)], w=[("ps_tp", bank, t)])
        Sh.dve((lambda dc, bank: lambda h: nc.vector.tensor_copy(
            xn_fm[:, dc * 512:(dc + 1) * 512], ps_tp[bank][:]))(dc, bank),
            r=[("ps_tp", bank, t) for t in range(4)], w=[kfm + (dc,)])


def _phase1a(p):
    nc, Sh, d = p.nc, p.S, p.d
    with ExitStack() as es:
        x_tm2 = [es.enter_context(nc.sbuf_tensor(f"x_tm{i}", [128, 4, D], F32)) for i in range(2)]
        xn_tm2 = [es.enter_context(nc.sbuf_tensor(f"xn_tm{i}", [128, 4, D], BF16)) for i in range(2)]
        xn_fm = [es.enter_context(nc.sbuf_tensor(f"xn_fm{i}", [128, NDC * 512], BF16)) for i in range(2)]
        sqs = es.enter_context(nc.sbuf_tensor("sqs", [128, 4 * D], BF16))
        stats2 = []
        for i in range(2):
            cols = [es.enter_context(nc.sbuf_tensor(f"lnst{i}", [128, 4 * 7], F32))]
            c = cols[0]
            stats2.append((c[:, 0:4], c[:, 4:8], c[:, 8:12], c[:, 12:16], c[:, 16:20], c[:, 20:24], sqs, c[:, 24:28]))
        w_big = es.enter_context(nc.sbuf_tensor("w_big", [128, NDC * QC], BF16))
        b_kvd = es.enter_context(nc.sbuf_tensor("sb_b_kvd", [128, NCC], F32))
        pco = es.enter_context(nc.sbuf_tensor("pco", [128, NQC * 512], BF16))
        kvco = pco[:, 0:NCC * 512]
        ps_tp = [es.enter_context(nc.psum_tensor(f"ps_tp{i}", [128, 512], BF16)) for i in range(2)]
        ps_kv = [es.enter_context(nc.psum_tensor(f"ps_kv{i}", [128, 512], F32)) for i in range(2)]

        Sh.dma("pool", lambda h: h.dma_start(
            out=w_big[:, 0:NDC * KVC], in_=d.w_kvd[:, :]), w=[("w_kvd",)])
        Sh.dma("sp", lambda h: h.dma_start(
            out=b_kvd[:], in_=d.b_kvd[:, :]), w=[("b_kvd",)])

        for blk in range(S // 512):
            kx = ("b", blk % 2)
            _ln_block(p, es, d.xb, blk, x_tm2[blk % 2], xn_tm2[blk % 2], stats2[blk % 2], kx)
            _transpose_block(p, es, xn_tm2[blk % 2], xn_fm[blk % 2], ps_tp, kx, ("fm", blk % 2))
            for cc in range(NCC):
                bank = cc % 2
                for dc in range(NDC):
                    Sh.pe((lambda cc, dc, bank, blk: lambda h: nc.tensor.matmul(
                        ps_kv[bank][:], w_big[:, dc * KVC + cc * 128:dc * KVC + (cc + 1) * 128],
                        xn_fm[blk % 2][:, dc * 512:(dc + 1) * 512],
                        start=(dc == 0), stop=(dc == NDC - 1)))(cc, dc, bank, blk),
                        r=[("w_kvd",), ("fm", blk % 2, dc)], w=[("ps_kv", bank)])
                Sh.act((lambda cc, bank: lambda h: nc.scalar.activation(
                    out=pco[:, cc * 512:(cc + 1) * 512], in_=ps_kv[bank][:],
                    func=AF.Identity, bias=b_kvd[:, cc:cc + 1], scale=1.0))(cc, bank),
                    r=[("ps_kv", bank), ("b_kvd",)], w=[("kvco", cc)])
                Sh.dma("act", (lambda cc, blk: lambda h: h.dma_start(
                    out=d.z_kvc[cc, :, blk * 512:(blk + 1) * 512],
                    in_=pco[:, cc * 512:(cc + 1) * 512]))(cc, blk),
                    r=[("kvco", cc)], w=[])
        Sh.barrier()

        b_qd = es.enter_context(nc.sbuf_tensor("sb_b_qd", [128, NQC], F32))
        qco = pco
        ps_q = ps_kv

        Sh.dma("pool", lambda h: h.dma_start(
            out=w_big[:, :], in_=d.w_qd[:, :]), w=[("w_qd",)])
        Sh.dma("sp", lambda h: h.dma_start(
            out=b_qd[:], in_=d.b_qd[:, :]), w=[("b_qd",)])

        for blk in range(OWN // 512):
            kx = ("o", blk % 2)
            _ln_block(p, es, d.xo, blk, x_tm2[blk % 2], xn_tm2[blk % 2], stats2[blk % 2], kx)
            _transpose_block(p, es, xn_tm2[blk % 2], xn_fm[blk % 2], ps_tp, kx, ("fm2", blk % 2))
            for dc in range(NDC):
                Sh.dma("sp", (lambda dc, blk: lambda h: h.dma_start(
                    out=d.z_xn[dc, :, blk * 512:(blk + 1) * 512],
                    in_=xn_fm[blk % 2][:, dc * 512:(dc + 1) * 512]))(dc, blk),
                    r=[("fm2", blk % 2, dc)], w=[])
            for qc in range(NQC):
                bank = qc % 2
                for dc in range(NDC):
                    Sh.pe((lambda qc, dc, bank, blk: lambda h: nc.tensor.matmul(
                        ps_q[bank][:], w_big[:, dc * QC + qc * 128:dc * QC + (qc + 1) * 128],
                        xn_fm[blk % 2][:, dc * 512:(dc + 1) * 512],
                        start=(dc == 0), stop=(dc == NDC - 1)))(qc, dc, bank, blk),
                        r=[("w_qd",), ("fm2", blk % 2, dc)], w=[("ps_q", bank)])
                Sh.act((lambda qc, bank: lambda h: nc.scalar.activation(
                    out=qco[:, qc * 512:(qc + 1) * 512], in_=ps_q[bank][:],
                    func=AF.Identity, bias=b_qd[:, qc:qc + 1], scale=1.0))(qc, bank),
                    r=[("ps_q", bank), ("b_qd",)], w=[("qco", qc)])
                Sh.dma("act", (lambda qc, blk: lambda h: h.dma_start(
                    out=d.z_qc[qc, :, blk * 512:(blk + 1) * 512],
                    in_=qco[:, qc * 512:(qc + 1) * 512]))(qc, blk),
                    r=[("qco", qc)], w=[])
        Sh.barrier()


_DBG_QCH = [False]
_P1B_PARTS = {"qr", "rope", "bound"}
_NB_ZERO = [False]


def _phase1b(p):
    nc, Sh, d = p.nc, p.S, p.d
    if _DBG_QCH[0]:
        d.dbg_qch = nc.dram_tensor("dbgq", [128, OWN], F32, kind="ExternalOutput").ap()
        d.dbg_ps = nc.dram_tensor("dbgps", [128, 512], F32, kind="ExternalOutput").ap()
        d.dbg_qcsb = nc.dram_tensor("dbgqcsb", [128, NQC * OWN], BF16, kind="ExternalOutput").ap()
    with ExitStack() as es:
        qc_sb = es.enter_context(nc.sbuf_tensor("qc_sb", [128, NQC, OWN], BF16))
        w_qu = [es.enter_context(nc.sbuf_tensor(f"sb_w_qu{i}", [128, NQC * 128], BF16)) for i in range(2)]
        w_qr = [es.enter_context(nc.sbuf_tensor(f"sb_w_qr{i}", [128, NQC * RD], BF16)) for i in range(2)]
        b_qu = es.enter_context(nc.sbuf_tensor("sb_b_qu", [128, NH], F32))
        b_qr = es.enter_context(nc.sbuf_tensor("sb_b_qr", [RD, NH], F32))
        bqu_s = es.enter_context(nc.sbuf_tensor("bqu_s", [128, NH], F32))
        bqr_s = es.enter_context(nc.sbuf_tensor("bqr_s", [RD, NH], F32))
        qtc = es.enter_context(nc.sbuf_tensor("qtc", [32, OWN], BF16))
        qts = es.enter_context(nc.sbuf_tensor("qts", [32, OWN], BF16))
        qch = es.enter_context(nc.sbuf_tensor("qch", [128, OWN], BF16))
        qrpre = es.enter_context(nc.sbuf_tensor("qrpre", [RD, OWN], BF16))
        t2b = es.enter_context(nc.sbuf_tensor("t2b", [32, OWN], BF16))
        rtmp = es.enter_context(nc.sbuf_tensor("rtmp", [32, OWN], BF16))
        rtm2 = es.enter_context(nc.sbuf_tensor("rtm2", [32, OWN], BF16))
        qrope = es.enter_context(nc.sbuf_tensor("qrope", [RD, OWN], BF16))
        sqb = es.enter_context(nc.sbuf_tensor("sqb", [128, OWN], BF16))
        sqr = es.enter_context(nc.sbuf_tensor("sqr", [RD, OWN], BF16))
        qbt = es.enter_context(nc.sbuf_tensor("qbt", [1, 4], F32))
        ps_a = es.enter_context(nc.psum_tensor("ps1b_a", [128, 512], F32))
        ps_b = es.enter_context(nc.psum_tensor("ps1b_b", [128, 512], F32))
        ps_n = es.enter_context(nc.psum_tensor("ps1b_n", [1, 512], F32))

        for qc in range(NQC):
            Sh.dma("sp", (lambda qc: lambda h: h.dma_start(
                out=qc_sb[:, qc, :], in_=d.z_qc[qc, :, :]))(qc), w=[("qc_sb",)])
        Sh.dma("sp", lambda h: h.dma_start(out=b_qu[:], in_=d.b_qu[:, :]), w=[("b_qu",)])
        Sh.dma("sp", lambda h: h.dma_start(out=b_qr[:], in_=d.b_qr[:, :]), w=[("b_qr",)])
        Sh.dma("pool", lambda h: h.dma_start(out=qtc[:], in_=d.qtbl_c[:, :]), w=[("qtc",)])
        Sh.dma("pool", lambda h: h.dma_start(out=qts[:], in_=d.qtbl_s[:, :]), w=[("qts",)])
        Sh.dve(lambda h: nc.vector.tensor_scalar_mul(bqu_s[:], b_qu[:], SCALE),
               r=[("b_qu",)], w=[("bqu_s",)])
        Sh.dve(lambda h: nc.vector.tensor_scalar_mul(bqr_s[:], b_qr[:], SCALE),
               r=[("b_qr",)], w=[("bqr_s",)])

        for hh in range(NH):
            par = hh % 2
            Sh.dma("pool", (lambda hh, par: lambda h: h.dma_start(
                out=w_qu[par][:], in_=d.w_qu[hh, :, :]))(hh, par), w=[("w_qu", par)])
            Sh.dma("pool", (lambda hh, par: lambda h: h.dma_start(
                out=w_qr[par][:], in_=d.w_qr[hh, :, :]))(hh, par), w=[("w_qr", par)])
            for t5 in range(4):
                bank = t5 % 2
                ps = [ps_a, ps_b][bank]
                for qc in range(NQC):
                    Sh.pe((lambda t5, qc, par, ps: lambda h: nc.tensor.matmul(
                        ps[:], w_qu[par][:, qc * 128:(qc + 1) * 128], qc_sb[:, qc, t5 * 512:(t5 + 1) * 512],
                        start=(qc == 0), stop=(qc == NQC - 1)))(t5, qc, par, ps),
                        r=[("w_qu", par), ("qc_sb",)], w=[("ps1b", bank)])
                Sh.act((lambda t5, hh, bank, ps: lambda h: nc.scalar.activation(
                    out=qch[:, t5 * 512:(t5 + 1) * 512], in_=ps[:],
                    func=AF.Identity, bias=bqu_s[:, hh:hh + 1], scale=SCALE))(t5, hh, bank, ps),
                    r=[("ps1b", bank), ("bqu_s",)], w=[("qch", t5)])
            for t5 in range(4) if "qr" in _P1B_PARTS else []:
                bank = t5 % 2
                ps = [ps_a, ps_b][bank]
                for qc in range(NQC):
                    Sh.pe((lambda t5, qc, par, ps: lambda h: nc.tensor.matmul(
                        ps[0:RD, :], w_qr[par][:, qc * RD:(qc + 1) * RD], qc_sb[:, qc, t5 * 512:(t5 + 1) * 512],
                        start=(qc == 0), stop=(qc == NQC - 1)))(t5, qc, par, ps),
                        r=[("w_qr", par), ("qc_sb",)], w=[("ps1b", bank)])
                Sh.act((lambda t5, hh, bank, ps: lambda h: nc.scalar.activation(
                    out=qrpre[:, t5 * 512:(t5 + 1) * 512], in_=ps[0:RD, :],
                    func=AF.Identity, bias=bqr_s[:, hh:hh + 1], scale=SCALE))(t5, hh, bank, ps),
                    r=[("ps1b", bank), ("bqr_s",)], w=[("qrpre",)])
            if "rope" not in _P1B_PARTS:
                Sh.dve(lambda h: nc.vector.memset(qrope[:], 0.0), w=[("qrope_t1",), ("qrope_t2",)])
            if "rope" in _P1B_PARTS:
                Sh.dma("pool", lambda h: h.dma_start(out=t2b[:], in_=qrpre[32:64, :]),
                       r=[("qrpre",)], w=[("t2b",)])
                Sh.dve(lambda h: nc.vector.tensor_tensor(
                    out=qrope[0:32, :], in0=qrpre[0:32, :], in1=qtc[:], op=ALU.mult),
                    r=[("qrpre",), ("qtc",)], w=[("qrope_t1",)])
                Sh.dve(lambda h: nc.vector.tensor_tensor(
                    out=rtmp[:], in0=t2b[:], in1=qts[:], op=ALU.mult),
                    r=[("t2b",), ("qts",)], w=[("rtmp",)])
                Sh.dve(lambda h: nc.vector.tensor_sub(qrope[0:32, :], qrope[0:32, :], rtmp[:]),
                       r=[("qrope_t1",), ("rtmp",)], w=[("qrope_t1",)])
                Sh.dve(lambda h: nc.vector.tensor_tensor(
                    out=rtm2[:], in0=t2b[:], in1=qtc[:], op=ALU.mult),
                    r=[("t2b",), ("qtc",)], w=[("rtm2",)])
                Sh.dve(lambda h: nc.vector.tensor_tensor(
                    out=rtmp[:], in0=qrpre[0:32, :], in1=qts[:], op=ALU.mult),
                    r=[("qrpre",), ("qts",)], w=[("rtmp",)])
                Sh.dve(lambda h: nc.vector.tensor_add(rtm2[:], rtm2[:], rtmp[:]),
                       r=[("rtm2",), ("rtmp",)], w=[("rtm2",)])
                Sh.dma("pool", lambda h: h.dma_start(out=qrope[32:64, :], in_=rtm2[:]),
                       r=[("rtm2",)], w=[("qrope_t2",)])
            if "bound" not in _P1B_PARTS:
                Sh.dve(lambda h: nc.vector.memset(p.qb[:], 1.0), w=[("qb",)])
            if "bound" in _P1B_PARTS:
                Sh.dve(lambda h: nc.vector.tensor_tensor(out=sqb[:], in0=qch[:], in1=qch[:], op=ALU.mult),
                       r=[("qch", 0), ("qch", 1), ("qch", 2), ("qch", 3)], w=[("sqb",)])
                Sh.dve(lambda h: nc.vector.tensor_tensor(
                    out=sqr[:], in0=qrope[:], in1=qrope[:], op=ALU.mult),
                    r=[("qrope_t1",), ("qrope_t2",)], w=[("sqr",)])
                for t5 in range(4):
                    Sh.pe((lambda t5: lambda h: nc.tensor.matmul(
                        ps_n[:], p.ones_b[:], sqb[:, t5 * 512:(t5 + 1) * 512],
                        start=True, stop=False))(t5),
                        r=[("sqb",), ("ones_b",)], w=[("ps1b_n",)])
                    Sh.pe((lambda t5: lambda h: nc.tensor.matmul(
                        ps_n[:], p.ones_b[0:RD, :], sqr[:, t5 * 512:(t5 + 1) * 512],
                        start=False, stop=True))(t5),
                        r=[("sqr",), ("ones_b",)], w=[("ps1b_n",)])
                    Sh.dve((lambda t5: lambda h: nc.vector.reduce_max(
                        qbt[:, t5:t5 + 1], ps_n[:], axis=AX.X))(t5),
                        r=[("ps1b_n",)], w=[("qbt", t5)])
                Sh.dve((lambda hh: lambda h: nc.vector.reduce_max(
                    p.qb[:, hh:hh + 1], qbt[:], axis=AX.X))(hh),
                    r=[("qbt", t) for t in range(4)], w=[("qb",)])
            Sh.dma("sp", (lambda hh: lambda h: h.dma_start(
                out=d.z_q[hh, :, :], in_=qch[:]))(hh),
                r=[("qch", t) for t in range(4)], w=[])
            Sh.dma("sp", (lambda hh: lambda h: h.dma_start(
                out=d.z_qr[hh, :, :], in_=qrope[:]))(hh),
                r=[("qrope_t1",), ("qrope_t2",)], w=[])
            if hh == NH - 1:
                Sh.dma("sp", lambda h: h.dma_start(out=d.z_qb[:, :], in_=p.qb[:]),
                       r=[("qb",)], w=[])
            if _DBG_QCH[0] and hh == 15:
                dbgf = es.enter_context(nc.sbuf_tensor("dbgf", [128, OWN], F32))
                Sh.dve(lambda h: nc.vector.tensor_copy(dbgf[:], qch[:]),
                       r=[("qch", t) for t in range(4)], w=[("dbgf",)])
                Sh.dma("sp", lambda h: h.dma_start(out=d.dbg_qch[:, :], in_=dbgf[:]),
                       r=[("dbgf",)], w=[])
                dbgp = es.enter_context(nc.sbuf_tensor("dbgp", [128, 512], F32))
                Sh.dve(lambda h: nc.vector.tensor_copy(dbgp[:], ps_b[:]),
                       r=[("ps1b", 1)], w=[("dbgp",)])
                Sh.dma("sp", lambda h: h.dma_start(out=d.dbg_ps[:, :], in_=dbgp[:]),
                       r=[("dbgp",)], w=[])
                Sh.dma("sp", lambda h: h.dma_start(
                    out=d.dbg_qcsb[:, :], in_=qc_sb[:].rearrange("p c n -> p (c n)")),
                    r=[("qc_sb",)], w=[])
        Sh.barrier()


def _phase2(p):
    nc, Sh, d = p.nc, p.S, p.d
    with ExitStack() as es:
        T = lambda n, s, dt: es.enter_context(nc.sbuf_tensor(n, s, dt))
        kvc = T("kvc", [128, NCC, S], BF16)
        ktc = T("ktc", [32, S], BF16)
        kts = T("kts", [32, S], BF16)
        wku = [T(f"wku{i}", [128, NCC * 128], BF16) for i in range(2)]
        wvu = [T(f"wvu{i}", [128, NCC * 128], BF16) for i in range(2)]
        wkr = [T(f"wkr{i}", [128, NCC * RD], BF16) for i in range(2)]
        kc_sb = T("kc_sb", [128, S], BF16)
        v_sb = T("v_sb", [128, S], BF16)
        krope = [T(f"krope{i}", [RD, S], BF16) for i in range(2)]
        CH = 1024
        NCH = S // CH
        krpre = [T(f"krpre{i}", [RD, CH], BF16) for i in range(2)]
        t2b2 = [T(f"t2b2{i}", [32, CH], BF16) for i in range(2)]
        rt1 = T("rt1", [32, CH], BF16)
        rt2 = T("rt2", [32, CH], BF16)
        qc_h = T("qc_h", [128, OWN], BF16)
        qr_h = T("qr_h", [RD, OWN], BF16)
        pt = [T(f"pt{i}", [128, 1024], BF16) for i in range(2)]
        msk = [T(f"msk{i}", [128, 8 * 256], BF16) for i in range(2)]
        l1r = T("l1r", [1, 192], F32)
        sc1 = T("sc1", [1, 8], F32)
        qbv = T("qbv", [1, NH], F32)
        nb = T("nb", [128, 1], F32)
        kvm = T("kvm", [128, 1], F32)
        kvm2 = T("kvm2", [RD, 1], F32)
        bvu_c = T("bvu_c", [128, NH], F32)
        bku_c = T("bku_c", [128, NH], F32)
        bkr_c = T("bkr_c", [RD, NH], F32)
        bvro = T("bvro", [1, 128], F32)
        bvbc = T("bvbc", [128, 128], F32)
        rc = T("rc", [1, 256], F32)
        rb = T("rb", [128, 256], F32)
        ctxev = T("ctxev", [128, 256], BF16)
        st = [es.enter_context(nc.psum_tensor(f"ps2st{i}", [128, 1024], F32)) for i in range(2)]
        ctx = [es.enter_context(nc.psum_tensor(f"ps2c{i}", [128, 512], F32)) for i in range(2)]
        eB = es.enter_context(nc.psum_tensor("ps2e", [128, 256], F32))
        bld = es.enter_context(nc.psum_tensor("ps2b", [RD, 512], F32))

        if _DBG_QCH[0]:
            d.dbg_b1 = nc.dram_tensor("dbgb1", [1, 8], F32, kind="ExternalOutput").ap()
            d.dbg_b2 = nc.dram_tensor("dbgb2", [1, 192], F32, kind="ExternalOutput").ap()
            d.dbg_b3 = nc.dram_tensor("dbgb3", [128, 1], F32, kind="ExternalOutput").ap()
            d.dbg_b4 = nc.dram_tensor("dbgb4", [1, NH], F32, kind="ExternalOutput").ap()

        for cc in range(NCC):
            Sh.dma("sp", (lambda cc: lambda h: h.dma_start(
                out=kvc[:, cc, :], in_=d.z_kvc[cc, :, :]))(cc), w=[("kvc",)])
        Sh.dma("sp", lambda h: h.dma_start(out=ktc[:], in_=d.ktbl_c[:, :]), w=[("ktc",)])
        Sh.dma("sp", lambda h: h.dma_start(out=kts[:], in_=d.ktbl_s[:, :]), w=[("kts",)])
        Sh.dma("sp", lambda h: h.dma_start(out=qbv[:], in_=d.z_qb[:, :]), w=[("qbv",)])
        Sh.dma("sp", lambda h: h.dma_start(out=bvu_c[:], in_=d.b_vu[:, :]), w=[("bvu_c",)])
        Sh.dma("sp", lambda h: h.dma_start(out=bku_c[:], in_=d.b_ku[:, :]), w=[("bku_c",)])
        Sh.dma("sp", lambda h: h.dma_start(out=bkr_c[:], in_=d.b_kr[:, :]), w=[("bkr_c",)])

        def head_loads(hh):
            par = hh % 2
            Sh.dma("pool", (lambda hh, par: lambda h: h.dma_start(
                out=wku[par][:], in_=d.w_ku[hh, :, :]))(hh, par), w=[("wku", par)])
            Sh.dma("pool", (lambda hh, par: lambda h: h.dma_start(
                out=wvu[par][:], in_=d.w_vu[hh, :, :]))(hh, par), w=[("wvu", par)])
            Sh.dma("pool", (lambda hh, par: lambda h: h.dma_start(
                out=wkr[par][:], in_=d.w_kr[hh, :, :]))(hh, par), w=[("wkr", par)])

        kck = [("kc_sb", k5) for k5 in range(16)]
        krkeys = lambda par: ([("krope1", par, c) for c in range(NCH)]
                              + [("krope2", par, c) for c in range(NCH)])

        def kr_chunks(hh):
            par = hh % 2
            kro = krope[par]
            out = []
            for ch in range(NCH):
                def c1(ch, k5, par=par, hh=hh):
                    kp = krpre[ch % 2]
                    for cc in range(NCC):
                        Sh.pe((lambda ch, k5, cc, par: lambda h: nc.tensor.matmul(
                            bld[:, :], wkr[par][:, cc * RD:(cc + 1) * RD],
                            kvc[:, cc, ch * CH + k5 * 512:ch * CH + (k5 + 1) * 512],
                            start=(cc == 0), stop=(cc == NCC - 1)))(ch, k5, cc, par),
                            r=[("wkr", par), ("kvc",)], w=[("bld",)])
                    Sh.act((lambda ch, k5, hh, kp: lambda h: nc.scalar.activation(
                        out=kp[:, k5 * 512:(k5 + 1) * 512], in_=bld[:, :],
                        func=AF.Identity, bias=bkr_c[:, hh:hh + 1], scale=1.0))(ch, k5, hh, kp),
                        r=[("bld",), ("bkr_c",)], w=[("krpre", ch % 2, k5)])

                def c2(ch=ch, par=par, kro=kro):
                    kp = krpre[ch % 2]
                    tb = t2b2[ch % 2]
                    kx = [("krpre", ch % 2, k5) for k5 in range(CH // 512)]
                    cs = ktc[:, ch * CH:(ch + 1) * CH]
                    sn = kts[:, ch * CH:(ch + 1) * CH]
                    Sh.dma("pool", (lambda kp, tb: lambda h: h.dma_start(
                        out=tb[:], in_=kp[32:64, :]))(kp, tb), r=kx, w=[("t2b2", ch % 2)])
                    Sh.dve((lambda ch, cs, kp, kro, par: lambda h: nc.vector.tensor_tensor(
                        out=kro[0:32, ch * CH:(ch + 1) * CH], in0=kp[0:32, :],
                        in1=cs, op=ALU.mult))(ch, cs, kp, kro, par),
                        r=kx + [("ktc",)], w=[("krope1", par, ch)])
                    Sh.dve((lambda sn, tb: lambda h: nc.vector.tensor_tensor(
                        out=rt1[:], in0=tb[:], in1=sn, op=ALU.mult))(sn, tb),
                        r=[("t2b2", ch % 2), ("kts",)], w=[("rt1",)])
                    Sh.dve((lambda ch, kro, par: lambda h: nc.vector.tensor_sub(
                        kro[0:32, ch * CH:(ch + 1) * CH],
                        kro[0:32, ch * CH:(ch + 1) * CH], rt1[:]))(ch, kro, par),
                        r=[("krope1", par, ch), ("rt1",)], w=[("krope1", par, ch)])
                    Sh.dve((lambda cs, tb: lambda h: nc.vector.tensor_tensor(
                        out=rt2[:], in0=tb[:], in1=cs, op=ALU.mult))(cs, tb),
                        r=[("t2b2", ch % 2), ("ktc",)], w=[("rt2",)])
                    Sh.dve((lambda sn, kp, kx: lambda h: nc.vector.tensor_tensor(
                        out=rt1[:], in0=kp[0:32, :], in1=sn, op=ALU.mult))(sn, kp, kx),
                        r=kx + [("kts",)], w=[("rt1",)])
                    Sh.dve(lambda h: nc.vector.tensor_add(rt2[:], rt2[:], rt1[:]),
                           r=[("rt2",), ("rt1",)], w=[("rt2",)])
                    Sh.dma("pool", (lambda ch, kro: lambda h: h.dma_start(
                        out=kro[32:64, ch * CH:(ch + 1) * CH], in_=rt2[:]))(ch, kro),
                        r=[("rt2",)], w=[("krope2", par, ch)])
                from functools import partial
                out += [partial(c1, ch, k5) for k5 in range(CH // 512)]
                out.append(c2)

            def c3(par=par, kro=kro):
                Sh.dve((lambda kro, par: lambda h: nc.vector.tensor_reduce(
                    kvm2[:], kro[:], axis=AX.X, op=ALU.max,
                    apply_absolute_value=True))(kro, par),
                    r=krkeys(par), w=[("kvm2",)])
                Sh.dve(lambda h: nc.vector.tensor_mul(kvm2[:], kvm2[:], kvm2[:]),
                       r=[("kvm2",)], w=[("kvm2",)])
                Sh.pe(lambda h: nc.tensor.transpose(eB[0:1, 128:192], kvm2[:], p.id_f[0:RD, 0:RD]),
                      r=[("kvm2",), ("id_f",)], w=[("ps2e",)])
                Sh.dve(lambda h: nc.vector.tensor_copy(l1r[:, 128:192], eB[0:1, 128:192]),
                       r=[("ps2e",)], w=[("l1r", 1)])
            out.append(c3)
            return out

        def bound_finish(hh):
            Sh.dve(lambda h: nc.vector.tensor_reduce(
                kvm[:], kc_sb[:], axis=AX.X, op=ALU.max, apply_absolute_value=True),
                r=kck, w=[("kvm",)])
            Sh.dve(lambda h: nc.vector.tensor_mul(kvm[:], kvm[:], kvm[:]),
                   r=[("kvm",)], w=[("kvm",)])
            Sh.pe(lambda h: nc.tensor.transpose(eB[0:1, 0:128], kvm[:], p.id_f[:]),
                  r=[("kvm",), ("id_f",)], w=[("ps2e",)])
            Sh.dve(lambda h: nc.vector.tensor_copy(l1r[:, 0:128], eB[0:1, 0:128]),
                   r=[("ps2e",)], w=[("l1r", 0)])
            Sh.dve(lambda h: nc.vector.reduce_sum(sc1[:, 1:2], l1r[:], axis=AX.X),
                   r=[("l1r", 0), ("l1r", 1)], w=[("sc1", 1)])
            Sh.act((lambda hh: lambda h: nc.scalar.activation(
                out=sc1[:, 2:3], in_=qbv[:, hh:hh + 1], func=AF.Ln))(hh),
                r=[("qbv",)], w=[("sc1", 2)])
            Sh.act(lambda h: nc.scalar.activation(
                out=sc1[:, 3:4], in_=sc1[:, 1:2], func=AF.Ln),
                r=[("sc1", 1)], w=[("sc1", 3)])
            Sh.dve(lambda h: nc.vector.tensor_add(sc1[:, 4:5], sc1[:, 2:3], sc1[:, 3:4]),
                   r=[("sc1", 2), ("sc1", 3)], w=[("sc1", 4)])
            Sh.act(lambda h: nc.scalar.activation(
                out=sc1[:, 5:6], in_=sc1[:, 4:5], func=AF.Exp, scale=0.5),
                r=[("sc1", 4)], w=[("sc1", 5)])
            Sh.dve(lambda h: nc.vector.tensor_scalar_mul(sc1[:, 6:7], sc1[:, 5:6], -1.0),
                   r=[("sc1", 5)], w=[("sc1", 6)])
            Sh.pe(lambda h: nc.tensor.matmul(
                eB[:, 0:1], p.ones_f[0:1, :], sc1[0:1, 6:7], start=True, stop=True),
                r=[("sc1", 6), ("ones_f",)], w=[("ps2e",)])
            Sh.dve(lambda h: nc.vector.tensor_copy(nb[:], eB[:, 0:1]),
                   r=[("ps2e",)], w=[("nb",)])
            if _NB_ZERO[0]:
                Sh.dve(lambda h: nc.vector.memset(nb[:], 0.0), w=[("nb",)])

        head_loads(0)
        pend = []
        slot_ctr = [0]

        def emit_slot():
            slot_ctr[0] += 1
            if pend and slot_ctr[0] % 2 == 0:
                pend.pop(0)()

        for hh in range(NH):
            par = hh % 2
            kroA = krope[par]
            krkA = krkeys(par)
            if hh + 1 < NH:
                head_loads(hh + 1)
            Sh.dma("sp", (lambda hh: lambda h: h.dma_start(
                out=qc_h[:], in_=d.z_q[hh, :, :]))(hh), w=[("qc_h",)])
            Sh.dma("sp", (lambda hh: lambda h: h.dma_start(
                out=qr_h[:], in_=d.z_qr[hh, :, :]))(hh), w=[("qr_h",)])

            for k5 in range(16):
                bank = k5 % 2
                ba = st[bank][:, 0:512]
                for cc in range(NCC):
                    Sh.pe((lambda k5, cc, par, ba: lambda h: nc.tensor.matmul(
                        ba, wku[par][:, cc * 128:(cc + 1) * 128],
                        kvc[:, cc, k5 * 512:(k5 + 1) * 512],
                        start=(cc == 0), stop=(cc == NCC - 1)))(k5, cc, par, ba),
                        r=[("wku", par), ("kvc",)], w=[("ps2st", bank)])
                Sh.act((lambda k5, hh, bank, ba: lambda h: nc.scalar.activation(
                    out=kc_sb[:, k5 * 512:(k5 + 1) * 512], in_=ba, func=AF.Identity,
                    bias=bku_c[:, hh:hh + 1], scale=1.0))(k5, hh, bank, ba),
                    r=[("ps2st", bank), ("bku_c",)], w=[("kc_sb", k5)])

            if hh == 0:
                for c in kr_chunks(0):
                    c()
            bound_finish(hh)
            if _DBG_QCH[0] and hh == 2:
                Sh.dma("sp", lambda h: h.dma_start(out=d.dbg_b1[:, :], in_=sc1[:]),
                       r=[("sc1", i) for i in range(7)], w=[])
                Sh.dma("sp", lambda h: h.dma_start(out=d.dbg_b2[:, :], in_=l1r[:]),
                       r=[("l1r", 0), ("l1r", 1)], w=[])
                Sh.dma("sp", lambda h: h.dma_start(out=d.dbg_b3[:, :], in_=nb[:]),
                       r=[("nb",)], w=[])
                Sh.dma("sp", lambda h: h.dma_start(out=d.dbg_b4[:, :], in_=qbv[:]),
                       r=[("qbv",)], w=[])

            Sh.pe((lambda hh: lambda h: nc.tensor.transpose(
                eB[0:1, 0:128], bvu_c[:, hh:hh + 1], p.id_f[:]))(hh),
                r=[("bvu_c",), ("id_f",)], w=[("ps2e",)])
            Sh.dve(lambda h: nc.vector.tensor_copy(bvro[:], eB[0:1, 0:128]),
                   r=[("ps2e",)], w=[("bvro",)])
            Sh.pe(lambda h: nc.tensor.matmul(
                eB[:, 0:128], p.ones_f[0:1, :], bvro[:], start=True, stop=True),
                r=[("bvro",), ("ones_f",)], w=[("ps2e",)])
            Sh.dve(lambda h: nc.vector.tensor_copy(bvbc[:], eB[:, 0:128]),
                   r=[("ps2e",)], w=[("bvbc",)])
            for kg in range(16):
                bank = kg % 2
                bt = st[bank][:, 0:512]
                for ki in range(4):
                    kt = kg * 4 + ki
                    for cc in range(NCC):
                        Sh.pe((lambda kt, ki, cc, par, bank: lambda h: nc.tensor.matmul(
                            st[bank][:, ki * 128:(ki + 1) * 128],
                            kvc[:, cc, kt * 128:(kt + 1) * 128],
                            wvu[par][:, cc * 128:(cc + 1) * 128],
                            start=(ki == 0 and cc == 0), stop=(ki == 3 and cc == NCC - 1),
                            skip_group_check=True))(kt, ki, cc, par, bank),
                            r=[("wvu", par), ("kvc",)], w=[("ps2st", bank)])
                Sh.dve((lambda kg, bank: lambda h: nc.vector.tensor_tensor(
                    out=v_sb[:, kg * 512:(kg + 1) * 512].rearrange("p (a b) -> p a b", a=4),
                    in0=st[bank][:, 0:512].rearrange("p (a b) -> p a b", a=4),
                    in1=bvbc[:].rearrange("p (o b) -> p o b", o=1).to_broadcast([128, 4, 128]),
                    op=ALU.add))(kg, bank),
                    r=[("ps2st", bank), ("bvbc",)], w=[("v_sb", kg)])

            pend = kr_chunks(hh + 1) if hh + 1 < NH else []
            vk = [("v_sb", kg) for kg in range(16)]
            pending_epi = []

            def flush_epi():
                while pending_epi:
                    pending_epi.pop(0)()

            def mk_epi(jt, hh):
                def epi():
                    Sh.dve((lambda jt: lambda h: nc.vector.reciprocal(
                        rc[:], ctx[jt % 2][0:1, 256:512]))(jt),
                        r=[("ps2c", jt % 2)], w=[("rc",)])
                    Sh.pe(lambda h: nc.tensor.matmul(
                        eB[:, 0:256], p.ones_f[0:1, :], rc[:], start=True, stop=True),
                        r=[("rc",), ("ones_f",)], w=[("ps2e",)])
                    Sh.dve(lambda h: nc.vector.tensor_copy(rb[:], eB[:, 0:256]),
                           r=[("ps2e",)], w=[("rb",)])
                    Sh.dve((lambda jt: lambda h: nc.vector.tensor_tensor(
                        out=ctxev[:], in0=ctx[jt % 2][:, 0:256], in1=rb[:], op=ALU.mult))(jt),
                        r=[("ps2c", jt % 2), ("rb",)], w=[("ctxev",)])
                    Sh.dma("sp", (lambda hh, jt: lambda h: h.dma_start(
                        out=d.z_ctx[hh, :, jt * 256:(jt + 1) * 256], in_=ctxev[:]))(hh, jt),
                        r=[("ctxev",)], w=[])
                return epi

            for pr in range(NTILE // 2):
                jtE, jtO = 2 * pr, 2 * pr + 1
                TE, TO = tile_T(jtE), tile_T(jtO)
                q0 = jtE * 256
                for jt in (jtE, jtO):
                    Sh.dma("sp", (lambda hh, jt: lambda h: h.dma_start(
                        out=msk[jt % 2][:], in_=d.masks[jt, :, :]))(hh, jt),
                        w=[("msk", jt % 2)])

                def s_joint(b):
                    sb = b % 2
                    for i in range(2):
                        kt = 2 * b + i
                        Sh.pe((lambda kt, i, sb, q0: lambda h: nc.tensor.matmul(
                            st[sb][:, i * 512:(i + 1) * 512],
                            kc_sb[:, kt * 128:(kt + 1) * 128], qc_h[:, q0:q0 + 512],
                            start=True, stop=False))(kt, i, sb, q0),
                            r=kck + [("qc_h",)], w=[("ps2st", sb)])
                        Sh.pe((lambda kt, i, sb, q0, kroA: lambda h: nc.tensor.matmul(
                            st[sb][:, i * 512:(i + 1) * 512],
                            kroA[:, kt * 128:(kt + 1) * 128], qr_h[:, q0:q0 + 512],
                            start=False, stop=True))(kt, i, sb, q0, kroA),
                            r=krkA + [("qr_h",)], w=[("ps2st", sb)])

                def exp_b(b):
                    sb = b % 2
                    Sh.act((lambda sb: lambda h: nc.scalar.activation(
                        out=pt[sb][:], in_=st[sb][:], func=AF.Exp,
                        bias=nb[:, 0:1], scale=1.0))(sb),
                        r=[("ps2st", sb), ("nb",)], w=[("pt", sb)])

                def pv_joint(b):
                    sb = b % 2
                    for i in range(2):
                        kt = 2 * b + i
                        if kt >= TO - 8:
                            mi = kt - (TO - 8)
                            Sh.dve((lambda sb, i, mi: lambda h: nc.vector.tensor_tensor(
                                out=pt[sb][:, i * 512 + 256:(i + 1) * 512],
                                in0=pt[sb][:, i * 512 + 256:(i + 1) * 512],
                                in1=msk[1][:, mi * 256:(mi + 1) * 256],
                                op=ALU.mult))(sb, i, mi),
                                r=[("pt", sb), ("msk", 1)], w=[("pt", sb)])
                    for i in range(2):
                        kt = 2 * b + i
                        for (jt, off) in ((jtE, 0), (jtO, 256)):
                            Sh.pe((lambda kt, i, sb, jt, off: lambda h: nc.tensor.matmul(
                                ctx[jt % 2][:, 0:256], v_sb[:, kt * 128:(kt + 1) * 128],
                                pt[sb][:, i * 512 + off:i * 512 + off + 256],
                                start=(kt == 0), stop=False,
                                skip_group_check=True))(kt, i, sb, jt, off),
                                r=vk + [("pt", sb)], w=[("ps2c", jt % 2)])
                            Sh.pe((lambda kt, i, sb, jt, off: lambda h: nc.tensor.matmul(
                                ctx[jt % 2][0:1, 256:512], p.ones_b[:],
                                pt[sb][:, i * 512 + off:i * 512 + off + 256],
                                start=False, stop=(jt == jtO and kt == TO - 1),
                                skip_group_check=True))(kt, i, sb, jt, off),
                                r=[("pt", sb), ("ones_b",)], w=[("ps2c", jt % 2)])

                def s_solo(b2):
                    sb = b2 % 2
                    for i in range(4):
                        kt = TO + 4 * b2 + i
                        Sh.pe((lambda kt, i, sb, q0: lambda h: nc.tensor.matmul(
                            st[sb][:, i * 256:(i + 1) * 256],
                            kc_sb[:, kt * 128:(kt + 1) * 128], qc_h[:, q0:q0 + 256],
                            start=True, stop=False))(kt, i, sb, q0),
                            r=kck + [("qc_h",)], w=[("ps2st", sb)])
                        Sh.pe((lambda kt, i, sb, q0, kroA: lambda h: nc.tensor.matmul(
                            st[sb][:, i * 256:(i + 1) * 256],
                            kroA[:, kt * 128:(kt + 1) * 128], qr_h[:, q0:q0 + 256],
                            start=False, stop=True))(kt, i, sb, q0, kroA),
                            r=krkA + [("qr_h",)], w=[("ps2st", sb)])

                def pv_solo(b2):
                    sb = b2 % 2
                    for i in range(4):
                        kt = TO + 4 * b2 + i
                        mi = kt - (TE - 8)
                        Sh.dve((lambda sb, i, mi: lambda h: nc.vector.tensor_tensor(
                            out=pt[sb][:, i * 256:(i + 1) * 256],
                            in0=pt[sb][:, i * 256:(i + 1) * 256],
                            in1=msk[0][:, mi * 256:(mi + 1) * 256],
                            op=ALU.mult))(sb, i, mi),
                            r=[("pt", sb), ("msk", 0)], w=[("pt", sb)])
                    for i in range(4):
                        kt = TO + 4 * b2 + i
                        Sh.pe((lambda kt, i, sb: lambda h: nc.tensor.matmul(
                            ctx[0][:, 0:256], v_sb[:, kt * 128:(kt + 1) * 128],
                            pt[sb][:, i * 256:(i + 1) * 256],
                            start=False, stop=False,
                            skip_group_check=True))(kt, i, sb),
                            r=vk + [("pt", sb)], w=[("ps2c", 0)])
                        Sh.pe((lambda kt, i, sb: lambda h: nc.tensor.matmul(
                            ctx[0][0:1, 256:512], p.ones_b[:],
                            pt[sb][:, i * 256:(i + 1) * 256],
                            start=False, stop=(kt == TE - 1),
                            skip_group_check=True))(kt, i, sb),
                            r=[("pt", sb), ("ones_b",)], w=[("ps2c", 0)])

                nbj = TO // 2
                for b in range(nbj):
                    emit_slot()
                    s_joint(b)
                    if b == 1:
                        flush_epi()
                    if b > 0:
                        pv_joint(b - 1)
                    exp_b(b)
                pv_joint(nbj - 1)
                pending_epi.append(mk_epi(jtO, hh))
                for b2 in range(2):
                    emit_slot()
                    s_solo(b2)
                    if b2 > 0:
                        pv_solo(b2 - 1)
                    exp_b(b2)
                pv_solo(1)
                pending_epi.append(mk_epi(jtE, hh))
            flush_epi()
            while pend:
                pend.pop(0)()
        Sh.barrier()


def _phase2b(p):
    nc, Sh, d = p.nc, p.S, p.d
    with ExitStack() as es:
        T = lambda n, s, dt: es.enter_context(nc.sbuf_tensor(n, s, dt))
        w_o = T("sb_w_o", [128, NDC * D], BF16)
        ctxt = [T(f"ctxt{i}", [128, NH * 128], BF16) for i in range(2)]
        bo_r = T("bo_r", [1, D], F32)
        bo_bc = T("bo_bc", [128, D], F32)
        aout = T("aout", [128, 512], F32)
        ps = [es.enter_context(nc.psum_tensor(f"ps2b{i}", [128, 512], F32)) for i in range(2)]
        psb = es.enter_context(nc.psum_tensor("ps2bb", [128, 512], F32))

        Sh.dma("pool", lambda h: h.dma_start(out=w_o[:], in_=d.w_o[:, :]), w=[("w_o",)])
        Sh.dma("sp", lambda h: h.dma_start(out=bo_r[:], in_=d.b_o[:, :]), w=[("bo_r",)])
        for q in range(4):
            Sh.pe((lambda q: lambda h: nc.tensor.matmul(
                psb[:], p.ones_f[0:1, :], bo_r[0:1, q * 512:(q + 1) * 512],
                start=True, stop=True))(q), r=[("bo_r",), ("ones_f",)], w=[("ps2bb",)])
            Sh.dve((lambda q: lambda h: nc.vector.tensor_copy(
                bo_bc[:, q * 512:(q + 1) * 512], psb[:]))(q),
                r=[("ps2bb",)], w=[("bo_bc",)])
        for t1 in range(16):
            par = t1 % 2
            for hc in range(NH):
                Sh.dma("sp", (lambda t1, hc, par: lambda h: h.dma_start(
                    out=ctxt[par][:, hc * 128:(hc + 1) * 128],
                    in_=d.z_ctx[hc, :, t1 * 128:(t1 + 1) * 128]))(t1, hc, par),
                    w=[("ctxt", par, hc)])
            for h5 in range(4):
                bank = h5 % 2
                for hc in range(NH):
                    Sh.pe((lambda t1, h5, hc, par, bank: lambda h: nc.tensor.matmul(
                        ps[bank][:], ctxt[par][:, hc * 128:(hc + 1) * 128],
                        w_o[:, hc * D + h5 * 512:hc * D + (h5 + 1) * 512],
                        start=(hc == 0), stop=(hc == NH - 1)))(t1, h5, hc, par, bank),
                        r=[("ctxt", par, hc), ("w_o",)], w=[("ps2b", bank)])
                Sh.dve((lambda t1, h5, bank: lambda h: nc.vector.tensor_tensor(
                    out=aout[:], in0=ps[bank][:], in1=bo_bc[:, h5 * 512:(h5 + 1) * 512],
                    op=ALU.add))(t1, h5, bank),
                    r=[("ps2b", bank), ("bo_bc",)], w=[("aout",)])
                Sh.dma("act", (lambda t1, h5: lambda h: h.dma_start(
                    out=d.z_attn[t1 * 128:(t1 + 1) * 128, h5 * 512:(h5 + 1) * 512],
                    in_=aout[:]))(t1, h5), r=[("aout",)], w=[])
        Sh.barrier()


def _phase3(p):
    nc, Sh, d = p.nc, p.S, p.d
    with ExitStack() as es:
        T = lambda n, s, dt: es.enter_context(nc.sbuf_tensor(n, s, dt))
        xn = T("xn3", [128, NDC, OWN], BF16)
        gw = T("gw", [128, NDC * NE], BF16)
        logit = T("logit", [128, 16 * NE], F32)
        nmx = T("nmx", [128, 16], F32)
        se = T("se", [128, 16], F32)
        rse = T("rse", [128, 16], F32)
        esb = T("esb", [128, 16 * NE], F32)
        scrs = T("scrs", [128, 16 * NE], F32)
        t8 = T("t8", [128, 8], F32)
        thr = T("thr", [128, 16], F32)
        msk3 = T("msk3", [128, NE], F32)
        wesb = T("wesb", [128, 16 * NE], F32)
        invf = T("invf", [1, NE], F32)
        invi = T("invi_sb", [1, NE], I32)
        invb = T("invb", [NE, NE], F32)
        iota = T("iota3", [NE, 1], F32)
        pone = T("pone", [NE, NE], F32)
        weT = T("weT", [NE, 128], F32)
        flat = T("flat", [128, 16 * NE], F32)
        sg = T("sg", [128, 16 * GS], F32)
        qq = T("qq", [128, 16 * NE], F32)
        mkd = T("mkd", [128, 16 * NE], F32)
        fmx = T("fmx", [128, 16 * GS], F32)
        fse = T("fse", [128, 16 * GS], F32)
        soft_bf = T("soft_bf", [128, 16 * NE], BF16)
        softT = T("softT", [GS, OWN], BF16)
        sgrow = T("sgrow", [1, OWN], F32)
        sbb = T("sbb", [128, OWN], F32)
        mu_sb = T("mu_sb", [128, NDC * GS], BF16)
        wgb = [T(f"wgb{i}", [128, NDC * 128], BF16) for i in range(2)]
        wub = [T(f"wub{i}", [128, NDC * 128], BF16) for i in range(2)]
        mgb = T("mgb", [128, GS], BF16)
        m_g = T("m_g", [GS, 128], BF16)
        m_u = T("m_u", [GS, 128], BF16)
        sg_bf = T("sg_bf", [128, 512], BF16)
        a1t = T("a1t", [128, 512], BF16)
        a_t = T("a_t", [128, 512], BF16)
        dcast = [T(f"dcast{i}", [128, D], BF16) for i in range(2)]
        ps_g = [es.enter_context(nc.psum_tensor(f"ps3g{i}", [128, 512], F32)) for i in range(2)]
        ps_u = [es.enter_context(nc.psum_tensor(f"ps3u{i}", [128, 512], F32)) for i in range(2)]
        ps_m = es.enter_context(nc.psum_tensor("ps3m", [128, 512], F32))
        ps_x = es.enter_context(nc.psum_tensor("ps3x", [128, 512], F32))
        ps_t = es.enter_context(nc.psum_tensor("ps3t", [128, 512], BF16))

        for dc in range(NDC):
            Sh.dma("sp", (lambda dc: lambda h: h.dma_start(
                out=xn[:, dc, :], in_=d.z_xn[dc, :, :]))(dc), w=[("xn", dc)])
        Sh.dma("pool", lambda h: h.dma_start(out=gw[:], in_=d.gate_wT[:, :]), w=[("gw",)])
        Sh.dma("sp", lambda h: h.dma_start(out=invi[:], in_=d.inv[:, :]), w=[("invi",)])
        Sh.dve(lambda h: nc.vector.tensor_copy(invf[:], invi[:]), r=[("invi",)], w=[("invf",)])
        Sh.pool(lambda h: nc.gpsimd.iota(iota[:], pattern=[[0, 1]], base=0,
                                         channel_multiplier=1,
                                         allow_small_or_imprecise_dtypes=True), w=[("iota",)])
        Sh.pe(lambda h: nc.tensor.matmul(ps_m[0:NE, 0:NE], p.ones_f[0:1, 0:NE], invf[:],
                                         start=True, stop=True),
              r=[("invf",), ("ones_f",)], w=[("ps3m",)])
        Sh.dve(lambda h: nc.vector.tensor_copy(invb[:], ps_m[0:NE, 0:NE]),
               r=[("ps3m",)], w=[("invb",)])
        Sh.dve(lambda h: nc.vector.tensor_single_scalar(
            out=pone[:], in_=invb[:], scalar=iota[:, 0:1], op=ALU.is_equal),
            r=[("invb",), ("iota",)], w=[("pone",)])
        for t1 in range(16):
            bank = t1 % 2
            pj = [ps_g, ps_u][bank][0]
            pk = ("ps3gb", 0) if bank == 0 else ("ps3ub", 0)
            for dc in range(NDC):
                Sh.pe((lambda t1, dc, pj: lambda h: nc.tensor.matmul(
                    pj[:, 0:NE], xn[:, dc, t1 * 128:(t1 + 1) * 128],
                    gw[:, dc * NE:(dc + 1) * NE],
                    start=(dc == 0), stop=(dc == NDC - 1)))(t1, dc, pj),
                    r=[("xn", dc), ("gw",)], w=[pk])
            Sh.dve((lambda t1, pj: lambda h: nc.vector.tensor_copy(
                logit[:, t1 * NE:(t1 + 1) * NE], pj[:, 0:NE]))(t1, pj),
                r=[pk], w=[("logit", t1)])
        for t1 in range(16):
            Sh.dve((lambda t1: lambda h: nc.vector.reduce_max(
                nmx[:, t1:t1 + 1], logit[:, t1 * NE:(t1 + 1) * NE], axis=AX.X,
                negate=True))(t1), r=[("logit", t1)], w=[("nmx", t1)])
            Sh.act((lambda t1: lambda h: nc.scalar.activation(
                out=esb[:, t1 * NE:(t1 + 1) * NE], in_=logit[:, t1 * NE:(t1 + 1) * NE],
                func=AF.Exp, bias=nmx[:, t1:t1 + 1],
                accum_out=se[:, t1:t1 + 1]))(t1),
                r=[("logit", t1), ("nmx", t1)], w=[("esb", t1), ("se", t1)])
        Sh.dve(lambda h: nc.vector.reciprocal(rse[:], se[:]),
               r=[("se", t1) for t1 in range(16)], w=[("rse",)])
        Sh.dve(lambda h: nc.vector.tensor_tensor(
            out=scrs[:].rearrange("p (a b) -> p a b", a=16),
            in0=esb[:].rearrange("p (a b) -> p a b", a=16),
            in1=rse[:].rearrange("p (a o) -> p a o", o=1).to_broadcast([128, 16, NE]),
            op=ALU.mult),
            r=[("esb", t1) for t1 in range(16)] + [("rse",)], w=[("scrs",)])
        for t1 in range(16):
            Sh.dve((lambda t1: lambda h: nc.vector.max(
                out=t8[:, 0:8], in_=scrs[:, t1 * NE:(t1 + 1) * NE]))(t1),
                r=[("scrs",)], w=[("t8",)])
            Sh.dve((lambda t1: lambda h: nc.vector.tensor_copy(
                thr[:, t1:t1 + 1], t8[:, 7:8]))(t1), r=[("t8",)], w=[("thr", t1)])
            Sh.dve((lambda t1: lambda h: nc.vector.tensor_single_scalar(
                out=msk3[:], in_=scrs[:, t1 * NE:(t1 + 1) * NE],
                scalar=thr[:, t1:t1 + 1], op=ALU.is_ge))(t1),
                r=[("scrs",), ("thr", t1)], w=[("msk3",)])
            Sh.dve((lambda t1: lambda h: nc.vector.tensor_mul(
                wesb[:, t1 * NE:(t1 + 1) * NE], scrs[:, t1 * NE:(t1 + 1) * NE],
                msk3[:]))(t1), r=[("scrs",), ("msk3",)], w=[("wesb", t1)])
            Sh.pe((lambda t1: lambda h: nc.tensor.transpose(
                ps_m[0:NE, 0:128], wesb[:, t1 * NE:(t1 + 1) * NE], p.id_f[:]))(t1),
                r=[("wesb", t1), ("id_f",)], w=[("ps3m",)])
            Sh.dve((lambda t1: lambda h: nc.vector.tensor_copy(
                weT[:], ps_m[0:NE, 0:128]))(t1), r=[("ps3m",)], w=[("weT",)])
            Sh.pe((lambda t1: lambda h: nc.tensor.matmul(
                ps_x[:, 0:NE], weT[:], pone[:], start=True, stop=True))(t1),
                r=[("weT",), ("pone",)], w=[("ps3x",)])
            Sh.dve((lambda t1: lambda h: nc.vector.tensor_copy(
                flat[:, t1 * NE:(t1 + 1) * NE], ps_x[:, 0:NE]))(t1),
                r=[("ps3x",)], w=[("flat", t1)])
        fk = [("flat", t1) for t1 in range(16)]
        Sh.dve(lambda h: nc.vector.reduce_sum(
            sg[:].rearrange("p (a g o) -> p a g o", a=16, o=1),
            flat[:].rearrange("p (a g i) -> p a g i", a=16, g=GS), axis=AX.X),
            r=fk, w=[("sg",)])
        Sh.dve(lambda h: nc.vector.tensor_single_scalar(
            out=qq[:], in_=flat[:], scalar=0.0, op=ALU.is_equal), r=fk, w=[("qq",)])
        Sh.dve(lambda h: nc.vector.scalar_tensor_tensor(
            out=mkd[:], in0=qq[:], scalar=-1e9, in1=flat[:], op0=ALU.mult, op1=ALU.add),
            r=[("qq",)] + fk, w=[("mkd",)])
        Sh.dve(lambda h: nc.vector.tensor_reduce(
            fmx[:].rearrange("p (a g o) -> p a g o", a=16, o=1),
            mkd[:].rearrange("p (a g i) -> p a g i", a=16, g=GS), axis=AX.X,
            op=ALU.max), r=[("mkd",)], w=[("fmx",)])
        Sh.dve(lambda h: nc.vector.tensor_tensor(
            out=mkd[:].rearrange("p (a g i) -> p a g i", a=16, g=GS),
            in0=mkd[:].rearrange("p (a g i) -> p a g i", a=16, g=GS),
            in1=fmx[:].rearrange("p (a g o) -> p a g o", a=16, o=1)
            .to_broadcast([128, 16, GS, GS]), op=ALU.subtract),
            r=[("mkd",), ("fmx",)], w=[("mkd",)])
        Sh.act(lambda h: nc.scalar.activation(out=esb[:], in_=mkd[:], func=AF.Exp),
               r=[("mkd",)], w=[("esb", "x")])
        Sh.dve(lambda h: nc.vector.reduce_sum(
            fse[:].rearrange("p (a g o) -> p a g o", a=16, o=1),
            esb[:].rearrange("p (a g i) -> p a g i", a=16, g=GS), axis=AX.X),
            r=[("esb", "x")], w=[("fse",)])
        Sh.dve(lambda h: nc.vector.reciprocal(fse[:], fse[:]), r=[("fse",)], w=[("fse",)])
        Sh.dve(lambda h: nc.vector.tensor_tensor(
            out=soft_bf[:].rearrange("p (a g i) -> p a g i", a=16, g=GS),
            in0=esb[:].rearrange("p (a g i) -> p a g i", a=16, g=GS),
            in1=fse[:].rearrange("p (a g o) -> p a g o", a=16, o=1)
            .to_broadcast([128, 16, GS, GS]), op=ALU.mult),
            r=[("esb", "x"), ("fse",)], w=[("soft_bf",)])

        for g in range(NG):
            for t1 in range(16):
                Sh.pe((lambda g, t1: lambda h: nc.tensor.transpose(
                    ps_t[0:GS, 0:128],
                    soft_bf[:, t1 * NE + g * GS:t1 * NE + (g + 1) * GS], p.id_b[:]))(g, t1),
                    r=[("soft_bf",), ("id_b",)], w=[("ps3t",)])
                Sh.dve((lambda g, t1: lambda h: nc.vector.tensor_copy(
                    softT[:, t1 * 128:(t1 + 1) * 128], ps_t[0:GS, 0:128]))(g, t1),
                    r=[("ps3t",)], w=[("softT",)])
                Sh.pe((lambda g, t1: lambda h: nc.tensor.transpose(
                    ps_x[0:1, 0:128], sg[:, t1 * GS + g:t1 * GS + g + 1], p.id_f[:]))(g, t1),
                    r=[("sg",), ("id_f",)], w=[("ps3x",)])
                Sh.dve((lambda g, t1: lambda h: nc.vector.tensor_copy(
                    sgrow[:, t1 * 128:(t1 + 1) * 128], ps_x[0:1, 0:128]))(g, t1),
                    r=[("ps3x",)], w=[("sgrow",)])
            for t5 in range(4):
                Sh.pe((lambda g, t5: lambda h: nc.tensor.matmul(
                    ps_x[:], p.ones_f[0:1, :], sgrow[0:1, t5 * 512:(t5 + 1) * 512],
                    start=True, stop=True))(g, t5),
                    r=[("sgrow",), ("ones_f",)], w=[("ps3x",)])
                Sh.dve((lambda g, t5: lambda h: nc.vector.tensor_copy(
                    sbb[:, t5 * 512:(t5 + 1) * 512], ps_x[:]))(g, t5),
                    r=[("ps3x",)], w=[("sbb",)])
            Sh.dma("pool", (lambda g: lambda h: h.dma_start(
                out=mu_sb[:], in_=d.maskup[g, :, :]))(g), w=[("mu_sb",)])
            for ic in range(NIC):
                par = ic % 2
                Sh.dma("pool", (lambda g, ic, par: lambda h: h.dma_start(
                    out=wgb[par][:], in_=d.wgT[g, ic, :, :]))(g, ic, par), w=[("wgb", par)])
                Sh.dma("pool", (lambda g, ic, par: lambda h: h.dma_start(
                    out=wub[par][:], in_=d.wuT[g, ic, :, :]))(g, ic, par), w=[("wub", par)])
                Sh.dma("pool", (lambda g, ic, par: lambda h: h.dma_start(
                    out=dcast[par][:], in_=d.downT[g, ic, :, :]))(g, ic, par),
                    w=[("dcast", par)])
                Sh.dma("act", (lambda g, ic, par: lambda h: h.dma_start(
                    out=d.z_dwn[g, ic, :, :], in_=dcast[par][:]))(g, ic, par),
                    r=[("dcast", par)], w=[])
                for (wb, mdst, mkey) in ((wgb, m_g, "m_g"), (wub, m_u, "m_u")):
                    for dc in range(NDC):
                        Sh.pe((lambda wb, par, dc: lambda h: nc.tensor.matmul(
                            ps_m[:, 0:GS], wb[par][:, dc * 128:(dc + 1) * 128],
                            mu_sb[:, dc * GS:(dc + 1) * GS],
                            start=(dc == 0), stop=(dc == NDC - 1)))(wb, par, dc),
                            r=[(("wgb" if wb is wgb else "wub"), par), ("mu_sb",)],
                            w=[("ps3m",)])
                    Sh.dve(lambda h: nc.vector.tensor_copy(mgb[:], ps_m[:, 0:GS]),
                           r=[("ps3m",)], w=[("mgb",)])
                    Sh.pe(lambda h: nc.tensor.transpose(
                        ps_t[0:GS, 0:128], mgb[:], p.id_b[:]),
                        r=[("mgb",), ("id_b",)], w=[("ps3t",)])
                    Sh.dve((lambda mdst, mkey: lambda h: nc.vector.tensor_copy(
                        mdst[:], ps_t[0:GS, 0:128]))(mdst, mkey),
                        r=[("ps3t",)], w=[(mkey,)])
                for t5 in range(4):
                    bg = t5 % 2
                    for dc in range(NDC):
                        Sh.pe((lambda par, dc, t5, bg: lambda h: nc.tensor.matmul(
                            ps_g[bg][:], wgb[par][:, dc * 128:(dc + 1) * 128],
                            xn[:, dc, t5 * 512:(t5 + 1) * 512],
                            start=(dc == 0), stop=False))(par, dc, t5, bg),
                            r=[("wgb", par), ("xn", dc)], w=[("ps3gb", bg)])
                    Sh.pe((lambda t5, bg: lambda h: nc.tensor.matmul(
                        ps_g[bg][:], m_g[:], softT[:, t5 * 512:(t5 + 1) * 512],
                        start=False, stop=True))(t5, bg),
                        r=[("m_g",), ("softT",)], w=[("ps3gb", bg)])
                    for dc in range(NDC):
                        Sh.pe((lambda par, dc, t5, bg: lambda h: nc.tensor.matmul(
                            ps_u[bg][:], wub[par][:, dc * 128:(dc + 1) * 128],
                            xn[:, dc, t5 * 512:(t5 + 1) * 512],
                            start=(dc == 0), stop=False))(par, dc, t5, bg),
                            r=[("wub", par), ("xn", dc)], w=[("ps3ub", bg)])
                    Sh.pe((lambda t5, bg: lambda h: nc.tensor.matmul(
                        ps_u[bg][:], m_u[:], softT[:, t5 * 512:(t5 + 1) * 512],
                        start=False, stop=True))(t5, bg),
                        r=[("m_u",), ("softT",)], w=[("ps3ub", bg)])
                    Sh.act((lambda t5, bg: lambda h: nc.scalar.activation(
                        out=sg_bf[:], in_=ps_g[bg][:], func=AF.Silu))(t5, bg),
                        r=[("ps3gb", bg)], w=[("sg_bf",)])
                    Sh.dve((lambda t5, bg: lambda h: nc.vector.tensor_tensor(
                        out=a1t[:], in0=ps_u[bg][:], in1=sg_bf[:], op=ALU.mult))(t5, bg),
                        r=[("ps3ub", bg), ("sg_bf",)], w=[("a1t",)])
                    Sh.dve((lambda t5: lambda h: nc.vector.tensor_tensor(
                        out=a_t[:], in0=a1t[:], in1=sbb[:, t5 * 512:(t5 + 1) * 512],
                        op=ALU.mult))(t5), r=[("a1t",), ("sbb",)], w=[("a_t",)])
                    Sh.dma("act", (lambda g, ic, t5: lambda h: h.dma_start(
                        out=d.z_a[g, ic, :, t5 * 512:(t5 + 1) * 512], in_=a_t[:]))(g, ic, t5),
                        r=[("a_t",)], w=[])
        Sh.barrier()

    with ExitStack() as es:
        T = lambda n, s, dt: es.enter_context(nc.sbuf_tensor(n, s, dt))
        dwn = [T(f"dwn{i}", [128, 512], BF16) for i in range(6)]
        abuf = [T(f"abuf{i}", [128, 1024], BF16) for i in range(6)]
        x_t = [T(f"x_t{i}", [128, 512], F32) for i in range(2)]
        at_t = [T(f"at_t{i}", [128, 512], F32) for i in range(2)]
        tmp = T("tmp3", [128, 512], F32)
        osb = T("osb", [128, 512], F32)
        psd = [es.enter_context(nc.psum_tensor(f"ps3d{i}", [128, 512], F32)) for i in range(8)]

        for th in range(2):
            for h5 in range(4):
                for g in range(NG):
                    for ic in range(NIC):
                        par = (g * NIC + ic) % 6
                        Sh.dma("act", (lambda g, ic, h5, par: lambda h: h.dma_start(
                            out=dwn[par][:], in_=d.z_dwn[g, ic, :, h5 * 512:(h5 + 1) * 512]))(g, ic, h5, par),
                            w=[("dwn", par)])
                        Sh.dma("pool", (lambda g, ic, th, par: lambda h: h.dma_start(
                            out=abuf[par][:], in_=d.z_a[g, ic, :, th * 1024:(th + 1) * 1024]))(g, ic, th, par),
                            w=[("abuf", par)])
                        first = (g == 0 and ic == 0)
                        last = (g == NG - 1 and ic == NIC - 1)
                        for t1 in range(8):
                            Sh.pe((lambda g, ic, t1, par, first, last: lambda h: nc.tensor.matmul(
                                psd[t1][:], abuf[par][:, t1 * 128:(t1 + 1) * 128],
                                dwn[par][:], start=first, stop=last))(g, ic, t1, par, first, last),
                                r=[("abuf", par), ("dwn", par)], w=[("ps3d", t1)])
                for t1 in range(8):
                    par = t1 % 2
                    tg = th * 8 + t1
                    Sh.dma("sp", (lambda tg, h5, par: lambda h: h.dma_start(
                        out=x_t[par][:], in_=d.xo[tg * 128:(tg + 1) * 128, h5 * 512:(h5 + 1) * 512]))(tg, h5, par),
                        w=[("x_t", par)])
                    Sh.dma("sp", (lambda tg, h5, par: lambda h: h.dma_start(
                        out=at_t[par][:], in_=d.z_attn[tg * 128:(tg + 1) * 128, h5 * 512:(h5 + 1) * 512]))(tg, h5, par),
                        w=[("at_t", par)])
                    Sh.dve((lambda t1, par: lambda h: nc.vector.tensor_tensor(
                        out=tmp[:], in0=psd[t1][:], in1=at_t[par][:], op=ALU.add))(t1, par),
                        r=[("ps3d", t1), ("at_t", par)], w=[("tmp3",)])
                    Sh.dve((lambda par: lambda h: nc.vector.tensor_tensor(
                        out=osb[:], in0=tmp[:], in1=x_t[par][:], op=ALU.add))(par),
                        r=[("tmp3",), ("x_t", par)], w=[("osb",)])
                    Sh.dma("sp", (lambda tg, h5: lambda h: h.dma_start(
                        out=d.out[tg * 128:(tg + 1) * 128, h5 * 512:(h5 + 1) * 512],
                        in_=osb[:]))(tg, h5), r=[("osb",)], w=[])
        Sh.barrier()


def _finish(p, upto):
    nc, Sh, d = p.nc, p.S, p.d
    if upto >= 5:
        return
    with ExitStack() as es:
        t = es.enter_context(nc.sbuf_tensor("fin_t", [128, D], F32))
        for j in range(NCHUNK):
            Sh.dma("sp", (lambda j: lambda h: h.dma_start(
                out=t[:], in_=d.xo[j * 128:(j + 1) * 128, :]))(j), r=(), w=[("fin_t",)])
            Sh.dma("sp", (lambda j: lambda h: h.dma_start(
                out=d.out[j * 128:(j + 1) * 128, :], in_=t[:]))(j), r=[("fin_t",)], w=())
        Sh.barrier()


def _rope_tables(positions):
    inv_freq = 1.0 / (10000.0 ** (np.arange(0, RD, 2, dtype=np.float64) / RD))
    ang = positions[None, :].astype(np.float64) * inv_freq[:, None]
    return (np.cos(ang).astype(np.float32).astype(bfdt),
            np.sin(ang).astype(np.float32).astype(bfdt))


def _host_masks(rows):
    m = np.zeros((NTILE, 8, 128, 256), np.float32)
    lim = rows.reshape(NTILE, 256) + 1
    for j in range(NTILE):
        T = tile_T(j)
        for i in range(8):
            kt = T - 8 + i
            kpos = 128 * kt + np.arange(128)
            m[j, i] = (kpos[:, None] < lim[j][None, :]).astype(np.float32)
    return np.ascontiguousarray(m.transpose(0, 2, 1, 3).reshape(NTILE, 128, 8 * 256)).astype(bfdt)


def _host_shared(g):
    f = lambda a: np.ascontiguousarray(np.asarray(a, np.float32))
    C = np.ascontiguousarray

    def pmaj_in(w, nch):
        n = w.shape[1]
        return C(w.reshape(nch, 128, n).transpose(1, 0, 2).reshape(128, nch * n))

    def pmaj_heads(w, nch, hd):
        n = w.shape[1] // hd
        a = w.reshape(nch, 128, n, hd).transpose(2, 1, 0, 3)
        return C(a.reshape(n, 128, nch * hd))

    shared = {
        "w_kvd": pmaj_in(f(g["L0_attn_W_kvd"]), NDC),
        "b_kvd": C(f(g["L0_attn_b_kvd"]).reshape(NCC, 128).T),
        "w_qd": pmaj_in(f(g["L0_attn_W_qd"]), NDC),
        "b_qd": C(f(g["L0_attn_b_qd"]).reshape(NQC, 128).T),
        "w_ku": pmaj_heads(f(g["L0_attn_W_ku"]), NCC, 128),
        "b_ku": C(f(g["L0_attn_b_ku"]).reshape(NH, 128).T),
        "w_vu": pmaj_heads(f(g["L0_attn_W_vu"]), NCC, 128),
        "b_vu": C(f(g["L0_attn_b_vu"]).reshape(NH, 128).T),
        "w_kr": pmaj_heads(f(g["L0_attn_W_kr"]), NCC, RD),
        "b_kr": C(f(g["L0_attn_b_kr"]).reshape(NH, RD).T),
        "w_qu": pmaj_heads(f(g["L0_attn_W_qu"]), NQC, 128),
        "b_qu": C(f(g["L0_attn_b_qu"]).reshape(NH, 128).T),
        "w_qr": pmaj_heads(f(g["L0_attn_W_qr"]), NQC, RD),
        "b_qr": C(f(g["L0_attn_b_qr"]).reshape(NH, RD).T),
        "w_o": pmaj_in(f(g["L0_attn_W_o"]), NDC),
        "b_o": f(g["L0_attn_b_o"]).reshape(1, D),
        "b_ku_r": f(g["L0_attn_b_ku"]).reshape(1, D),
        "b_kr_r": f(g["L0_attn_b_kr"]).reshape(1, NH * RD),
        "gate_wT": pmaj_in(C(f(g["L0_ffn_gate_w"]).T), NDC),
        "maskup": np.stack([pmaj_in(f(g["L0_ffn_mask_up_w"][i]), NDC) for i in range(NG)]),
        "wgT": np.stack([np.stack([pmaj_in(C(f(g["L0_ffn_gate_proj_w"][i]).T)[:, j * 128:(j + 1) * 128], NDC)
                                   for j in range(NIC)]) for i in range(NG)]),
        "wuT": np.stack([np.stack([pmaj_in(C(f(g["L0_ffn_up_proj_w"][i]).T)[:, j * 128:(j + 1) * 128], NDC)
                                   for j in range(NIC)]) for i in range(NG)]),
        "downT": np.stack([C(f(g["L0_ffn_down_proj_w"][i]).T).reshape(NIC, 128, D) for i in range(NG)]),
        "inv": np.asarray(g["L0_ffn_inv_map"], np.int64).astype(np.int32).reshape(1, NE),
        "ident": np.eye(128, dtype=bfdt),
        "identf": np.eye(128, dtype=np.float32),
    }
    kc, ks = _rope_tables(np.arange(S))
    shared["ktbl_c"], shared["ktbl_s"] = kc, ks
    return shared


def _host_core(x, c):
    b, qi = c // 4, c % 4
    chunks = own_chunks(qi)
    rows = np.concatenate([np.arange(128 * gch, 128 * gch + 128) for gch in chunks])
    qc_, qs_ = _rope_tables(rows)
    im = {
        "xb": x[b],
        "xo": np.ascontiguousarray(x[b][rows]),
        "qtbl_c": qc_, "qtbl_s": qs_,
        "masks": _host_masks(rows),
    }
    return im, b, rows


def kernel(**inputs):
    g = {k: np.asarray(v) for k, v in inputs.items()}
    x = np.ascontiguousarray(g["L0_attn_x"].astype(np.float32, copy=False))
    shared = _host_shared(g)

    nc = _build()

    in_maps = []
    owns = []
    for c in range(8):
        im, b, rows = _host_core(x, c)
        owns.append((b, rows))
        im.update(shared)
        in_maps.append(im)

    res = run_bass_kernel_spmd(nc, in_maps, core_ids=list(range(8)))

    out = np.empty((B, S, D), np.float32)
    for c in range(8):
        b, rows = owns[c]
        out[b][rows] = np.asarray(res.results[c]["out"], np.float32)
    return out


# version 13
# speedup vs baseline: 1.0641x; 1.0031x over previous
import numpy as np
import ml_dtypes
from contextlib import ExitStack

import concourse.bass as bass
import concourse.mybir as mybir
from concourse.bass_utils import run_bass_kernel_spmd

F32 = mybir.dt.float32
BF16 = mybir.dt.bfloat16
I32 = mybir.dt.int32
AF = mybir.ActivationFunctionType
ALU = mybir.AluOpType
AX = mybir.AxisListType

B, S, D = 2, 8192, 2048
NH, HD, RD = 16, 128, 64
KVC, QC = 512, 1536
NG, GS, NE, GI = 8, 8, 64, 1408
OWN = 2048
NCHUNK = 16
NDC = D // 128
NCC = KVC // 128
NQC = QC // 128
NIC = GI // 128
SCALE = 1.0 / float(np.sqrt(HD + RD))
EPS = 1e-5
NTILE = 8
bfdt = ml_dtypes.bfloat16


def own_chunks(qi):
    return [63 - qi - 4 * j for j in range(NCHUNK)]


def tile_T(j):
    return 64 - 8 * j


class _Op:
    __slots__ = ("idx", "eng", "emit", "deps", "dma", "sem", "inc", "cnt", "waits", "wlist")


NQSUB = 6


class Sched:
    def __init__(self, nc, es):
        self.nc = nc
        self.ops = []
        self.w = {}
        self.r = {}
        self.sems = {}
        for name in ("e:pe", "e:act", "e:dve", "e:pool"):
            self.sems[name] = es.enter_context(nc.semaphore(name.replace(":", "_")))
        for q in ("sp", "pool", "act"):
            for i in range(NQSUB):
                self.sems[("q:" + q, i)] = es.enter_context(nc.semaphore(f"q_{q}_{i}"))
        self.cnt = {k: 0 for k in self.sems}
        self.qn = {"sp": 0, "pool": 0, "act": 0}

    def op(self, eng, emit, reads=(), writes=(), dma=False):
        idx = len(self.ops)
        deps = set()
        for k in reads:
            j = self.w.get(k)
            if j is not None:
                deps.add(j)
        for k in writes:
            j = self.w.get(k)
            if j is not None:
                deps.add(j)
            deps.update(self.r.get(k, ()))
        o = _Op()
        o.idx, o.eng, o.emit, o.deps, o.dma = idx, eng, emit, deps, dma
        if dma:
            n = self.qn[eng]
            self.qn[eng] = n + 1
            o.sem = ("q:" + eng, n % NQSUB)
            o.inc = 16
        else:
            o.sem = "e:" + eng
            o.inc = 1
        self.cnt[o.sem] += o.inc
        o.cnt = self.cnt[o.sem]
        self.ops.append(o)
        for k in reads:
            self.r.setdefault(k, []).append(idx)
        for k in writes:
            self.w[k] = idx
            self.r[k] = []
        return idx

    def pe(self, emit, r=(), w=()):
        return self.op("pe", emit, r, w)

    def dve(self, emit, r=(), w=()):
        return self.op("dve", emit, r, w)

    def act(self, emit, r=(), w=()):
        return self.op("act", emit, r, w)

    def pool(self, emit, r=(), w=()):
        return self.op("pool", emit, r, w)

    def dma(self, eng, emit, r=(), w=()):
        return self.op(eng, emit, r, w, dma=True)

    def barrier(self):
        snapshot = dict(self.cnt)
        for eng in ("pe", "act", "dve", "pool", "sp"):
            o = _Op()
            o.idx, o.eng, o.emit, o.dma = len(self.ops), eng, None, False
            o.deps = set()
            o.sem, o.inc, o.cnt = None, 0, None
            o.waits = snapshot
            self.ops.append(o)

    def finalize(self):
        engvc = {e: {} for e in ("pe", "act", "dve", "pool", "sp")}
        opvc = [None] * len(self.ops)
        for o in self.ops:
            if o.emit is None and o.sem is None:
                vc = engvc[o.eng]
                o.wlist = []
                for semname, val in o.waits.items():
                    if val > vc.get(semname, 0):
                        o.wlist.append((semname, val))
                        vc[semname] = val
                opvc[o.idx] = dict(vc)
                continue
            vc = engvc[o.eng]
            wl = []
            if o.dma and o.cnt > 16 and vc.get(o.sem, 0) < o.cnt - 16:
                wl.append((o.sem, o.cnt - 16))
                vc[o.sem] = o.cnt - 16
            for j in sorted(o.deps):
                d = self.ops[j]
                if d.sem is None:
                    continue
                if d.eng == "pe" and o.eng == "pe" and not d.dma and not o.dma:
                    continue
                if vc.get(d.sem, 0) >= d.cnt:
                    continue
                wl.append((d.sem, d.cnt))
                vc[d.sem] = d.cnt
                dvc = opvc[j]
                if dvc:
                    for s2, v2 in dvc.items():
                        if v2 > vc.get(s2, 0):
                            vc[s2] = v2
            o.wlist = wl
            myvc = dict(vc)
            myvc[o.sem] = max(myvc.get(o.sem, 0), o.cnt)
            opvc[o.idx] = myvc

        nc = self.nc
        with nc.Block() as block:
            def run_engine(name, handle):
                for o in self.ops:
                    if o.eng != name:
                        continue
                    for semname, val in o.wlist:
                        handle.wait_ge(self.sems[semname], val)
                    if o.emit is not None:
                        o.emit(handle).then_inc(self.sems[o.sem], o.inc)

            @block.sync
            def _(h):
                run_engine("sp", h)

            @block.gpsimd
            def _(h):
                run_engine("pool", h)

            @block.vector
            def _(h):
                run_engine("dve", h)

            @block.scalar
            def _(h):
                run_engine("act", h)

            @block.tensor
            def _(h):
                run_engine("pe", h)


class Ctx:
    pass


def _declare_io(nc, stage=()):
    d = Ctx()
    def din(name, shape, dt):
        t = nc.dram_tensor(name, list(shape), dt, kind="ExternalInput").ap()
        setattr(d, name, t)
        return t

    din("xb", [S, D], F32)
    din("xo", [OWN, D], F32)
    din("w_kvd", [128, NDC * KVC], F32); din("b_kvd", [128, NCC], F32)
    din("w_qd", [128, NDC * QC], F32); din("b_qd", [128, NQC], F32)
    din("w_ku", [NH, 128, NCC * 128], F32); din("b_ku", [128, NH], F32)
    din("w_vu", [NH, 128, NCC * 128], F32); din("b_vu", [128, NH], F32)
    din("w_kr", [NH, 128, NCC * RD], F32); din("b_kr", [RD, NH], F32)
    din("w_qu", [NH, 128, NQC * 128], F32); din("b_qu", [128, NH], F32)
    din("w_qr", [NH, 128, NQC * RD], F32); din("b_qr", [RD, NH], F32)
    din("w_o", [128, NDC * D], F32); din("b_o", [1, D], F32)
    din("b_ku_r", [1, D], F32); din("b_kr_r", [1, NH * RD], F32)
    din("gate_wT", [128, NDC * NE], F32)
    din("maskup", [NG, 128, NDC * GS], F32)
    din("wgT", [NG, NIC, 128, NDC * 128], F32)
    din("wuT", [NG, NIC, 128, NDC * 128], F32)
    din("downT", [NG, NIC, 128, D], F32)
    din("inv", [1, NE], I32)
    din("ktbl_c", [32, S], BF16); din("ktbl_s", [32, S], BF16)
    din("qtbl_c", [32, OWN], BF16); din("qtbl_s", [32, OWN], BF16)
    din("masks", [NTILE, 128, 8 * 256], BF16)
    din("ident", [128, 128], BF16)
    din("identf", [128, 128], F32)

    d.out = nc.dram_tensor("out", [OWN, D], F32, kind="ExternalOutput").ap()
    def scr(name, shape, dt):
        kind = "ExternalInput" if name in stage else "Internal"
        t = nc.dram_tensor(name, list(shape), dt, kind=kind).ap()
        setattr(d, name, t)
        return t
    scr("z_qb", [1, NH], F32)
    scr("z_kvc", [NCC, 128, S], BF16)
    scr("z_qc", [NQC, 128, OWN], BF16)
    scr("z_xn", [NDC, 128, OWN], BF16)
    scr("z_q", [NH, 128, OWN], BF16)
    scr("z_qr", [NH, RD, OWN], BF16)
    scr("z_ctx", [NH, 128, OWN], BF16)
    scr("z_attn", [OWN, D], F32)
    scr("z_a", [NG, NIC, 128, OWN], BF16)
    scr("z_dwn", [NG, NIC, 128, D], BF16)
    return d


_SCHED_REF = []


def _build(upto=99, debug=(), upfrom=0, stage=()):
    nc = bass.Bass("TRN2", target_bir_lowering=False, detect_race_conditions=False,
                   dynamic_dma_scratch_size=16384)
    d = _declare_io(nc, stage)
    dbg_outs = {}
    for name in debug:
        zt = getattr(d, name)
        dbg_outs[name] = nc.dram_tensor("dbg_" + name, list(zt.shape), zt.dtype,
                                        kind="ExternalOutput").ap()
    root = ExitStack()
    with root:
        Sh = Sched(nc, root)
        _SCHED_REF.clear(); _SCHED_REF.append(Sh)
        ones_b = root.enter_context(nc.sbuf_tensor("ones_b", [128, 1], BF16))
        ones_f = root.enter_context(nc.sbuf_tensor("ones_f", [1, 128], F32))
        id_b = root.enter_context(nc.sbuf_tensor("id_b", [128, 128], BF16))
        id_f = root.enter_context(nc.sbuf_tensor("id_f", [128, 128], F32))
        qb = root.enter_context(nc.sbuf_tensor("qb", [1, NH], F32))
        p = Ctx()
        p.nc, p.S, p.d = nc, Sh, d
        p.ones_b, p.ones_f, p.id_b, p.id_f, p.qb = ones_b, ones_f, id_b, id_f, qb

        Sh.dve(lambda h: nc.vector.memset(ones_b[:], 1.0), w=[("ones_b",)])
        Sh.dve(lambda h: nc.vector.memset(ones_f[:], 1.0), w=[("ones_f",)])
        Sh.dma("sp", lambda h: h.dma_start(out=id_b[:], in_=d.ident[:, :]), w=[("id_b",)])
        Sh.dma("sp", lambda h: h.dma_start(out=id_f[:], in_=d.identf[:, :]), w=[("id_f",)])

        if upfrom <= 1:
            _phase1a(p)
        if upto >= 2 and upfrom <= 2:
            _phase1b(p)
        if upto >= 3 and upfrom <= 3:
            _phase2(p)
        if upto >= 4 and upfrom <= 4:
            _phase2b(p)
        if upto >= 5:
            _phase3(p)
        _finish(p, upto)
        for name, oap in dbg_outs.items():
            zt = getattr(d, name)
            sl = tuple(slice(None) for _ in zt.shape)
            Sh.dma("sp", (lambda zt, oap, sl: lambda h: h.dma_start(out=oap[sl], in_=zt[sl]))(zt, oap, sl))
        Sh.barrier()
        Sh.finalize()
    return nc


def _ln_block(p, es, src_ap, blk, x_tm, xn_tm, stats, kx):
    nc, Sh = p.nc, p.S
    ssq, sums, mean, var, sd, rstd, sqs, msq = stats
    Sh.dma("sp", lambda h: h.dma_start(
        out=x_tm[:], in_=src_ap[blk * 512:(blk + 1) * 512, :].rearrange("(t p) d -> p t d", p=128)),
        w=[(kx, "xtm")])
    for t in range(4):
        Sh.act((lambda t: lambda h: nc.scalar.activation(
            out=sqs[:, t * D:(t + 1) * D], in_=x_tm[:, t, :], func=AF.Square,
            accum_out=ssq[:, t:t + 1]))(t),
            r=[(kx, "xtm")], w=[(kx, "ssq", t), (kx, "sqs", t)])
        Sh.dve((lambda t: lambda h: nc.vector.reduce_sum(
            sums[:, t:t + 1], x_tm[:, t, :], axis=AX.X))(t),
            r=[(kx, "xtm")], w=[(kx, "sum", t)])
    Sh.dve(lambda h: nc.vector.tensor_scalar_mul(mean[:], sums[:], 1.0 / D),
           r=[(kx, "sum", t) for t in range(4)], w=[(kx, "mean")])
    Sh.dve(lambda h: nc.vector.tensor_scalar_mul(var[:], ssq[:], 1.0 / D),
           r=[(kx, "ssq", t) for t in range(4)], w=[(kx, "var")])
    Sh.dve(lambda h: nc.vector.tensor_tensor(out=msq[:], in0=mean[:], in1=mean[:], op=ALU.mult),
           r=[(kx, "mean")], w=[(kx, "msq")])
    Sh.dve(lambda h: nc.vector.tensor_sub(var[:], var[:], msq[:]),
           r=[(kx, "var"), (kx, "msq")], w=[(kx, "var")])
    Sh.dve(lambda h: nc.vector.tensor_scalar_add(var[:], var[:], EPS),
           r=[(kx, "var")], w=[(kx, "var")])
    Sh.act(lambda h: nc.scalar.activation(out=sd[:], in_=var[:], func=AF.Sqrt),
           r=[(kx, "var")], w=[(kx, "sd")])
    Sh.dve(lambda h: nc.vector.reciprocal(rstd[:], sd[:]),
           r=[(kx, "sd")], w=[(kx, "rstd")])
    for t in range(4):
        Sh.dve((lambda t: lambda h: nc.vector.scalar_tensor_tensor(
            out=xn_tm[:, t, :], in0=x_tm[:, t, :], scalar=mean[:, t:t + 1],
            in1=rstd[:, t:t + 1].to_broadcast([128, D]),
            op0=ALU.subtract, op1=ALU.mult))(t),
            r=[(kx, "xtm"), (kx, "mean"), (kx, "rstd")], w=[(kx, "xn", t)])


def _transpose_block(p, es, xn_tm, xn_fm, ps_tp, kx, kfm):
    nc, Sh = p.nc, p.S
    for dc in range(NDC):
        bank = dc % 2
        for t in range(4):
            Sh.pe((lambda dc, t, bank: lambda h: nc.tensor.transpose(
                ps_tp[bank][:, t * 128:(t + 1) * 128], xn_tm[:, t, dc * 128:(dc + 1) * 128],
                p.id_b[:]))(dc, t, bank),
                r=[(kx, "xn", t), ("id_b",)], w=[("ps_tp", bank, t)])
        Sh.dve((lambda dc, bank: lambda h: nc.vector.tensor_copy(
            xn_fm[:, dc * 512:(dc + 1) * 512], ps_tp[bank][:]))(dc, bank),
            r=[("ps_tp", bank, t) for t in range(4)], w=[kfm + (dc,)])


def _phase1a(p):
    nc, Sh, d = p.nc, p.S, p.d
    with ExitStack() as es:
        x_tm2 = [es.enter_context(nc.sbuf_tensor(f"x_tm{i}", [128, 4, D], F32)) for i in range(2)]
        xn_tm2 = [es.enter_context(nc.sbuf_tensor(f"xn_tm{i}", [128, 4, D], BF16)) for i in range(2)]
        xn_fm = [es.enter_context(nc.sbuf_tensor(f"xn_fm{i}", [128, NDC * 512], BF16)) for i in range(2)]
        sqs = es.enter_context(nc.sbuf_tensor("sqs", [128, 4 * D], BF16))
        stats2 = []
        for i in range(2):
            cols = [es.enter_context(nc.sbuf_tensor(f"lnst{i}", [128, 4 * 7], F32))]
            c = cols[0]
            stats2.append((c[:, 0:4], c[:, 4:8], c[:, 8:12], c[:, 12:16], c[:, 16:20], c[:, 20:24], sqs, c[:, 24:28]))
        w_big = es.enter_context(nc.sbuf_tensor("w_big", [128, NDC * QC], BF16))
        b_kvd = es.enter_context(nc.sbuf_tensor("sb_b_kvd", [128, NCC], F32))
        pco = es.enter_context(nc.sbuf_tensor("pco", [128, NQC * 512], BF16))
        kvco = pco[:, 0:NCC * 512]
        ps_tp = [es.enter_context(nc.psum_tensor(f"ps_tp{i}", [128, 512], BF16)) for i in range(2)]
        ps_kv = [es.enter_context(nc.psum_tensor(f"ps_kv{i}", [128, 512], F32)) for i in range(2)]

        Sh.dma("pool", lambda h: h.dma_start(
            out=w_big[:, 0:NDC * KVC], in_=d.w_kvd[:, :]), w=[("w_kvd",)])
        Sh.dma("sp", lambda h: h.dma_start(
            out=b_kvd[:], in_=d.b_kvd[:, :]), w=[("b_kvd",)])

        for blk in range(S // 512):
            kx = ("b", blk % 2)
            _ln_block(p, es, d.xb, blk, x_tm2[blk % 2], xn_tm2[blk % 2], stats2[blk % 2], kx)
            _transpose_block(p, es, xn_tm2[blk % 2], xn_fm[blk % 2], ps_tp, kx, ("fm", blk % 2))
            for cc in range(NCC):
                bank = cc % 2
                for dc in range(NDC):
                    Sh.pe((lambda cc, dc, bank, blk: lambda h: nc.tensor.matmul(
                        ps_kv[bank][:], w_big[:, dc * KVC + cc * 128:dc * KVC + (cc + 1) * 128],
                        xn_fm[blk % 2][:, dc * 512:(dc + 1) * 512],
                        start=(dc == 0), stop=(dc == NDC - 1)))(cc, dc, bank, blk),
                        r=[("w_kvd",), ("fm", blk % 2, dc)], w=[("ps_kv", bank)])
                Sh.act((lambda cc, bank: lambda h: nc.scalar.activation(
                    out=pco[:, cc * 512:(cc + 1) * 512], in_=ps_kv[bank][:],
                    func=AF.Identity, bias=b_kvd[:, cc:cc + 1], scale=1.0))(cc, bank),
                    r=[("ps_kv", bank), ("b_kvd",)], w=[("kvco", cc)])
                Sh.dma("act", (lambda cc, blk: lambda h: h.dma_start(
                    out=d.z_kvc[cc, :, blk * 512:(blk + 1) * 512],
                    in_=pco[:, cc * 512:(cc + 1) * 512]))(cc, blk),
                    r=[("kvco", cc)], w=[])
        Sh.barrier()

        b_qd = es.enter_context(nc.sbuf_tensor("sb_b_qd", [128, NQC], F32))
        qco = pco
        ps_q = ps_kv

        Sh.dma("pool", lambda h: h.dma_start(
            out=w_big[:, :], in_=d.w_qd[:, :]), w=[("w_qd",)])
        Sh.dma("sp", lambda h: h.dma_start(
            out=b_qd[:], in_=d.b_qd[:, :]), w=[("b_qd",)])

        for blk in range(OWN // 512):
            kx = ("o", blk % 2)
            _ln_block(p, es, d.xo, blk, x_tm2[blk % 2], xn_tm2[blk % 2], stats2[blk % 2], kx)
            _transpose_block(p, es, xn_tm2[blk % 2], xn_fm[blk % 2], ps_tp, kx, ("fm2", blk % 2))
            for dc in range(NDC):
                Sh.dma("sp", (lambda dc, blk: lambda h: h.dma_start(
                    out=d.z_xn[dc, :, blk * 512:(blk + 1) * 512],
                    in_=xn_fm[blk % 2][:, dc * 512:(dc + 1) * 512]))(dc, blk),
                    r=[("fm2", blk % 2, dc)], w=[])
            for qc in range(NQC):
                bank = qc % 2
                for dc in range(NDC):
                    Sh.pe((lambda qc, dc, bank, blk: lambda h: nc.tensor.matmul(
                        ps_q[bank][:], w_big[:, dc * QC + qc * 128:dc * QC + (qc + 1) * 128],
                        xn_fm[blk % 2][:, dc * 512:(dc + 1) * 512],
                        start=(dc == 0), stop=(dc == NDC - 1)))(qc, dc, bank, blk),
                        r=[("w_qd",), ("fm2", blk % 2, dc)], w=[("ps_q", bank)])
                Sh.act((lambda qc, bank: lambda h: nc.scalar.activation(
                    out=qco[:, qc * 512:(qc + 1) * 512], in_=ps_q[bank][:],
                    func=AF.Identity, bias=b_qd[:, qc:qc + 1], scale=1.0))(qc, bank),
                    r=[("ps_q", bank), ("b_qd",)], w=[("qco", qc)])
                Sh.dma("act", (lambda qc, blk: lambda h: h.dma_start(
                    out=d.z_qc[qc, :, blk * 512:(blk + 1) * 512],
                    in_=qco[:, qc * 512:(qc + 1) * 512]))(qc, blk),
                    r=[("qco", qc)], w=[])
        Sh.barrier()


_DBG_QCH = [False]
_P1B_PARTS = {"qr", "rope", "bound"}
_NB_ZERO = [False]


def _phase1b(p):
    nc, Sh, d = p.nc, p.S, p.d
    if _DBG_QCH[0]:
        d.dbg_qch = nc.dram_tensor("dbgq", [128, OWN], F32, kind="ExternalOutput").ap()
        d.dbg_ps = nc.dram_tensor("dbgps", [128, 512], F32, kind="ExternalOutput").ap()
        d.dbg_qcsb = nc.dram_tensor("dbgqcsb", [128, NQC * OWN], BF16, kind="ExternalOutput").ap()
    with ExitStack() as es:
        qc_sb = es.enter_context(nc.sbuf_tensor("qc_sb", [128, NQC, OWN], BF16))
        w_qu = [es.enter_context(nc.sbuf_tensor(f"sb_w_qu{i}", [128, NQC * 128], BF16)) for i in range(2)]
        w_qr = [es.enter_context(nc.sbuf_tensor(f"sb_w_qr{i}", [128, NQC * RD], BF16)) for i in range(2)]
        b_qu = es.enter_context(nc.sbuf_tensor("sb_b_qu", [128, NH], F32))
        b_qr = es.enter_context(nc.sbuf_tensor("sb_b_qr", [RD, NH], F32))
        bqu_s = es.enter_context(nc.sbuf_tensor("bqu_s", [128, NH], F32))
        bqr_s = es.enter_context(nc.sbuf_tensor("bqr_s", [RD, NH], F32))
        qtc = es.enter_context(nc.sbuf_tensor("qtc", [32, OWN], BF16))
        qts = es.enter_context(nc.sbuf_tensor("qts", [32, OWN], BF16))
        qch = es.enter_context(nc.sbuf_tensor("qch", [128, OWN], BF16))
        qrpre = es.enter_context(nc.sbuf_tensor("qrpre", [RD, OWN], BF16))
        t2b = es.enter_context(nc.sbuf_tensor("t2b", [32, OWN], BF16))
        rtmp = es.enter_context(nc.sbuf_tensor("rtmp", [32, OWN], BF16))
        rtm2 = es.enter_context(nc.sbuf_tensor("rtm2", [32, OWN], BF16))
        qrope = es.enter_context(nc.sbuf_tensor("qrope", [RD, OWN], BF16))
        sqb = es.enter_context(nc.sbuf_tensor("sqb", [128, OWN], BF16))
        sqr = es.enter_context(nc.sbuf_tensor("sqr", [RD, OWN], BF16))
        qbt = es.enter_context(nc.sbuf_tensor("qbt", [1, 4], F32))
        ps_a = es.enter_context(nc.psum_tensor("ps1b_a", [128, 512], F32))
        ps_b = es.enter_context(nc.psum_tensor("ps1b_b", [128, 512], F32))
        ps_n = es.enter_context(nc.psum_tensor("ps1b_n", [1, 512], F32))

        for qc in range(NQC):
            Sh.dma("sp", (lambda qc: lambda h: h.dma_start(
                out=qc_sb[:, qc, :], in_=d.z_qc[qc, :, :]))(qc), w=[("qc_sb",)])
        Sh.dma("sp", lambda h: h.dma_start(out=b_qu[:], in_=d.b_qu[:, :]), w=[("b_qu",)])
        Sh.dma("sp", lambda h: h.dma_start(out=b_qr[:], in_=d.b_qr[:, :]), w=[("b_qr",)])
        Sh.dma("pool", lambda h: h.dma_start(out=qtc[:], in_=d.qtbl_c[:, :]), w=[("qtc",)])
        Sh.dma("pool", lambda h: h.dma_start(out=qts[:], in_=d.qtbl_s[:, :]), w=[("qts",)])
        Sh.dve(lambda h: nc.vector.tensor_scalar_mul(bqu_s[:], b_qu[:], SCALE),
               r=[("b_qu",)], w=[("bqu_s",)])
        Sh.dve(lambda h: nc.vector.tensor_scalar_mul(bqr_s[:], b_qr[:], SCALE),
               r=[("b_qr",)], w=[("bqr_s",)])

        for hh in range(NH):
            par = hh % 2
            Sh.dma("pool", (lambda hh, par: lambda h: h.dma_start(
                out=w_qu[par][:], in_=d.w_qu[hh, :, :]))(hh, par), w=[("w_qu", par)])
            Sh.dma("pool", (lambda hh, par: lambda h: h.dma_start(
                out=w_qr[par][:], in_=d.w_qr[hh, :, :]))(hh, par), w=[("w_qr", par)])
            for t5 in range(4):
                bank = t5 % 2
                ps = [ps_a, ps_b][bank]
                for qc in range(NQC):
                    Sh.pe((lambda t5, qc, par, ps: lambda h: nc.tensor.matmul(
                        ps[:], w_qu[par][:, qc * 128:(qc + 1) * 128], qc_sb[:, qc, t5 * 512:(t5 + 1) * 512],
                        start=(qc == 0), stop=(qc == NQC - 1)))(t5, qc, par, ps),
                        r=[("w_qu", par), ("qc_sb",)], w=[("ps1b", bank)])
                Sh.act((lambda t5, hh, bank, ps: lambda h: nc.scalar.activation(
                    out=qch[:, t5 * 512:(t5 + 1) * 512], in_=ps[:],
                    func=AF.Identity, bias=bqu_s[:, hh:hh + 1], scale=SCALE))(t5, hh, bank, ps),
                    r=[("ps1b", bank), ("bqu_s",)], w=[("qch", t5)])
            for t5 in range(4) if "qr" in _P1B_PARTS else []:
                bank = t5 % 2
                ps = [ps_a, ps_b][bank]
                for qc in range(NQC):
                    Sh.pe((lambda t5, qc, par, ps: lambda h: nc.tensor.matmul(
                        ps[0:RD, :], w_qr[par][:, qc * RD:(qc + 1) * RD], qc_sb[:, qc, t5 * 512:(t5 + 1) * 512],
                        start=(qc == 0), stop=(qc == NQC - 1)))(t5, qc, par, ps),
                        r=[("w_qr", par), ("qc_sb",)], w=[("ps1b", bank)])
                Sh.act((lambda t5, hh, bank, ps: lambda h: nc.scalar.activation(
                    out=qrpre[:, t5 * 512:(t5 + 1) * 512], in_=ps[0:RD, :],
                    func=AF.Identity, bias=bqr_s[:, hh:hh + 1], scale=SCALE))(t5, hh, bank, ps),
                    r=[("ps1b", bank), ("bqr_s",)], w=[("qrpre",)])
            if "rope" not in _P1B_PARTS:
                Sh.dve(lambda h: nc.vector.memset(qrope[:], 0.0), w=[("qrope_t1",), ("qrope_t2",)])
            if "rope" in _P1B_PARTS:
                Sh.dma("pool", lambda h: h.dma_start(out=t2b[:], in_=qrpre[32:64, :]),
                       r=[("qrpre",)], w=[("t2b",)])
                Sh.dve(lambda h: nc.vector.tensor_tensor(
                    out=qrope[0:32, :], in0=qrpre[0:32, :], in1=qtc[:], op=ALU.mult),
                    r=[("qrpre",), ("qtc",)], w=[("qrope_t1",)])
                Sh.dve(lambda h: nc.vector.tensor_tensor(
                    out=rtmp[:], in0=t2b[:], in1=qts[:], op=ALU.mult),
                    r=[("t2b",), ("qts",)], w=[("rtmp",)])
                Sh.dve(lambda h: nc.vector.tensor_sub(qrope[0:32, :], qrope[0:32, :], rtmp[:]),
                       r=[("qrope_t1",), ("rtmp",)], w=[("qrope_t1",)])
                Sh.dve(lambda h: nc.vector.tensor_tensor(
                    out=rtm2[:], in0=t2b[:], in1=qtc[:], op=ALU.mult),
                    r=[("t2b",), ("qtc",)], w=[("rtm2",)])
                Sh.dve(lambda h: nc.vector.tensor_tensor(
                    out=rtmp[:], in0=qrpre[0:32, :], in1=qts[:], op=ALU.mult),
                    r=[("qrpre",), ("qts",)], w=[("rtmp",)])
                Sh.dve(lambda h: nc.vector.tensor_add(rtm2[:], rtm2[:], rtmp[:]),
                       r=[("rtm2",), ("rtmp",)], w=[("rtm2",)])
                Sh.dma("pool", lambda h: h.dma_start(out=qrope[32:64, :], in_=rtm2[:]),
                       r=[("rtm2",)], w=[("qrope_t2",)])
            if "bound" not in _P1B_PARTS:
                Sh.dve(lambda h: nc.vector.memset(p.qb[:], 1.0), w=[("qb",)])
            if "bound" in _P1B_PARTS:
                Sh.dve(lambda h: nc.vector.tensor_tensor(out=sqb[:], in0=qch[:], in1=qch[:], op=ALU.mult),
                       r=[("qch", 0), ("qch", 1), ("qch", 2), ("qch", 3)], w=[("sqb",)])
                Sh.dve(lambda h: nc.vector.tensor_tensor(
                    out=sqr[:], in0=qrope[:], in1=qrope[:], op=ALU.mult),
                    r=[("qrope_t1",), ("qrope_t2",)], w=[("sqr",)])
                for t5 in range(4):
                    Sh.pe((lambda t5: lambda h: nc.tensor.matmul(
                        ps_n[:], p.ones_b[:], sqb[:, t5 * 512:(t5 + 1) * 512],
                        start=True, stop=False))(t5),
                        r=[("sqb",), ("ones_b",)], w=[("ps1b_n",)])
                    Sh.pe((lambda t5: lambda h: nc.tensor.matmul(
                        ps_n[:], p.ones_b[0:RD, :], sqr[:, t5 * 512:(t5 + 1) * 512],
                        start=False, stop=True))(t5),
                        r=[("sqr",), ("ones_b",)], w=[("ps1b_n",)])
                    Sh.dve((lambda t5: lambda h: nc.vector.reduce_max(
                        qbt[:, t5:t5 + 1], ps_n[:], axis=AX.X))(t5),
                        r=[("ps1b_n",)], w=[("qbt", t5)])
                Sh.dve((lambda hh: lambda h: nc.vector.reduce_max(
                    p.qb[:, hh:hh + 1], qbt[:], axis=AX.X))(hh),
                    r=[("qbt", t) for t in range(4)], w=[("qb",)])
            Sh.dma("sp", (lambda hh: lambda h: h.dma_start(
                out=d.z_q[hh, :, :], in_=qch[:]))(hh),
                r=[("qch", t) for t in range(4)], w=[])
            Sh.dma("sp", (lambda hh: lambda h: h.dma_start(
                out=d.z_qr[hh, :, :], in_=qrope[:]))(hh),
                r=[("qrope_t1",), ("qrope_t2",)], w=[])
            if hh == NH - 1:
                Sh.dma("sp", lambda h: h.dma_start(out=d.z_qb[:, :], in_=p.qb[:]),
                       r=[("qb",)], w=[])
            if _DBG_QCH[0] and hh == 15:
                dbgf = es.enter_context(nc.sbuf_tensor("dbgf", [128, OWN], F32))
                Sh.dve(lambda h: nc.vector.tensor_copy(dbgf[:], qch[:]),
                       r=[("qch", t) for t in range(4)], w=[("dbgf",)])
                Sh.dma("sp", lambda h: h.dma_start(out=d.dbg_qch[:, :], in_=dbgf[:]),
                       r=[("dbgf",)], w=[])
                dbgp = es.enter_context(nc.sbuf_tensor("dbgp", [128, 512], F32))
                Sh.dve(lambda h: nc.vector.tensor_copy(dbgp[:], ps_b[:]),
                       r=[("ps1b", 1)], w=[("dbgp",)])
                Sh.dma("sp", lambda h: h.dma_start(out=d.dbg_ps[:, :], in_=dbgp[:]),
                       r=[("dbgp",)], w=[])
                Sh.dma("sp", lambda h: h.dma_start(
                    out=d.dbg_qcsb[:, :], in_=qc_sb[:].rearrange("p c n -> p (c n)")),
                    r=[("qc_sb",)], w=[])
        Sh.barrier()


def _phase2(p):
    nc, Sh, d = p.nc, p.S, p.d
    with ExitStack() as es:
        T = lambda n, s, dt: es.enter_context(nc.sbuf_tensor(n, s, dt))
        kvc = T("kvc", [128, NCC, S], BF16)
        ktc = T("ktc", [32, S], BF16)
        kts = T("kts", [32, S], BF16)
        wku = [T(f"wku{i}", [128, NCC * 128], BF16) for i in range(2)]
        wvu = [T(f"wvu{i}", [128, NCC * 128], BF16) for i in range(2)]
        wkr = [T(f"wkr{i}", [128, NCC * RD], BF16) for i in range(2)]
        kc_sb = T("kc_sb", [128, S], BF16)
        v_sb = T("v_sb", [128, S], BF16)
        krope = [T(f"krope{i}", [RD, S], BF16) for i in range(2)]
        CH = 1024
        NCH = S // CH
        krpre = [T(f"krpre{i}", [RD, CH], BF16) for i in range(2)]
        t2b2 = [T(f"t2b2{i}", [32, CH], BF16) for i in range(2)]
        rt1 = T("rt1", [32, CH], BF16)
        rt2 = T("rt2", [32, CH], BF16)
        qc_h = T("qc_h", [128, OWN], BF16)
        qr_h = T("qr_h", [RD, OWN], BF16)
        pt = [T(f"pt{i}", [128, 1024], BF16) for i in range(2)]
        msk = [T(f"msk{i}", [128, 8 * 256], BF16) for i in range(2)]
        l1r = T("l1r", [1, 192], F32)
        sc1 = T("sc1", [1, 8], F32)
        qbv = T("qbv", [1, NH], F32)
        nb = T("nb", [128, 1], F32)
        kvm = T("kvm", [128, 1], F32)
        kvm2 = T("kvm2", [RD, 1], F32)
        bvu_c = T("bvu_c", [128, NH], F32)
        bku_c = T("bku_c", [128, NH], F32)
        bkr_c = T("bkr_c", [RD, NH], F32)
        bvro = T("bvro", [1, 128], F32)
        bvbc = T("bvbc", [128, 128], F32)
        rc = T("rc", [1, 256], F32)
        rb = T("rb", [128, 256], F32)
        ctxev = T("ctxev", [128, 256], BF16)
        st = [es.enter_context(nc.psum_tensor(f"ps2st{i}", [128, 1024], F32)) for i in range(2)]
        ctx = [es.enter_context(nc.psum_tensor(f"ps2c{i}", [128, 512], F32)) for i in range(2)]
        eB = es.enter_context(nc.psum_tensor("ps2e", [128, 256], F32))
        bld = es.enter_context(nc.psum_tensor("ps2b", [RD, 512], F32))

        if _DBG_QCH[0]:
            d.dbg_b1 = nc.dram_tensor("dbgb1", [1, 8], F32, kind="ExternalOutput").ap()
            d.dbg_b2 = nc.dram_tensor("dbgb2", [1, 192], F32, kind="ExternalOutput").ap()
            d.dbg_b3 = nc.dram_tensor("dbgb3", [128, 1], F32, kind="ExternalOutput").ap()
            d.dbg_b4 = nc.dram_tensor("dbgb4", [1, NH], F32, kind="ExternalOutput").ap()

        for cc in range(NCC):
            Sh.dma("sp", (lambda cc: lambda h: h.dma_start(
                out=kvc[:, cc, :], in_=d.z_kvc[cc, :, :]))(cc), w=[("kvc",)])
        Sh.dma("sp", lambda h: h.dma_start(out=ktc[:], in_=d.ktbl_c[:, :]), w=[("ktc",)])
        Sh.dma("sp", lambda h: h.dma_start(out=kts[:], in_=d.ktbl_s[:, :]), w=[("kts",)])
        Sh.dma("sp", lambda h: h.dma_start(out=qbv[:], in_=d.z_qb[:, :]), w=[("qbv",)])
        Sh.dma("sp", lambda h: h.dma_start(out=bvu_c[:], in_=d.b_vu[:, :]), w=[("bvu_c",)])
        Sh.dma("sp", lambda h: h.dma_start(out=bku_c[:], in_=d.b_ku[:, :]), w=[("bku_c",)])
        Sh.dma("sp", lambda h: h.dma_start(out=bkr_c[:], in_=d.b_kr[:, :]), w=[("bkr_c",)])

        def head_loads(hh):
            par = hh % 2
            Sh.dma("pool", (lambda hh, par: lambda h: h.dma_start(
                out=wku[par][:], in_=d.w_ku[hh, :, :]))(hh, par), w=[("wku", par)])
            Sh.dma("pool", (lambda hh, par: lambda h: h.dma_start(
                out=wvu[par][:], in_=d.w_vu[hh, :, :]))(hh, par), w=[("wvu", par)])
            Sh.dma("pool", (lambda hh, par: lambda h: h.dma_start(
                out=wkr[par][:], in_=d.w_kr[hh, :, :]))(hh, par), w=[("wkr", par)])

        kck = [("kc_sb", k5) for k5 in range(16)]
        krkeys = lambda par: ([("krope1", par, c) for c in range(NCH)]
                              + [("krope2", par, c) for c in range(NCH)])

        def kr_chunks(hh):
            par = hh % 2
            kro = krope[par]
            out = []
            for ch in range(NCH):
                def c1(ch, k5, par=par, hh=hh):
                    kp = krpre[ch % 2]
                    for cc in range(NCC):
                        Sh.pe((lambda ch, k5, cc, par: lambda h: nc.tensor.matmul(
                            bld[:, :], wkr[par][:, cc * RD:(cc + 1) * RD],
                            kvc[:, cc, ch * CH + k5 * 512:ch * CH + (k5 + 1) * 512],
                            start=(cc == 0), stop=(cc == NCC - 1)))(ch, k5, cc, par),
                            r=[("wkr", par), ("kvc",)], w=[("bld",)])
                    Sh.act((lambda ch, k5, hh, kp: lambda h: nc.scalar.activation(
                        out=kp[:, k5 * 512:(k5 + 1) * 512], in_=bld[:, :],
                        func=AF.Identity, bias=bkr_c[:, hh:hh + 1], scale=1.0))(ch, k5, hh, kp),
                        r=[("bld",), ("bkr_c",)], w=[("krpre", ch % 2, k5)])

                def c2(ch=ch, par=par, kro=kro):
                    kp = krpre[ch % 2]
                    tb = t2b2[ch % 2]
                    kx = [("krpre", ch % 2, k5) for k5 in range(CH // 512)]
                    cs = ktc[:, ch * CH:(ch + 1) * CH]
                    sn = kts[:, ch * CH:(ch + 1) * CH]
                    Sh.dma("pool", (lambda kp, tb: lambda h: h.dma_start(
                        out=tb[:], in_=kp[32:64, :]))(kp, tb), r=kx, w=[("t2b2", ch % 2)])
                    Sh.dve((lambda ch, cs, kp, kro, par: lambda h: nc.vector.tensor_tensor(
                        out=kro[0:32, ch * CH:(ch + 1) * CH], in0=kp[0:32, :],
                        in1=cs, op=ALU.mult))(ch, cs, kp, kro, par),
                        r=kx + [("ktc",)], w=[("krope1", par, ch)])
                    Sh.dve((lambda sn, tb: lambda h: nc.vector.tensor_tensor(
                        out=rt1[:], in0=tb[:], in1=sn, op=ALU.mult))(sn, tb),
                        r=[("t2b2", ch % 2), ("kts",)], w=[("rt1",)])
                    Sh.dve((lambda ch, kro, par: lambda h: nc.vector.tensor_sub(
                        kro[0:32, ch * CH:(ch + 1) * CH],
                        kro[0:32, ch * CH:(ch + 1) * CH], rt1[:]))(ch, kro, par),
                        r=[("krope1", par, ch), ("rt1",)], w=[("krope1", par, ch)])
                    Sh.dve((lambda cs, tb: lambda h: nc.vector.tensor_tensor(
                        out=rt2[:], in0=tb[:], in1=cs, op=ALU.mult))(cs, tb),
                        r=[("t2b2", ch % 2), ("ktc",)], w=[("rt2",)])
                    Sh.dve((lambda sn, kp, kx: lambda h: nc.vector.tensor_tensor(
                        out=rt1[:], in0=kp[0:32, :], in1=sn, op=ALU.mult))(sn, kp, kx),
                        r=kx + [("kts",)], w=[("rt1",)])
                    Sh.dve(lambda h: nc.vector.tensor_add(rt2[:], rt2[:], rt1[:]),
                           r=[("rt2",), ("rt1",)], w=[("rt2",)])
                    Sh.dma("pool", (lambda ch, kro: lambda h: h.dma_start(
                        out=kro[32:64, ch * CH:(ch + 1) * CH], in_=rt2[:]))(ch, kro),
                        r=[("rt2",)], w=[("krope2", par, ch)])
                from functools import partial
                out += [partial(c1, ch, k5) for k5 in range(CH // 512)]
                out.append(c2)

            def c3(par=par, kro=kro):
                Sh.dve((lambda kro, par: lambda h: nc.vector.tensor_reduce(
                    kvm2[:], kro[:], axis=AX.X, op=ALU.max,
                    apply_absolute_value=True))(kro, par),
                    r=krkeys(par), w=[("kvm2",)])
                Sh.dve(lambda h: nc.vector.tensor_mul(kvm2[:], kvm2[:], kvm2[:]),
                       r=[("kvm2",)], w=[("kvm2",)])
                Sh.pe(lambda h: nc.tensor.transpose(eB[0:1, 128:192], kvm2[:], p.id_f[0:RD, 0:RD]),
                      r=[("kvm2",), ("id_f",)], w=[("ps2e",)])
                Sh.dve(lambda h: nc.vector.tensor_copy(l1r[:, 128:192], eB[0:1, 128:192]),
                       r=[("ps2e",)], w=[("l1r", 1)])
            out.append(c3)
            return out

        def bound_finish(hh):
            Sh.dve(lambda h: nc.vector.tensor_reduce(
                kvm[:], kc_sb[:], axis=AX.X, op=ALU.max, apply_absolute_value=True),
                r=kck, w=[("kvm",)])
            Sh.dve(lambda h: nc.vector.tensor_mul(kvm[:], kvm[:], kvm[:]),
                   r=[("kvm",)], w=[("kvm",)])
            Sh.pe(lambda h: nc.tensor.transpose(eB[0:1, 0:128], kvm[:], p.id_f[:]),
                  r=[("kvm",), ("id_f",)], w=[("ps2e",)])
            Sh.dve(lambda h: nc.vector.tensor_copy(l1r[:, 0:128], eB[0:1, 0:128]),
                   r=[("ps2e",)], w=[("l1r", 0)])
            Sh.dve(lambda h: nc.vector.reduce_sum(sc1[:, 1:2], l1r[:], axis=AX.X),
                   r=[("l1r", 0), ("l1r", 1)], w=[("sc1", 1)])
            Sh.act((lambda hh: lambda h: nc.scalar.activation(
                out=sc1[:, 2:3], in_=qbv[:, hh:hh + 1], func=AF.Ln))(hh),
                r=[("qbv",)], w=[("sc1", 2)])
            Sh.act(lambda h: nc.scalar.activation(
                out=sc1[:, 3:4], in_=sc1[:, 1:2], func=AF.Ln),
                r=[("sc1", 1)], w=[("sc1", 3)])
            Sh.dve(lambda h: nc.vector.tensor_add(sc1[:, 4:5], sc1[:, 2:3], sc1[:, 3:4]),
                   r=[("sc1", 2), ("sc1", 3)], w=[("sc1", 4)])
            Sh.act(lambda h: nc.scalar.activation(
                out=sc1[:, 5:6], in_=sc1[:, 4:5], func=AF.Exp, scale=0.5),
                r=[("sc1", 4)], w=[("sc1", 5)])
            Sh.dve(lambda h: nc.vector.tensor_scalar_mul(sc1[:, 6:7], sc1[:, 5:6], -1.0),
                   r=[("sc1", 5)], w=[("sc1", 6)])
            Sh.pe(lambda h: nc.tensor.matmul(
                eB[:, 0:1], p.ones_f[0:1, :], sc1[0:1, 6:7], start=True, stop=True),
                r=[("sc1", 6), ("ones_f",)], w=[("ps2e",)])
            Sh.dve(lambda h: nc.vector.tensor_copy(nb[:], eB[:, 0:1]),
                   r=[("ps2e",)], w=[("nb",)])
            if _NB_ZERO[0]:
                Sh.dve(lambda h: nc.vector.memset(nb[:], 0.0), w=[("nb",)])

        head_loads(0)
        pend = []
        slot_ctr = [0]

        def emit_slot():
            slot_ctr[0] += 1
            if pend and slot_ctr[0] % 2 == 0:
                pend.pop(0)()

        for hh in range(NH):
            par = hh % 2
            kroA = krope[par]
            krkA = krkeys(par)
            if hh + 1 < NH:
                head_loads(hh + 1)
            Sh.dma("sp", (lambda hh: lambda h: h.dma_start(
                out=qc_h[:], in_=d.z_q[hh, :, :]))(hh), w=[("qc_h",)])
            Sh.dma("sp", (lambda hh: lambda h: h.dma_start(
                out=qr_h[:], in_=d.z_qr[hh, :, :]))(hh), w=[("qr_h",)])

            for k5 in range(16):
                bank = k5 % 2
                ba = st[bank][:, 0:512]
                for cc in range(NCC):
                    Sh.pe((lambda k5, cc, par, ba: lambda h: nc.tensor.matmul(
                        ba, wku[par][:, cc * 128:(cc + 1) * 128],
                        kvc[:, cc, k5 * 512:(k5 + 1) * 512],
                        start=(cc == 0), stop=(cc == NCC - 1)))(k5, cc, par, ba),
                        r=[("wku", par), ("kvc",)], w=[("ps2st", bank)])
                Sh.act((lambda k5, hh, bank, ba: lambda h: nc.scalar.activation(
                    out=kc_sb[:, k5 * 512:(k5 + 1) * 512], in_=ba, func=AF.Identity,
                    bias=bku_c[:, hh:hh + 1], scale=1.0))(k5, hh, bank, ba),
                    r=[("ps2st", bank), ("bku_c",)], w=[("kc_sb", k5)])

            if hh == 0:
                for c in kr_chunks(0):
                    c()
            bound_finish(hh)
            if _DBG_QCH[0] and hh == 2:
                Sh.dma("sp", lambda h: h.dma_start(out=d.dbg_b1[:, :], in_=sc1[:]),
                       r=[("sc1", i) for i in range(7)], w=[])
                Sh.dma("sp", lambda h: h.dma_start(out=d.dbg_b2[:, :], in_=l1r[:]),
                       r=[("l1r", 0), ("l1r", 1)], w=[])
                Sh.dma("sp", lambda h: h.dma_start(out=d.dbg_b3[:, :], in_=nb[:]),
                       r=[("nb",)], w=[])
                Sh.dma("sp", lambda h: h.dma_start(out=d.dbg_b4[:, :], in_=qbv[:]),
                       r=[("qbv",)], w=[])

            Sh.pe((lambda hh: lambda h: nc.tensor.transpose(
                eB[0:1, 0:128], bvu_c[:, hh:hh + 1], p.id_f[:]))(hh),
                r=[("bvu_c",), ("id_f",)], w=[("ps2e",)])
            Sh.dve(lambda h: nc.vector.tensor_copy(bvro[:], eB[0:1, 0:128]),
                   r=[("ps2e",)], w=[("bvro",)])
            Sh.pe(lambda h: nc.tensor.matmul(
                eB[:, 0:128], p.ones_f[0:1, :], bvro[:], start=True, stop=True),
                r=[("bvro",), ("ones_f",)], w=[("ps2e",)])
            Sh.dve(lambda h: nc.vector.tensor_copy(bvbc[:], eB[:, 0:128]),
                   r=[("ps2e",)], w=[("bvbc",)])
            for kg in range(16):
                bank = kg % 2
                bt = st[bank][:, 0:512]
                for ki in range(4):
                    kt = kg * 4 + ki
                    for cc in range(NCC):
                        Sh.pe((lambda kt, ki, cc, par, bank: lambda h: nc.tensor.matmul(
                            st[bank][:, ki * 128:(ki + 1) * 128],
                            kvc[:, cc, kt * 128:(kt + 1) * 128],
                            wvu[par][:, cc * 128:(cc + 1) * 128],
                            start=(ki == 0 and cc == 0), stop=(ki == 3 and cc == NCC - 1),
                            skip_group_check=True))(kt, ki, cc, par, bank),
                            r=[("wvu", par), ("kvc",)], w=[("ps2st", bank)])
                Sh.dve((lambda kg, bank: lambda h: nc.vector.tensor_tensor(
                    out=v_sb[:, kg * 512:(kg + 1) * 512].rearrange("p (a b) -> p a b", a=4),
                    in0=st[bank][:, 0:512].rearrange("p (a b) -> p a b", a=4),
                    in1=bvbc[:].rearrange("p (o b) -> p o b", o=1).to_broadcast([128, 4, 128]),
                    op=ALU.add))(kg, bank),
                    r=[("ps2st", bank), ("bvbc",)], w=[("v_sb", kg)])

            pend = kr_chunks(hh + 1) if hh + 1 < NH else []
            vk = [("v_sb", kg) for kg in range(16)]
            pending_epi = []

            def flush_epi():
                while pending_epi:
                    pending_epi.pop(0)()

            def mk_epi(jt, hh):
                def epi():
                    Sh.dve((lambda jt: lambda h: nc.vector.reciprocal(
                        rc[:], ctx[jt % 2][0:1, 256:512]))(jt),
                        r=[("ps2c", jt % 2)], w=[("rc",)])
                    Sh.pe(lambda h: nc.tensor.matmul(
                        eB[:, 0:256], p.ones_f[0:1, :], rc[:], start=True, stop=True),
                        r=[("rc",), ("ones_f",)], w=[("ps2e",)])
                    Sh.dve(lambda h: nc.vector.tensor_copy(rb[:], eB[:, 0:256]),
                           r=[("ps2e",)], w=[("rb",)])
                    Sh.dve((lambda jt: lambda h: nc.vector.tensor_tensor(
                        out=ctxev[:], in0=ctx[jt % 2][:, 0:256], in1=rb[:], op=ALU.mult))(jt),
                        r=[("ps2c", jt % 2), ("rb",)], w=[("ctxev",)])
                    Sh.dma("sp", (lambda hh, jt: lambda h: h.dma_start(
                        out=d.z_ctx[hh, :, jt * 256:(jt + 1) * 256], in_=ctxev[:]))(hh, jt),
                        r=[("ctxev",)], w=[])
                return epi

            for pr in range(NTILE // 2):
                jtE, jtO = 2 * pr, 2 * pr + 1
                TE, TO = tile_T(jtE), tile_T(jtO)
                q0 = jtE * 256
                for jt in (jtE, jtO):
                    Sh.dma("sp", (lambda hh, jt: lambda h: h.dma_start(
                        out=msk[jt % 2][:], in_=d.masks[jt, :, :]))(hh, jt),
                        w=[("msk", jt % 2)])

                def s_joint(b):
                    sb = b % 2
                    for i in range(2):
                        kt = 2 * b + i
                        Sh.pe((lambda kt, i, sb, q0: lambda h: nc.tensor.matmul(
                            st[sb][:, i * 512:(i + 1) * 512],
                            kc_sb[:, kt * 128:(kt + 1) * 128], qc_h[:, q0:q0 + 512],
                            start=True, stop=False))(kt, i, sb, q0),
                            r=kck + [("qc_h",)], w=[("ps2st", sb)])
                        Sh.pe((lambda kt, i, sb, q0, kroA: lambda h: nc.tensor.matmul(
                            st[sb][:, i * 512:(i + 1) * 512],
                            kroA[:, kt * 128:(kt + 1) * 128], qr_h[:, q0:q0 + 512],
                            start=False, stop=True))(kt, i, sb, q0, kroA),
                            r=krkA + [("qr_h",)], w=[("ps2st", sb)])

                def exp_b(b):
                    sb = b % 2
                    Sh.act((lambda sb: lambda h: nc.scalar.activation(
                        out=pt[sb][:], in_=st[sb][:], func=AF.Exp,
                        bias=nb[:, 0:1], scale=1.0))(sb),
                        r=[("ps2st", sb), ("nb",)], w=[("pt", sb)])

                def pv_joint(b):
                    sb = b % 2
                    for i in range(2):
                        kt = 2 * b + i
                        if kt >= TO - 8:
                            mi = kt - (TO - 8)
                            Sh.dve((lambda sb, i, mi: lambda h: nc.vector.tensor_tensor(
                                out=pt[sb][:, i * 512 + 256:(i + 1) * 512],
                                in0=pt[sb][:, i * 512 + 256:(i + 1) * 512],
                                in1=msk[1][:, mi * 256:(mi + 1) * 256],
                                op=ALU.mult))(sb, i, mi),
                                r=[("pt", sb), ("msk", 1)], w=[("pt", sb)])
                    for i in range(2):
                        kt = 2 * b + i
                        for (jt, off) in ((jtE, 0), (jtO, 256)):
                            Sh.pe((lambda kt, i, sb, jt, off: lambda h: nc.tensor.matmul(
                                ctx[jt % 2][:, 0:256], v_sb[:, kt * 128:(kt + 1) * 128],
                                pt[sb][:, i * 512 + off:i * 512 + off + 256],
                                start=(kt == 0), stop=False,
                                skip_group_check=True))(kt, i, sb, jt, off),
                                r=vk + [("pt", sb)], w=[("ps2c", jt % 2)])
                            Sh.pe((lambda kt, i, sb, jt, off: lambda h: nc.tensor.matmul(
                                ctx[jt % 2][0:1, 256:512], p.ones_b[:],
                                pt[sb][:, i * 512 + off:i * 512 + off + 256],
                                start=False, stop=(jt == jtO and kt == TO - 1),
                                skip_group_check=True))(kt, i, sb, jt, off),
                                r=[("pt", sb), ("ones_b",)], w=[("ps2c", jt % 2)])

                def s_solo(b2):
                    sb = b2 % 2
                    for i in range(4):
                        kt = TO + 4 * b2 + i
                        Sh.pe((lambda kt, i, sb, q0: lambda h: nc.tensor.matmul(
                            st[sb][:, i * 256:(i + 1) * 256],
                            kc_sb[:, kt * 128:(kt + 1) * 128], qc_h[:, q0:q0 + 256],
                            start=True, stop=False))(kt, i, sb, q0),
                            r=kck + [("qc_h",)], w=[("ps2st", sb)])
                        Sh.pe((lambda kt, i, sb, q0, kroA: lambda h: nc.tensor.matmul(
                            st[sb][:, i * 256:(i + 1) * 256],
                            kroA[:, kt * 128:(kt + 1) * 128], qr_h[:, q0:q0 + 256],
                            start=False, stop=True))(kt, i, sb, q0, kroA),
                            r=krkA + [("qr_h",)], w=[("ps2st", sb)])

                def pv_solo(b2):
                    sb = b2 % 2
                    for i in range(4):
                        kt = TO + 4 * b2 + i
                        mi = kt - (TE - 8)
                        Sh.dve((lambda sb, i, mi: lambda h: nc.vector.tensor_tensor(
                            out=pt[sb][:, i * 256:(i + 1) * 256],
                            in0=pt[sb][:, i * 256:(i + 1) * 256],
                            in1=msk[0][:, mi * 256:(mi + 1) * 256],
                            op=ALU.mult))(sb, i, mi),
                            r=[("pt", sb), ("msk", 0)], w=[("pt", sb)])
                    for i in range(4):
                        kt = TO + 4 * b2 + i
                        Sh.pe((lambda kt, i, sb: lambda h: nc.tensor.matmul(
                            ctx[0][:, 0:256], v_sb[:, kt * 128:(kt + 1) * 128],
                            pt[sb][:, i * 256:(i + 1) * 256],
                            start=False, stop=False,
                            skip_group_check=True))(kt, i, sb),
                            r=vk + [("pt", sb)], w=[("ps2c", 0)])
                        Sh.pe((lambda kt, i, sb: lambda h: nc.tensor.matmul(
                            ctx[0][0:1, 256:512], p.ones_b[:],
                            pt[sb][:, i * 256:(i + 1) * 256],
                            start=False, stop=(kt == TE - 1),
                            skip_group_check=True))(kt, i, sb),
                            r=[("pt", sb), ("ones_b",)], w=[("ps2c", 0)])

                nbj = TO // 2
                for b in range(nbj):
                    emit_slot()
                    s_joint(b)
                    if b == 1:
                        flush_epi()
                    if b > 0:
                        pv_joint(b - 1)
                    exp_b(b)
                pv_joint(nbj - 1)
                pending_epi.append(mk_epi(jtO, hh))
                for b2 in range(2):
                    emit_slot()
                    s_solo(b2)
                    if b2 > 0:
                        pv_solo(b2 - 1)
                    exp_b(b2)
                pv_solo(1)
                pending_epi.append(mk_epi(jtE, hh))
            flush_epi()
            while pend:
                pend.pop(0)()
        Sh.barrier()


def _phase2b(p):
    nc, Sh, d = p.nc, p.S, p.d
    with ExitStack() as es:
        T = lambda n, s, dt: es.enter_context(nc.sbuf_tensor(n, s, dt))
        w_o = T("sb_w_o", [128, NDC * D], BF16)
        ctxt = [T(f"ctxt{i}", [128, NH * 128], BF16) for i in range(2)]
        bo_r = T("bo_r", [1, D], F32)
        bo_bc = T("bo_bc", [128, D], F32)
        aout = T("aout", [128, 512], F32)
        ps = [es.enter_context(nc.psum_tensor(f"ps2b{i}", [128, 512], F32)) for i in range(2)]
        psb = es.enter_context(nc.psum_tensor("ps2bb", [128, 512], F32))

        Sh.dma("pool", lambda h: h.dma_start(out=w_o[:], in_=d.w_o[:, :]), w=[("w_o",)])
        Sh.dma("sp", lambda h: h.dma_start(out=bo_r[:], in_=d.b_o[:, :]), w=[("bo_r",)])
        for q in range(4):
            Sh.pe((lambda q: lambda h: nc.tensor.matmul(
                psb[:], p.ones_f[0:1, :], bo_r[0:1, q * 512:(q + 1) * 512],
                start=True, stop=True))(q), r=[("bo_r",), ("ones_f",)], w=[("ps2bb",)])
            Sh.dve((lambda q: lambda h: nc.vector.tensor_copy(
                bo_bc[:, q * 512:(q + 1) * 512], psb[:]))(q),
                r=[("ps2bb",)], w=[("bo_bc",)])
        for t1 in range(16):
            par = t1 % 2
            for hc in range(NH):
                Sh.dma("sp", (lambda t1, hc, par: lambda h: h.dma_start(
                    out=ctxt[par][:, hc * 128:(hc + 1) * 128],
                    in_=d.z_ctx[hc, :, t1 * 128:(t1 + 1) * 128]))(t1, hc, par),
                    w=[("ctxt", par, hc)])
            for h5 in range(4):
                bank = h5 % 2
                for hc in range(NH):
                    Sh.pe((lambda t1, h5, hc, par, bank: lambda h: nc.tensor.matmul(
                        ps[bank][:], ctxt[par][:, hc * 128:(hc + 1) * 128],
                        w_o[:, hc * D + h5 * 512:hc * D + (h5 + 1) * 512],
                        start=(hc == 0), stop=(hc == NH - 1)))(t1, h5, hc, par, bank),
                        r=[("ctxt", par, hc), ("w_o",)], w=[("ps2b", bank)])
                Sh.dve((lambda t1, h5, bank: lambda h: nc.vector.tensor_tensor(
                    out=aout[:], in0=ps[bank][:], in1=bo_bc[:, h5 * 512:(h5 + 1) * 512],
                    op=ALU.add))(t1, h5, bank),
                    r=[("ps2b", bank), ("bo_bc",)], w=[("aout",)])
                Sh.dma("act", (lambda t1, h5: lambda h: h.dma_start(
                    out=d.z_attn[t1 * 128:(t1 + 1) * 128, h5 * 512:(h5 + 1) * 512],
                    in_=aout[:]))(t1, h5), r=[("aout",)], w=[])
        Sh.barrier()


def _phase3(p):
    nc, Sh, d = p.nc, p.S, p.d
    with ExitStack() as es:
        T = lambda n, s, dt: es.enter_context(nc.sbuf_tensor(n, s, dt))
        xn = T("xn3", [128, NDC, OWN], BF16)
        gw = T("gw", [128, NDC * NE], BF16)
        logit = T("logit", [128, 16 * NE], F32)
        nmx = T("nmx", [128, 16], F32)
        se = T("se", [128, 16], F32)
        rse = T("rse", [128, 16], F32)
        esb = T("esb", [128, 16 * NE], F32)
        scrs = T("scrs", [128, 16 * NE], F32)
        t8 = T("t8", [128, 8], F32)
        thr = T("thr", [128, 16], F32)
        msk3 = T("msk3", [128, NE], F32)
        wesb = T("wesb", [128, 16 * NE], F32)
        invf = T("invf", [1, NE], F32)
        invi = T("invi_sb", [1, NE], I32)
        invb = T("invb", [NE, NE], F32)
        iota = T("iota3", [NE, 1], F32)
        pone = T("pone", [NE, NE], F32)
        weT = T("weT", [NE, 128], F32)
        flat = T("flat", [128, 16 * NE], F32)
        sg = T("sg", [128, 16 * GS], F32)
        qq = T("qq", [128, 16 * NE], F32)
        mkd = T("mkd", [128, 16 * NE], F32)
        fmx = T("fmx", [128, 16 * GS], F32)
        fse = T("fse", [128, 16 * GS], F32)
        soft_bf = T("soft_bf", [128, 16 * NE], BF16)
        softT = T("softT", [GS, OWN], BF16)
        sgrow = T("sgrow", [1, OWN], F32)
        sbb = T("sbb", [128, OWN], F32)
        mu_sb = T("mu_sb", [128, NDC * GS], BF16)
        wgb = [T(f"wgb{i}", [128, NDC * 128], BF16) for i in range(2)]
        wub = [T(f"wub{i}", [128, NDC * 128], BF16) for i in range(2)]
        mgb = T("mgb", [128, GS], BF16)
        m_g = T("m_g", [GS, 128], BF16)
        m_u = T("m_u", [GS, 128], BF16)
        sg_bf = T("sg_bf", [128, 512], BF16)
        a1t = T("a1t", [128, 512], BF16)
        a_t = T("a_t", [128, 512], BF16)
        dcast = [T(f"dcast{i}", [128, D], BF16) for i in range(2)]
        ps_g = [es.enter_context(nc.psum_tensor(f"ps3g{i}", [128, 512], F32)) for i in range(2)]
        ps_u = [es.enter_context(nc.psum_tensor(f"ps3u{i}", [128, 512], F32)) for i in range(2)]
        ps_m = es.enter_context(nc.psum_tensor("ps3m", [128, 512], F32))
        ps_x = es.enter_context(nc.psum_tensor("ps3x", [128, 512], F32))
        ps_t = es.enter_context(nc.psum_tensor("ps3t", [128, 512], BF16))

        for dc in range(NDC):
            Sh.dma("sp", (lambda dc: lambda h: h.dma_start(
                out=xn[:, dc, :], in_=d.z_xn[dc, :, :]))(dc), w=[("xn", dc)])
        Sh.dma("pool", lambda h: h.dma_start(out=gw[:], in_=d.gate_wT[:, :]), w=[("gw",)])
        Sh.dma("sp", lambda h: h.dma_start(out=invi[:], in_=d.inv[:, :]), w=[("invi",)])
        Sh.dve(lambda h: nc.vector.tensor_copy(invf[:], invi[:]), r=[("invi",)], w=[("invf",)])
        Sh.pool(lambda h: nc.gpsimd.iota(iota[:], pattern=[[0, 1]], base=0,
                                         channel_multiplier=1,
                                         allow_small_or_imprecise_dtypes=True), w=[("iota",)])
        Sh.pe(lambda h: nc.tensor.matmul(ps_m[0:NE, 0:NE], p.ones_f[0:1, 0:NE], invf[:],
                                         start=True, stop=True),
              r=[("invf",), ("ones_f",)], w=[("ps3m",)])
        Sh.dve(lambda h: nc.vector.tensor_copy(invb[:], ps_m[0:NE, 0:NE]),
               r=[("ps3m",)], w=[("invb",)])
        Sh.dve(lambda h: nc.vector.tensor_single_scalar(
            out=pone[:], in_=invb[:], scalar=iota[:, 0:1], op=ALU.is_equal),
            r=[("invb",), ("iota",)], w=[("pone",)])
        for t1 in range(16):
            bank = t1 % 2
            pj = [ps_g, ps_u][bank][0]
            pk = ("ps3gb", 0) if bank == 0 else ("ps3ub", 0)
            for dc in range(NDC):
                Sh.pe((lambda t1, dc, pj: lambda h: nc.tensor.matmul(
                    pj[:, 0:NE], xn[:, dc, t1 * 128:(t1 + 1) * 128],
                    gw[:, dc * NE:(dc + 1) * NE],
                    start=(dc == 0), stop=(dc == NDC - 1)))(t1, dc, pj),
                    r=[("xn", dc), ("gw",)], w=[pk])
            Sh.dve((lambda t1, pj: lambda h: nc.vector.tensor_copy(
                logit[:, t1 * NE:(t1 + 1) * NE], pj[:, 0:NE]))(t1, pj),
                r=[pk], w=[("logit", t1)])
        for t1 in range(16):
            Sh.dve((lambda t1: lambda h: nc.vector.reduce_max(
                nmx[:, t1:t1 + 1], logit[:, t1 * NE:(t1 + 1) * NE], axis=AX.X,
                negate=True))(t1), r=[("logit", t1)], w=[("nmx", t1)])
            Sh.act((lambda t1: lambda h: nc.scalar.activation(
                out=esb[:, t1 * NE:(t1 + 1) * NE], in_=logit[:, t1 * NE:(t1 + 1) * NE],
                func=AF.Exp, bias=nmx[:, t1:t1 + 1],
                accum_out=se[:, t1:t1 + 1]))(t1),
                r=[("logit", t1), ("nmx", t1)], w=[("esb", t1), ("se", t1)])
        Sh.dve(lambda h: nc.vector.reciprocal(rse[:], se[:]),
               r=[("se", t1) for t1 in range(16)], w=[("rse",)])
        Sh.dve(lambda h: nc.vector.tensor_tensor(
            out=scrs[:].rearrange("p (a b) -> p a b", a=16),
            in0=esb[:].rearrange("p (a b) -> p a b", a=16),
            in1=rse[:].rearrange("p (a o) -> p a o", o=1).to_broadcast([128, 16, NE]),
            op=ALU.mult),
            r=[("esb", t1) for t1 in range(16)] + [("rse",)], w=[("scrs",)])
        for t1 in range(16):
            Sh.dve((lambda t1: lambda h: nc.vector.max(
                out=t8[:, 0:8], in_=scrs[:, t1 * NE:(t1 + 1) * NE]))(t1),
                r=[("scrs",)], w=[("t8",)])
            Sh.dve((lambda t1: lambda h: nc.vector.tensor_copy(
                thr[:, t1:t1 + 1], t8[:, 7:8]))(t1), r=[("t8",)], w=[("thr", t1)])
            Sh.dve((lambda t1: lambda h: nc.vector.tensor_single_scalar(
                out=msk3[:], in_=scrs[:, t1 * NE:(t1 + 1) * NE],
                scalar=thr[:, t1:t1 + 1], op=ALU.is_ge))(t1),
                r=[("scrs",), ("thr", t1)], w=[("msk3",)])
            Sh.dve((lambda t1: lambda h: nc.vector.tensor_mul(
                wesb[:, t1 * NE:(t1 + 1) * NE], scrs[:, t1 * NE:(t1 + 1) * NE],
                msk3[:]))(t1), r=[("scrs",), ("msk3",)], w=[("wesb", t1)])
            Sh.pe((lambda t1: lambda h: nc.tensor.transpose(
                ps_m[0:NE, 0:128], wesb[:, t1 * NE:(t1 + 1) * NE], p.id_f[:]))(t1),
                r=[("wesb", t1), ("id_f",)], w=[("ps3m",)])
            Sh.dve((lambda t1: lambda h: nc.vector.tensor_copy(
                weT[:], ps_m[0:NE, 0:128]))(t1), r=[("ps3m",)], w=[("weT",)])
            Sh.pe((lambda t1: lambda h: nc.tensor.matmul(
                ps_x[:, 0:NE], weT[:], pone[:], start=True, stop=True))(t1),
                r=[("weT",), ("pone",)], w=[("ps3x",)])
            Sh.dve((lambda t1: lambda h: nc.vector.tensor_copy(
                flat[:, t1 * NE:(t1 + 1) * NE], ps_x[:, 0:NE]))(t1),
                r=[("ps3x",)], w=[("flat", t1)])
        fk = [("flat", t1) for t1 in range(16)]
        Sh.dve(lambda h: nc.vector.reduce_sum(
            sg[:].rearrange("p (a g o) -> p a g o", a=16, o=1),
            flat[:].rearrange("p (a g i) -> p a g i", a=16, g=GS), axis=AX.X),
            r=fk, w=[("sg",)])
        Sh.dve(lambda h: nc.vector.tensor_single_scalar(
            out=qq[:], in_=flat[:], scalar=0.0, op=ALU.is_equal), r=fk, w=[("qq",)])
        Sh.dve(lambda h: nc.vector.scalar_tensor_tensor(
            out=mkd[:], in0=qq[:], scalar=-1e9, in1=flat[:], op0=ALU.mult, op1=ALU.add),
            r=[("qq",)] + fk, w=[("mkd",)])
        Sh.dve(lambda h: nc.vector.tensor_reduce(
            fmx[:].rearrange("p (a g o) -> p a g o", a=16, o=1),
            mkd[:].rearrange("p (a g i) -> p a g i", a=16, g=GS), axis=AX.X,
            op=ALU.max), r=[("mkd",)], w=[("fmx",)])
        Sh.dve(lambda h: nc.vector.tensor_tensor(
            out=mkd[:].rearrange("p (a g i) -> p a g i", a=16, g=GS),
            in0=mkd[:].rearrange("p (a g i) -> p a g i", a=16, g=GS),
            in1=fmx[:].rearrange("p (a g o) -> p a g o", a=16, o=1)
            .to_broadcast([128, 16, GS, GS]), op=ALU.subtract),
            r=[("mkd",), ("fmx",)], w=[("mkd",)])
        Sh.act(lambda h: nc.scalar.activation(out=esb[:], in_=mkd[:], func=AF.Exp),
               r=[("mkd",)], w=[("esb", "x")])
        Sh.dve(lambda h: nc.vector.reduce_sum(
            fse[:].rearrange("p (a g o) -> p a g o", a=16, o=1),
            esb[:].rearrange("p (a g i) -> p a g i", a=16, g=GS), axis=AX.X),
            r=[("esb", "x")], w=[("fse",)])
        Sh.dve(lambda h: nc.vector.reciprocal(fse[:], fse[:]), r=[("fse",)], w=[("fse",)])
        Sh.dve(lambda h: nc.vector.tensor_tensor(
            out=soft_bf[:].rearrange("p (a g i) -> p a g i", a=16, g=GS),
            in0=esb[:].rearrange("p (a g i) -> p a g i", a=16, g=GS),
            in1=fse[:].rearrange("p (a g o) -> p a g o", a=16, o=1)
            .to_broadcast([128, 16, GS, GS]), op=ALU.mult),
            r=[("esb", "x"), ("fse",)], w=[("soft_bf",)])

        for g in range(NG):
            for t1 in range(16):
                Sh.pe((lambda g, t1: lambda h: nc.tensor.transpose(
                    ps_t[0:GS, 0:128],
                    soft_bf[:, t1 * NE + g * GS:t1 * NE + (g + 1) * GS], p.id_b[:]))(g, t1),
                    r=[("soft_bf",), ("id_b",)], w=[("ps3t",)])
                Sh.dve((lambda g, t1: lambda h: nc.vector.tensor_copy(
                    softT[:, t1 * 128:(t1 + 1) * 128], ps_t[0:GS, 0:128]))(g, t1),
                    r=[("ps3t",)], w=[("softT",)])
                Sh.pe((lambda g, t1: lambda h: nc.tensor.transpose(
                    ps_x[0:1, 0:128], sg[:, t1 * GS + g:t1 * GS + g + 1], p.id_f[:]))(g, t1),
                    r=[("sg",), ("id_f",)], w=[("ps3x",)])
                Sh.dve((lambda g, t1: lambda h: nc.vector.tensor_copy(
                    sgrow[:, t1 * 128:(t1 + 1) * 128], ps_x[0:1, 0:128]))(g, t1),
                    r=[("ps3x",)], w=[("sgrow",)])
            for t5 in range(4):
                Sh.pe((lambda g, t5: lambda h: nc.tensor.matmul(
                    ps_x[:], p.ones_f[0:1, :], sgrow[0:1, t5 * 512:(t5 + 1) * 512],
                    start=True, stop=True))(g, t5),
                    r=[("sgrow",), ("ones_f",)], w=[("ps3x",)])
                Sh.dve((lambda g, t5: lambda h: nc.vector.tensor_copy(
                    sbb[:, t5 * 512:(t5 + 1) * 512], ps_x[:]))(g, t5),
                    r=[("ps3x",)], w=[("sbb",)])
            Sh.dma("pool", (lambda g: lambda h: h.dma_start(
                out=mu_sb[:], in_=d.maskup[g, :, :]))(g), w=[("mu_sb",)])
            for ic in range(NIC):
                par = ic % 2
                Sh.dma("pool", (lambda g, ic, par: lambda h: h.dma_start(
                    out=wgb[par][:], in_=d.wgT[g, ic, :, :]))(g, ic, par), w=[("wgb", par)])
                Sh.dma("pool", (lambda g, ic, par: lambda h: h.dma_start(
                    out=wub[par][:], in_=d.wuT[g, ic, :, :]))(g, ic, par), w=[("wub", par)])
                Sh.dma("pool", (lambda g, ic, par: lambda h: h.dma_start(
                    out=dcast[par][:], in_=d.downT[g, ic, :, :]))(g, ic, par),
                    w=[("dcast", par)])
                Sh.dma("act", (lambda g, ic, par: lambda h: h.dma_start(
                    out=d.z_dwn[g, ic, :, :], in_=dcast[par][:]))(g, ic, par),
                    r=[("dcast", par)], w=[])
                for (wb, mdst, mkey) in ((wgb, m_g, "m_g"), (wub, m_u, "m_u")):
                    for dc in range(NDC):
                        Sh.pe((lambda wb, par, dc: lambda h: nc.tensor.matmul(
                            ps_m[:, 0:GS], wb[par][:, dc * 128:(dc + 1) * 128],
                            mu_sb[:, dc * GS:(dc + 1) * GS],
                            start=(dc == 0), stop=(dc == NDC - 1)))(wb, par, dc),
                            r=[(("wgb" if wb is wgb else "wub"), par), ("mu_sb",)],
                            w=[("ps3m",)])
                    Sh.dve(lambda h: nc.vector.tensor_copy(mgb[:], ps_m[:, 0:GS]),
                           r=[("ps3m",)], w=[("mgb",)])
                    Sh.pe(lambda h: nc.tensor.transpose(
                        ps_t[0:GS, 0:128], mgb[:], p.id_b[:]),
                        r=[("mgb",), ("id_b",)], w=[("ps3t",)])
                    Sh.dve((lambda mdst, mkey: lambda h: nc.vector.tensor_copy(
                        mdst[:], ps_t[0:GS, 0:128]))(mdst, mkey),
                        r=[("ps3t",)], w=[(mkey,)])
                for t5 in range(4):
                    bg = t5 % 2
                    for dc in range(NDC):
                        Sh.pe((lambda par, dc, t5, bg: lambda h: nc.tensor.matmul(
                            ps_g[bg][:], wgb[par][:, dc * 128:(dc + 1) * 128],
                            xn[:, dc, t5 * 512:(t5 + 1) * 512],
                            start=(dc == 0), stop=False))(par, dc, t5, bg),
                            r=[("wgb", par), ("xn", dc)], w=[("ps3gb", bg)])
                    Sh.pe((lambda t5, bg: lambda h: nc.tensor.matmul(
                        ps_g[bg][:], m_g[:], softT[:, t5 * 512:(t5 + 1) * 512],
                        start=False, stop=True))(t5, bg),
                        r=[("m_g",), ("softT",)], w=[("ps3gb", bg)])
                    for dc in range(NDC):
                        Sh.pe((lambda par, dc, t5, bg: lambda h: nc.tensor.matmul(
                            ps_u[bg][:], wub[par][:, dc * 128:(dc + 1) * 128],
                            xn[:, dc, t5 * 512:(t5 + 1) * 512],
                            start=(dc == 0), stop=False))(par, dc, t5, bg),
                            r=[("wub", par), ("xn", dc)], w=[("ps3ub", bg)])
                    Sh.pe((lambda t5, bg: lambda h: nc.tensor.matmul(
                        ps_u[bg][:], m_u[:], softT[:, t5 * 512:(t5 + 1) * 512],
                        start=False, stop=True))(t5, bg),
                        r=[("m_u",), ("softT",)], w=[("ps3ub", bg)])
                    Sh.act((lambda t5, bg: lambda h: nc.scalar.activation(
                        out=sg_bf[:], in_=ps_g[bg][:], func=AF.Silu))(t5, bg),
                        r=[("ps3gb", bg)], w=[("sg_bf",)])
                    Sh.dve((lambda t5, bg: lambda h: nc.vector.tensor_tensor(
                        out=a1t[:], in0=ps_u[bg][:], in1=sg_bf[:], op=ALU.mult))(t5, bg),
                        r=[("ps3ub", bg), ("sg_bf",)], w=[("a1t",)])
                    Sh.dve((lambda t5: lambda h: nc.vector.tensor_tensor(
                        out=a_t[:], in0=a1t[:], in1=sbb[:, t5 * 512:(t5 + 1) * 512],
                        op=ALU.mult))(t5), r=[("a1t",), ("sbb",)], w=[("a_t",)])
                    Sh.dma("act", (lambda g, ic, t5: lambda h: h.dma_start(
                        out=d.z_a[g, ic, :, t5 * 512:(t5 + 1) * 512], in_=a_t[:]))(g, ic, t5),
                        r=[("a_t",)], w=[])
        Sh.barrier()

    with ExitStack() as es:
        T = lambda n, s, dt: es.enter_context(nc.sbuf_tensor(n, s, dt))
        dwn_all = T("dwn_all", [128, NG * NIC * 512], BF16)
        abuf = [T(f"abuf{i}", [128, 1024], BF16) for i in range(6)]
        x_t = [T(f"x_t{i}", [128, 512], F32) for i in range(2)]
        at_t = [T(f"at_t{i}", [128, 512], F32) for i in range(2)]
        tmp = T("tmp3", [128, 512], F32)
        osb = T("osb", [128, 512], F32)
        psd = [es.enter_context(nc.psum_tensor(f"ps3d{i}", [128, 512], F32)) for i in range(8)]

        for h5 in range(4):
            for th in range(2):
                for g in range(NG):
                    for ic in range(NIC):
                        j = g * NIC + ic
                        par = j % 6
                        if th == 0:
                            Sh.dma("act", (lambda g, ic, h5, j: lambda h: h.dma_start(
                                out=dwn_all[:, j * 512:(j + 1) * 512],
                                in_=d.z_dwn[g, ic, :, h5 * 512:(h5 + 1) * 512]))(g, ic, h5, j),
                                w=[("dwn", j)])
                        Sh.dma("pool", (lambda g, ic, th, par: lambda h: h.dma_start(
                            out=abuf[par][:], in_=d.z_a[g, ic, :, th * 1024:(th + 1) * 1024]))(g, ic, th, par),
                            w=[("abuf", par)])
                        first = (g == 0 and ic == 0)
                        last = (g == NG - 1 and ic == NIC - 1)
                        for t1 in range(8):
                            Sh.pe((lambda j, t1, par, first, last: lambda h: nc.tensor.matmul(
                                psd[t1][:], abuf[par][:, t1 * 128:(t1 + 1) * 128],
                                dwn_all[:, j * 512:(j + 1) * 512],
                                start=first, stop=last))(j, t1, par, first, last),
                                r=[("abuf", par), ("dwn", j)], w=[("ps3d", t1)])
                for t1 in range(8):
                    par = t1 % 2
                    tg = th * 8 + t1
                    Sh.dma("sp", (lambda tg, h5, par: lambda h: h.dma_start(
                        out=x_t[par][:], in_=d.xo[tg * 128:(tg + 1) * 128, h5 * 512:(h5 + 1) * 512]))(tg, h5, par),
                        w=[("x_t", par)])
                    Sh.dma("sp", (lambda tg, h5, par: lambda h: h.dma_start(
                        out=at_t[par][:], in_=d.z_attn[tg * 128:(tg + 1) * 128, h5 * 512:(h5 + 1) * 512]))(tg, h5, par),
                        w=[("at_t", par)])
                    Sh.dve((lambda t1, par: lambda h: nc.vector.tensor_tensor(
                        out=tmp[:], in0=psd[t1][:], in1=at_t[par][:], op=ALU.add))(t1, par),
                        r=[("ps3d", t1), ("at_t", par)], w=[("tmp3",)])
                    Sh.dve((lambda par: lambda h: nc.vector.tensor_tensor(
                        out=osb[:], in0=tmp[:], in1=x_t[par][:], op=ALU.add))(par),
                        r=[("tmp3",), ("x_t", par)], w=[("osb",)])
                    Sh.dma("sp", (lambda tg, h5: lambda h: h.dma_start(
                        out=d.out[tg * 128:(tg + 1) * 128, h5 * 512:(h5 + 1) * 512],
                        in_=osb[:]))(tg, h5), r=[("osb",)], w=[])
        Sh.barrier()


def _finish(p, upto):
    nc, Sh, d = p.nc, p.S, p.d
    if upto >= 5:
        return
    with ExitStack() as es:
        t = es.enter_context(nc.sbuf_tensor("fin_t", [128, D], F32))
        for j in range(NCHUNK):
            Sh.dma("sp", (lambda j: lambda h: h.dma_start(
                out=t[:], in_=d.xo[j * 128:(j + 1) * 128, :]))(j), r=(), w=[("fin_t",)])
            Sh.dma("sp", (lambda j: lambda h: h.dma_start(
                out=d.out[j * 128:(j + 1) * 128, :], in_=t[:]))(j), r=[("fin_t",)], w=())
        Sh.barrier()


def _rope_tables(positions):
    inv_freq = 1.0 / (10000.0 ** (np.arange(0, RD, 2, dtype=np.float64) / RD))
    ang = positions[None, :].astype(np.float64) * inv_freq[:, None]
    return (np.cos(ang).astype(np.float32).astype(bfdt),
            np.sin(ang).astype(np.float32).astype(bfdt))


def _host_masks(rows):
    m = np.zeros((NTILE, 8, 128, 256), np.float32)
    lim = rows.reshape(NTILE, 256) + 1
    for j in range(NTILE):
        T = tile_T(j)
        for i in range(8):
            kt = T - 8 + i
            kpos = 128 * kt + np.arange(128)
            m[j, i] = (kpos[:, None] < lim[j][None, :]).astype(np.float32)
    return np.ascontiguousarray(m.transpose(0, 2, 1, 3).reshape(NTILE, 128, 8 * 256)).astype(bfdt)


def _host_shared(g):
    f = lambda a: np.ascontiguousarray(np.asarray(a, np.float32))
    C = np.ascontiguousarray

    def pmaj_in(w, nch):
        n = w.shape[1]
        return C(w.reshape(nch, 128, n).transpose(1, 0, 2).reshape(128, nch * n))

    def pmaj_heads(w, nch, hd):
        n = w.shape[1] // hd
        a = w.reshape(nch, 128, n, hd).transpose(2, 1, 0, 3)
        return C(a.reshape(n, 128, nch * hd))

    shared = {
        "w_kvd": pmaj_in(f(g["L0_attn_W_kvd"]), NDC),
        "b_kvd": C(f(g["L0_attn_b_kvd"]).reshape(NCC, 128).T),
        "w_qd": pmaj_in(f(g["L0_attn_W_qd"]), NDC),
        "b_qd": C(f(g["L0_attn_b_qd"]).reshape(NQC, 128).T),
        "w_ku": pmaj_heads(f(g["L0_attn_W_ku"]), NCC, 128),
        "b_ku": C(f(g["L0_attn_b_ku"]).reshape(NH, 128).T),
        "w_vu": pmaj_heads(f(g["L0_attn_W_vu"]), NCC, 128),
        "b_vu": C(f(g["L0_attn_b_vu"]).reshape(NH, 128).T),
        "w_kr": pmaj_heads(f(g["L0_attn_W_kr"]), NCC, RD),
        "b_kr": C(f(g["L0_attn_b_kr"]).reshape(NH, RD).T),
        "w_qu": pmaj_heads(f(g["L0_attn_W_qu"]), NQC, 128),
        "b_qu": C(f(g["L0_attn_b_qu"]).reshape(NH, 128).T),
        "w_qr": pmaj_heads(f(g["L0_attn_W_qr"]), NQC, RD),
        "b_qr": C(f(g["L0_attn_b_qr"]).reshape(NH, RD).T),
        "w_o": pmaj_in(f(g["L0_attn_W_o"]), NDC),
        "b_o": f(g["L0_attn_b_o"]).reshape(1, D),
        "b_ku_r": f(g["L0_attn_b_ku"]).reshape(1, D),
        "b_kr_r": f(g["L0_attn_b_kr"]).reshape(1, NH * RD),
        "gate_wT": pmaj_in(C(f(g["L0_ffn_gate_w"]).T), NDC),
        "maskup": np.stack([pmaj_in(f(g["L0_ffn_mask_up_w"][i]), NDC) for i in range(NG)]),
        "wgT": np.stack([np.stack([pmaj_in(C(f(g["L0_ffn_gate_proj_w"][i]).T)[:, j * 128:(j + 1) * 128], NDC)
                                   for j in range(NIC)]) for i in range(NG)]),
        "wuT": np.stack([np.stack([pmaj_in(C(f(g["L0_ffn_up_proj_w"][i]).T)[:, j * 128:(j + 1) * 128], NDC)
                                   for j in range(NIC)]) for i in range(NG)]),
        "downT": np.stack([C(f(g["L0_ffn_down_proj_w"][i]).T).reshape(NIC, 128, D) for i in range(NG)]),
        "inv": np.asarray(g["L0_ffn_inv_map"], np.int64).astype(np.int32).reshape(1, NE),
        "ident": np.eye(128, dtype=bfdt),
        "identf": np.eye(128, dtype=np.float32),
    }
    kc, ks = _rope_tables(np.arange(S))
    shared["ktbl_c"], shared["ktbl_s"] = kc, ks
    return shared


def _host_core(x, c):
    b, qi = c // 4, c % 4
    chunks = own_chunks(qi)
    rows = np.concatenate([np.arange(128 * gch, 128 * gch + 128) for gch in chunks])
    qc_, qs_ = _rope_tables(rows)
    im = {
        "xb": x[b],
        "xo": np.ascontiguousarray(x[b][rows]),
        "qtbl_c": qc_, "qtbl_s": qs_,
        "masks": _host_masks(rows),
    }
    return im, b, rows


def kernel(**inputs):
    g = {k: np.asarray(v) for k, v in inputs.items()}
    x = np.ascontiguousarray(g["L0_attn_x"].astype(np.float32, copy=False))
    shared = _host_shared(g)

    nc = _build()

    in_maps = []
    owns = []
    for c in range(8):
        im, b, rows = _host_core(x, c)
        owns.append((b, rows))
        im.update(shared)
        in_maps.append(im)

    res = run_bass_kernel_spmd(nc, in_maps, core_ids=list(range(8)))

    out = np.empty((B, S, D), np.float32)
    for c in range(8):
        b, rows = owns[c]
        out[b][rows] = np.asarray(res.results[c]["out"], np.float32)
    return out
